# version 16
# baseline (speedup 1.0000x reference)
import numpy as np
from contextlib import ExitStack

import concourse.bass as bass
import concourse.mybir as mybir
from concourse.bass_utils import run_bass_kernel_spmd

F32 = mybir.dt.float32
BF16 = mybir.dt.bfloat16
FP8 = mybir.dt.float8e4
AF = mybir.ActivationFunctionType
ALU = mybir.AluOpType
AX = mybir.AxisListType

NCORES = 8
TT = 8
D = 2048
DC = 16
S = 4096
NH = 4
QCW = 512
NQC = S // QCW
C_SHIFT = 20.0
SCALE = 0.08838834764831845
INV_D = 1.0 / 2048.0
EPS = 1e-5
GROUPS = [[0, 1, 2, 3], [4, 5, 6, 7]]


class Builder:
    def __init__(self, nc, paranoid=False):
        self.nc = nc
        self.paranoid = paranoid
        self.q = {k: [] for k in ("sync", "gpsimd", "vector", "scalar", "tensor")}
        self.sem = {}
        self.cnt = {}
        self.bank_state = [None] * 8
        self.risky = {}

    def add_sem(self, ctx, name):
        self.sem[name] = ctx.enter_context(self.nc.semaphore(name))
        self.cnt[name] = 0

    def flush(self, blockname):
        with self.nc.Block(blockname) as block:
            for eng, fns in self.q.items():
                if fns:
                    getattr(block, eng)(lambda e, fns=tuple(fns): [f(e) for f in fns])
        self.q = {k: [] for k in self.q}

    def op(self, eng, semname, fn, waits=(), inc=1, chain=False, risky=False):
        w = [(s, v) for (s, v) in waits if v > 0]
        if chain or (self.paranoid and eng == "vector"):
            prev = self.cnt[semname]
            if prev > 0:
                w.append((semname, prev))
        pend = self.risky.pop(eng, None)
        if pend is not None:
            w.append(pend)
        self.cnt[semname] += inc
        if risky:
            self.risky[eng] = (semname, self.cnt[semname])
        sems = self.sem

        def run(e, w=tuple(w), fn=fn, semname=semname, inc=inc):
            seen = {}
            for (s, v) in w:
                if seen.get(s, -1) < v:
                    e.wait_ge(sems[s], v)
                    seen[s] = v
            fn(e).then_inc(sems[semname], inc)
        self.q[eng].append(run)
        return self.cnt[semname]

    def wait(self, eng, waits):
        w = [(s, v) for (s, v) in waits if v > 0]
        sems = self.sem

        def run(e):
            for (s, v) in w:
                e.wait_ge(sems[s], v)
        self.q[eng].append(run)

    def dma(self, eng, semname, out, in_, waits=()):
        return self.op(eng, semname, lambda e: e.dma_start(out=out, in_=in_),
                       waits=waits, inc=16)

    def use_bank(self, k):
        st = self.bank_state[k]
        return [st] if st is not None else []

    def done_bank(self, k, semname, val):
        self.bank_state[k] = (semname, val)


def build(paranoid=False, taps=False):
    nc = bass.Bass("TRN2", target_bir_lowering=False)

    x_sh = nc.dram_tensor("x_sh", [1024, D], F32, kind="ExternalInput")
    wq_t = nc.dram_tensor("wq_t", [D, 512], F32, kind="ExternalInput")
    wk_t = nc.dram_tensor("wk_t", [D, 512], F32, kind="ExternalInput")
    wv_t = nc.dram_tensor("wv_t", [D, 512], F32, kind="ExternalInput")
    wo_t = nc.dram_tensor("wo_t", [512, D], F32, kind="ExternalInput")
    gwt = nc.dram_tensor("gwt", [128, 128], F32, kind="ExternalInput")
    gb = nc.dram_tensor("gb", [128, 1], F32, kind="ExternalInput")
    wf = nc.dram_tensor("wf", [256, D], F32, kind="ExternalInput")
    out_sh = nc.dram_tensor("out_sh", [1024, D], F32, kind="ExternalOutput")

    zag_in = [nc.dram_tensor(f"zag_in{i}", [D, QCW], BF16) for i in range(2)]
    zag_out = [nc.dram_tensor(f"zag_out{i}", [4 * D, QCW], BF16) for i in range(2)]
    rs_in = [nc.dram_tensor(f"rs_in{p}", [4 * 128, D], BF16) for p in range(8)]
    rs_out = [nc.dram_tensor(f"rs_out{p}", [128, D], BF16) for p in range(8)]
    y_spill = nc.dram_tensor("y_spill", [1024, D], F32)
    wag_in = nc.dram_tensor("wag_in", [256, D], BF16)
    wag_out = nc.dram_tensor("wag_out", [D, D], BF16)
    wdeq_spill = nc.dram_tensor("wdeq_spill", [DC, 128, D], BF16)
    if taps:
        dbg_am = nc.dram_tensor("dbg_am", [DC, 128, DC], F32, kind="ExternalOutput")
        dbg_r2 = nc.dram_tensor("dbg_r2", [DC, 128, DC], F32, kind="ExternalOutput")
        dbg_q8 = nc.dram_tensor("dbg_q8", [DC, 128, D], FP8, kind="ExternalOutput")
        dbg_dq = nc.dram_tensor("dbg_dq", [DC, 128, D], BF16, kind="ExternalOutput")
        dbg_y = nc.dram_tensor("dbg_y", [1024, D], F32, kind="ExternalOutput")
        dbg_wd = nc.dram_tensor("dbg_wd", [DC, 128, D], BF16, kind="ExternalOutput")
        dbg_zo = nc.dram_tensor("dbg_zo", [4 * D, 1024], BF16, kind="ExternalOutput")
        dbg_q = nc.dram_tensor("dbg_q", [128, NH * S], BF16, kind="ExternalOutput")
        dbg_k = nc.dram_tensor("dbg_k", [128, NH * S], BF16, kind="ExternalOutput")
        dbg_v = nc.dram_tensor("dbg_v", [128, 32 * 512], BF16, kind="ExternalOutput")
        dbg_m = nc.dram_tensor("dbg_m", [128, NH * S], BF16, kind="ExternalOutput")
        dbg_g = nc.dram_tensor("dbg_g", [128, 2 * S], BF16, kind="ExternalOutput")
        dbg_rs = nc.dram_tensor("dbg_rs", [128, D], BF16, kind="ExternalOutput")

    b = Builder(nc, paranoid=paranoid)
    with ExitStack() as ctx:
        for s in ("ld", "gs", "cc", "pe", "dv", "sc", "gp", "lx",
                  "lwf0", "lwf1", "lz0", "lz1", "lw30", "lw31", "lcs",
                  "lwo", "lwd", "ly0", "ly1", "lr0", "lr1", "gr0", "gr1",
                  "sy", "sw", "vy", "ts", "gz0", "gz1", "gw", "lwn0", "lwn1",
                  "lk0", "lk1", "lc2",
                  "lx0", "lx1"):
            b.add_sem(ctx, s)

        P = [ctx.enter_context(nc.psum_tensor(f"P{k}", [128, 1024], F32))
             for k in range(3)]
        pt = [ctx.enter_context(nc.psum_tensor(f"pt{k}", [128, 1024], BF16))
              for k in range(2)]

        class PV:
            def __init__(self, t, off):
                self.t = t
                self.off = off

            def __getitem__(self, idx):
                rows, cols = idx
                start = self.off + (cols.start or 0)
                stop = self.off + (cols.stop if cols.stop is not None else QCW)
                return self.t[rows, start:stop]

        ps = [PV(P[k // 2], QCW * (k % 2)) for k in range(6)]
        banks = ps + pt

        ident_f = ctx.enter_context(nc.sbuf_tensor("ident_f", [128, 128], F32))
        ident_b = ctx.enter_context(nc.sbuf_tensor("ident_b", [128, 128], BF16))
        mask_add = ctx.enter_context(nc.sbuf_tensor("mask_add", [128, 128], F32))
        ones_b = ctx.enter_context(nc.sbuf_tensor("ones_b", [128, 1], BF16))
        ones1f = ctx.enter_context(nc.sbuf_tensor("ones1f", [1, 128], F32))
        negC = ctx.enter_context(nc.sbuf_tensor("negC", [128, 1], F32))
        epsb = ctx.enter_context(nc.sbuf_tensor("epsb", [128, 1], F32))
        gb_sb = ctx.enter_context(nc.sbuf_tensor("gb_sb", [128, 1], F32))
        gwt_f = ctx.enter_context(nc.sbuf_tensor("gwt_f", [128, 128], F32))
        gwt_b = ctx.enter_context(nc.sbuf_tensor("gwt_b", [128, 128], BF16))
        rsum = ctx.enter_context(nc.sbuf_tensor("rsum", [128, 2], F32))
        nmean = ctx.enter_context(nc.sbuf_tensor("nmean", [128, 2], F32))
        ssq = ctx.enter_context(nc.sbuf_tensor("ssq", [128, 2], F32))
        sdvv = ctx.enter_context(nc.sbuf_tensor("sdvv", [128, 2], F32))
        rstd = ctx.enter_context(nc.sbuf_tensor("rstd", [128, 2], F32))
        tbx = ctx.enter_context(nc.sbuf_tensor("tbx", [128, 2], F32))
        rsum8 = ctx.enter_context(nc.sbuf_tensor("rsum8", [128, 8], F32))
        nm8 = ctx.enter_context(nc.sbuf_tensor("nm8", [128, 8], F32))
        ssq8 = ctx.enter_context(nc.sbuf_tensor("ssq8", [128, 8], F32))
        tb8 = ctx.enter_context(nc.sbuf_tensor("tb8", [128, 8], F32))
        sd8 = ctx.enter_context(nc.sbuf_tensor("sd8", [128, 8], F32))
        rst8 = ctx.enter_context(nc.sbuf_tensor("rst8", [128, 8], F32))
        nm8b = ctx.enter_context(nc.sbuf_tensor("nm8b", [128, 8], F32))
        ssq8b = ctx.enter_context(nc.sbuf_tensor("ssq8b", [128, 8], F32))
        tb8b = ctx.enter_context(nc.sbuf_tensor("tb8b", [128, 8], F32))
        sd8b = ctx.enter_context(nc.sbuf_tensor("sd8b", [128, 8], F32))
        rst8b = ctx.enter_context(nc.sbuf_tensor("rst8b", [128, 8], F32))

        def emit_ln_stats(xs, tiles, x_waits, NM, SQ, TB, SD, RS):
            for t in tiles:
                b.op("vector", "dv", lambda e, t=t: e.tensor_reduce(
                    out=rsum8[:, t:t + 1], in_=xs[:, t, :], axis=AX.X,
                    op=ALU.add, negate=True), waits=x_waits[t], risky=True)
            sc_q = 0
            for t in tiles:
                sc_q = b.op("scalar", "sc", lambda e, t=t: e.activation(
                    out=sq_scrap[:, 0, :], in_=xs[:, t, :], func=AF.Square,
                    scale=1.0, accum_out=SQ[:, t:t + 1]),
                    waits=list(x_waits[t]), chain=True)
            lo, n = tiles[0], len(tiles)
            b.op("vector", "dv", lambda e, lo=lo, n=n: e.tensor_scalar_mul(
                out=NM[:, lo:lo + n], in0=rsum8[:, lo:lo + n], scalar1=INV_D),
                risky=True)
            b.op("vector", "dv", lambda e, lo=lo, n=n: e.tensor_tensor(
                out=TB[:, lo:lo + n], in0=NM[:, lo:lo + n],
                in1=NM[:, lo:lo + n], op=ALU.mult), risky=True)
            dv_tb = b.op("vector", "dv", lambda e, lo=lo, n=n: e.tensor_scalar(
                out=TB[:, lo:lo + n], in0=TB[:, lo:lo + n], scalar1=-1.0,
                scalar2=EPS, op0=ALU.mult, op1=ALU.add), risky=True)
            sc_sd = 0
            for t in tiles:
                sc_sd = b.op("scalar", "sc", lambda e, t=t: e.activation(
                    out=SD[:, t:t + 1], in_=SQ[:, t:t + 1], func=AF.Sqrt,
                    bias=TB[:, t:t + 1], scale=INV_D),
                    waits=([("dv", dv_tb)] if t == tiles[0] else []), chain=True)
            dv_rs = b.op("vector", "dv", lambda e, lo=lo, n=n: e.reciprocal(
                out=RS[:, lo:lo + n], in_=SD[:, lo:lo + n]),
                waits=[("sc", sc_sd)], risky=True)
            return sc_sd, dv_rs
        nmean2 = ctx.enter_context(nc.sbuf_tensor("nmean2", [128, 2], F32))
        ssq2 = ctx.enter_context(nc.sbuf_tensor("ssq2", [128, 2], F32))
        sdv2 = ctx.enter_context(nc.sbuf_tensor("sdv2", [128, 2], F32))
        rstd2 = ctx.enter_context(nc.sbuf_tensor("rstd2", [128, 2], F32))
        tbx2 = ctx.enter_context(nc.sbuf_tensor("tbx2", [128, 2], F32))
        amax = ctx.enter_context(nc.sbuf_tensor("amax", [128, 2, DC], F32))
        r2 = ctx.enter_context(nc.sbuf_tensor("r2", [128, 2, DC], F32))
        s2 = ctx.enter_context(nc.sbuf_tensor("s2", [128, 2, DC], F32))
        sq_scrap = ctx.enter_context(nc.sbuf_tensor("sq_scrap", [128, 1, D], BF16))

        for (tname, val) in ((ident_f, 0.0), (ident_b, 0.0), (mask_add, 0.0),
                             (ones_b, 1.0), (ones1f, 1.0), (negC, -C_SHIFT),
                             (epsb, EPS)):
            b.op("gpsimd", "gp", lambda e, t=tname, v=val: e.memset(t[:], v), chain=True)
        b.op("gpsimd", "gp", lambda e: e.affine_select(
            out=ident_f[:], in_=ident_f[:], compare_op=ALU.not_equal,
            fill=1.0, base=0, pattern=[[-1, 128]], channel_multiplier=1), chain=True)
        b.op("gpsimd", "gp", lambda e: e.affine_select(
            out=ident_b[:], in_=ident_b[:], compare_op=ALU.not_equal,
            fill=1.0, base=0, pattern=[[-1, 128]], channel_multiplier=1), chain=True)
        gm = b.op("gpsimd", "gp", lambda e: e.affine_select(
            out=mask_add[:], in_=mask_add[:], compare_op=ALU.is_ge,
            fill=-1.0e30, base=0, pattern=[[1, 128]], channel_multiplier=-1), chain=True)
        b.dma("sync", "ld", gb_sb[:], gb[:, :])
        lgw = b.dma("sync", "ld", gwt_f[:], gwt[:, :])
        GWT_B = b.op("vector", "dv", lambda e: e.tensor_copy(out=gwt_b[:], in_=gwt_f[:]),
                     waits=[("ld", lgw)])

        ctxA = ExitStack()
        x_str = ctxA.enter_context(nc.sbuf_tensor("x_str", [128, TT, D], F32))
        wdeqT = ctxA.enter_context(nc.sbuf_tensor("wdeqT", [128, DC, D], BF16))
        xh = ctxA.enter_context(nc.sbuf_tensor("xh", [128, 2, D], F32))
        xq8 = ctxA.enter_context(nc.sbuf_tensor("xq8", [128, 1, D], FP8))
        xdq = ctxA.enter_context(nc.sbuf_tensor("xdq", [128, 2, D], BF16))
        xdT = ctxA.enter_context(nc.sbuf_tensor("xdT", [128, 2, DC, 128], BF16))
        scrA = dict(xh=xh, xq8=xq8, xdq=xdq, xdT=xdT, sq=sq_scrap)
        zbuf = ctxA.enter_context(nc.sbuf_tensor("zbuf", [128, 2, 4, 128], BF16))
        zln = ctxA.enter_context(nc.sbuf_tensor("zln", [128, 2, D], BF16))
        ctxA0 = ExitStack()
        wfn = ctxA0.enter_context(nc.sbuf_tensor("wfn", [128, 2, D], F32))
        wq8 = ctxA0.enter_context(nc.sbuf_tensor("wq8", [128, 1, D], FP8))
        wdq = ctxA0.enter_context(nc.sbuf_tensor("wdq", [128, 2, D], BF16))

        ld_x = []
        for t in range(TT):
            sem = "lx0" if t == 0 else ("lx1" if t == 1 else "lx")
            ld_x.append((sem, b.dma("sync", sem, x_str[:, t, :],
                                    x_sh[128 * t:128 * (t + 1), :])))
        ld_x = [(s, v if s != "lx" else b.cnt["lx"]) for (s, v) in ld_x]

        GP_INIT = gm
        pe_wtr = [[0] * 4 for _ in range(DC)]
        dv_wev_flat = [0] * DC
        gw_sp = 0
        for i in range(2):
            p = i % 2
            ldwf = b.dma("sync", f"lwf{p}", wfn[:, p, :], wf[128 * i:128 * (i + 1), :])
            b.op("vector", "dv", lambda e, p=p: e.tensor_reduce(
                out=amax[:, p, :],
                in_=wfn[:, p, :].rearrange("p (c k) -> p c k", k=128),
                axis=AX.X, op=ALU.max, apply_absolute_value=True),
                waits=[(f"lwf{p}", ldwf)], risky=True)
            b.op("vector", "dv", lambda e, p=p: e.reciprocal(
                out=r2[:, p, :], in_=amax[:, p, :]), risky=True)
            b.op("vector", "dv", lambda e, p=p: e.tensor_scalar_mul(
                out=r2[:, p, :], in0=r2[:, p, :], scalar1=224.0), risky=True)
            dv_r2w = b.op("vector", "dv", lambda e, p=p: e.tensor_scalar_mul(
                out=s2[:, p, :], in0=amax[:, p, :], scalar1=1.0 / 224.0), risky=True)
            for blk in range(DC):
                b.op("vector", "dv", lambda e, p=p, blk=blk: e.tensor_scalar(
                    out=wq8[:, 0, 128 * blk:128 * (blk + 1)],
                    in0=wfn[:, p, 128 * blk:128 * (blk + 1)],
                    scalar1=r2[:, p, blk:blk + 1], scalar2=None, op0=ALU.mult))
            dv_wdq = 0
            for blk in range(DC):
                dv_wdq = b.op("vector", "dv", lambda e, p=p, blk=blk: e.tensor_scalar(
                    out=wdq[:, p, 128 * blk:128 * (blk + 1)],
                    in0=wq8[:, 0, 128 * blk:128 * (blk + 1)],
                    scalar1=s2[:, p, blk:blk + 1], scalar2=None, op0=ALU.mult))
            gw_sp = b.dma("gpsimd", "gw", wag_in[128 * i:128 * (i + 1), :],
                          wdq[:, p, :], waits=[("dv", dv_wdq)])
        cc_w = b.op("gpsimd", "cc", lambda e: e.collective_compute(
            "AllGather", ALU.bypass, replica_groups=[list(range(8))],
            ins=[wag_in.ap().opt()], outs=[wag_out.ap().opt()]),
            waits=[("gw", gw_sp)], chain=True)
        for i in range(DC):
            p = i % 2
            ldn = b.dma("sync", f"lwn{p}", wdq[:, p, :],
                        wag_out[128 * i:128 * (i + 1), :],
                        waits=[("cc", cc_w)] + ([("pe", pe_wtr[i - 2][3])]
                                                if i >= 2 else [("gw", gw_sp)]))
            for k in range(4):
                bank = 6 + (k % 2)
                for j in range(4):
                    c = 4 * k + j
                    w_ = [(f"lwn{p}", ldn)] if (k == 0 and j == 0) else []
                    if j == 0:
                        w_ += b.use_bank(bank)
                    if (i == 0 and k == 0 and j == 0) or b.paranoid:
                        w_.append(("gp", GP_INIT))
                    pe_wtr[i][k] = b.op("tensor", "pe", lambda e, p=p, c=c, bank=bank, j=j: e.transpose(
                        banks[bank][:, 128 * j:128 * (j + 1)],
                        wdq[:, p, 128 * c:128 * (c + 1)], ident_b[:]),
                        waits=w_)
                dvev = b.op("vector", "dv", lambda e, i=i, k=k, bank=bank: e.tensor_copy(
                    out=wdeqT[:, 4 * k:4 * (k + 1), 128 * i:128 * (i + 1)],
                    in_=banks[bank][:, 0:512].rearrange("p (j k) -> p j k", k=128)),
                    waits=[("pe", pe_wtr[i][k])])
                b.done_bank(bank, "dv", dvev)
                dv_wev_flat[i] = dvev
        W_DONE = dvev

        def emit_ffn(xs, wT, w_ready, x_ready, scr, NM, RS, ln_sc, ln_dv,
                     store=None, store_sem="gs", hook=None, ln_per_tile=None):
            xh = scr["xh"]; xq8 = scr["xq8"]; xdq = scr["xdq"]
            xdT = scr["xdT"]
            sc_sdv = [0] * TT
            sc_ssqv = [0] * TT
            sc_q8 = [0] * TT
            dv_xdq = [0] * TT
            dv_ev = [[0] * 4 for _ in range(TT)]
            dv_res = [[0] * 4 for _ in range(TT)]
            pe_tr = [[0] * 4 for _ in range(TT)]
            pe_mm = [[0] * 4 for _ in range(TT)]
            st_out = []
            for t in range(TT):
                p = t % 2
                if ln_per_tile is not None:
                    lsc, ldv = ln_per_tile[t]
                else:
                    lsc, ldv = ln_sc, ln_dv
                b.op("vector", "dv", lambda e, t=t, p=p: e.tensor_scalar(
                    out=xh[:, p, :], in0=xs[:, t, :], scalar1=NM[:, t:t + 1],
                    scalar2=RS[:, t:t + 1], op0=ALU.add, op1=ALU.mult),
                    waits=[("dv", ldv), ("sc", lsc)] + list(x_ready[t]))
                b.op("vector", "dv", lambda e, p=p: e.tensor_reduce(
                    out=amax[:, p, :],
                    in_=xh[:, p, :].rearrange("p (c k) -> p c k", k=128),
                    axis=AX.X, op=ALU.max, apply_absolute_value=True), risky=True)
                b.op("vector", "dv", lambda e, p=p: e.reciprocal(
                    out=r2[:, p, :], in_=amax[:, p, :]), risky=True)
                b.op("vector", "dv", lambda e, p=p: e.tensor_scalar_mul(
                    out=r2[:, p, :], in0=r2[:, p, :], scalar1=224.0), risky=True)
                dv_r2 = b.op("vector", "dv", lambda e, p=p: e.tensor_scalar_mul(
                    out=s2[:, p, :], in0=amax[:, p, :], scalar1=1.0 / 224.0), risky=True)
                for blk in range(DC):
                    sc_q8[t] = b.op("vector", "dv", lambda e, p=p, blk=blk: e.tensor_scalar(
                        out=xq8[:, 0, 128 * blk:128 * (blk + 1)],
                        in0=xh[:, p, 128 * blk:128 * (blk + 1)],
                        scalar1=r2[:, p, blk:blk + 1], scalar2=None, op0=ALU.mult))
                for blk in range(DC):
                    w_ = [("pe", pe_tr[t - 2][3])] if (t >= 2 and blk == 0) else []
                    dv_xdq[t] = b.op("vector", "dv", lambda e, p=p, blk=blk: e.tensor_scalar(
                        out=xdq[:, p, 128 * blk:128 * (blk + 1)],
                        in0=xq8[:, 0, 128 * blk:128 * (blk + 1)],
                        scalar1=s2[:, p, blk:blk + 1], scalar2=None, op0=ALU.mult),
                        waits=w_)
                for k in range(4):
                    bank = 6 + (k % 2)
                    for j in range(4):
                        c = 4 * k + j
                        w_ = [("dv", dv_xdq[t])] if (k == 0 and j == 0) else []
                        if j == 0:
                            w_ += b.use_bank(bank)
                        if b.paranoid:
                            w_.append(("gp", gm))
                        pe_tr[t][k] = b.op("tensor", "pe", lambda e, p=p, c=c, bank=bank, j=j: e.transpose(
                            banks[bank][:, 128 * j:128 * (j + 1)],
                            xdq[:, p, 128 * c:128 * (c + 1)], ident_b[:]), waits=w_)
                    w_ev = [("pe", pe_tr[t][k])]
                    if t >= 2 and k == 0:
                        w_ev.append(("pe", pe_mm[t - 2][3]))
                    dv_ev[t][k] = b.op("vector", "dv", lambda e, k=k, p=p, bank=bank: e.tensor_copy(
                        out=xdT[:, p, 4 * k:4 * (k + 1), :],
                        in_=banks[bank][:, 0:512].rearrange("p (j k) -> p j k", k=128)),
                        waits=w_ev)
                    b.done_bank(bank, "dv", dv_ev[t][k])
                def emit_resid(tt):
                    for oc in range(4):
                        dv_res[tt][oc] = b.op("vector", "dv",
                                              lambda e, tt=tt, oc=oc: e.scalar_tensor_tensor(
                                                  out=xs[:, tt, 512 * oc:512 * (oc + 1)],
                                                  in0=ps[oc][:, 0:512], scalar=0.5,
                                                  in1=xs[:, tt, 512 * oc:512 * (oc + 1)],
                                                  op0=ALU.mult, op1=ALU.add),
                                              waits=[("pe", pe_mm[tt][oc])])
                        b.done_bank(oc, "dv", dv_res[tt][oc])
                    if store is not None:
                        eng = "scalar" if store_sem == "vy" else "gpsimd"
                        st_out.append(b.dma(eng, store_sem,
                                            store[128 * tt:128 * (tt + 1), :],
                                            xs[:, tt, :], waits=[("dv", dv_res[tt][3])]))
                if t >= 1:
                    emit_resid(t - 1)
                mm_first = True
                for oc in range(4):
                    for c in range(DC):
                        w_ = []
                        if c == 0:
                            w_ = [("dv", dv_ev[t][3]), w_ready[oc]] + b.use_bank(oc)
                        pe_mm[t][oc] = b.op("tensor", "pe", lambda e, p=p, c=c, oc=oc: e.matmul(
                            ps[oc][:, 0:512], xdT[:, p, c, :],
                            wT[:, c, 512 * oc:512 * (oc + 1)],
                            start=(c == 0), stop=(c == DC - 1)), waits=w_)
                if hook is not None and t >= 1:
                    hook(t - 1, dv_res[t - 1][3])
            emit_resid(TT - 1)
            if hook is not None:
                hook(TT - 1, dv_res[TT - 1][3])
            return dv_res, st_out

        W_OC = [("dv", 0)] * 4
        for oc in range(4):
            W_OC[oc] = ("dv", dv_wev_flat[4 * oc + 3])
        SW_SPILL = 0
        for c in range(DC):
            SW_SPILL = b.dma("gpsimd", "ts", wdeq_spill[c, :, :], wdeqT[:, c, :],
                             waits=[("dv", W_DONE)])
        st2 = {"sc_sd": [0] * TT, "pe_ztr": [0] * TT, "gs_zb": [[0] * 4 for _ in range(TT)],
               "cc_ag": [0, 0], "dv_z": [0] * TT}
        xh_a = scrA["xh"]

        def ln2_hook(t, dv_resv):
            if t != 3 and t != TT - 1:
                return
            half = 0 if t == 3 else 1
            tiles = [4 * half + i for i in range(4)]
            xw = {tt: [("dv", dv_resv)] for tt in tiles}
            sc_sd, dv_rs = emit_ln_stats(x_str, tiles, xw,
                                         nm8b, ssq8b, tb8b, sd8b, rst8b)
            for tt in tiles:
                p = tt % 2
                dv_z = b.op("vector", "dv", lambda e, tt=tt, p=p: e.tensor_scalar(
                    out=zln[:, p, :], in0=x_str[:, tt, :],
                    scalar1=nm8b[:, tt:tt + 1], scalar2=rst8b[:, tt:tt + 1],
                    op0=ALU.add, op1=ALU.mult),
                    waits=([("dv", dv_rs), ("sc", sc_sd)] if tt == tiles[0] else [])
                    + ([("pe", st2["pe_ztr"][tt - 2])] if tt >= 2 else []))
                st2["dv_z"][tt] = dv_z
                for k in range(4):
                    bank = 6 + (k % 2)
                    pe_tr = 0
                    for j in range(4):
                        c = 4 * k + j
                        w_ = [("dv", dv_z)] if (k == 0 and j == 0) else []
                        if j == 0:
                            w_ += b.use_bank(bank)
                        pe_tr = b.op("tensor", "pe", lambda e, p=p, c=c, bank=bank, j=j: e.transpose(
                            banks[bank][:, 128 * j:128 * (j + 1)],
                            zln[:, p, 128 * c:128 * (c + 1)], ident_b[:]), waits=w_)
                    st2["pe_ztr"][tt] = pe_tr
                    zpar = (4 * tt + k) % 2
                    w_ev = [("pe", pe_tr)]
                    if tt >= 1 or k >= 2:
                        pk = (4 * tt + k) - 2
                        w_ev.append((f"gz{zpar}", st2["gs_zb"][pk // 4][pk % 4]))
                    dv_ze = b.op("vector", "dv", lambda e, zpar=zpar, bank=bank: e.tensor_copy(
                        out=zbuf[:, zpar, :, :],
                        in_=banks[bank][:, 0:512].rearrange("p (j k) -> p j k", k=128)),
                        waits=w_ev)
                    b.done_bank(bank, "dv", dv_ze)
                    halfq, tq = divmod(tt, 4)
                    for j in range(4):
                        c = 4 * k + j
                        st2["gs_zb"][tt][k] = b.dma(
                            "gpsimd", f"gz{zpar}",
                            zag_in[halfq][128 * c:128 * (c + 1), 128 * tq:128 * (tq + 1)],
                            zbuf[:, zpar, j, :], waits=([("dv", dv_ze)] if j == 0 else []))
            st2["cc_ag"][half] = b.op("gpsimd", "cc", lambda e, half=half: e.collective_compute(
                "AllGather", ALU.bypass, replica_groups=GROUPS,
                ins=[zag_in[half].ap().opt()], outs=[zag_out[half].ap().opt()]),
                waits=[("gz0", b.cnt["gz0"]), ("gz1", b.cnt["gz1"])], chain=True)

        ln1_sc, ln1_dv = emit_ln_stats(x_str, list(range(TT)),
                                       [[ld_x[t]] for t in range(TT)],
                                       nm8, ssq8, tb8, sd8, rst8)
        dv_res1, y_st = emit_ffn(x_str, wdeqT, W_OC,
                                 [[ld_x[t]] for t in range(TT)], scrA,
                                 nm8, rst8, ln1_sc, ln1_dv,
                                 store=y_spill, store_sem="vy", hook=ln2_hook)
        cc_ag = st2["cc_ag"]
        GS_SPILL = b.cnt["gs"]
        FENCE2 = [(s, b.cnt[s]) for s in ("ts", "pe", "dv", "sc")]
        FENCE2V = FENCE2 + [(s, b.cnt[s]) for s in ("gz0", "gz1")]
        b.flush("ph01_wq_ffn1")
        ctxA0.close()

        ctxA.close()
        ctxBm = ExitStack()
        m_t = ctxBm.enter_context(nc.sbuf_tensor("m_t", [128, NH, S], BF16, side="right"))
        ctxB = ExitStack()
        q_t = ctxB.enter_context(nc.sbuf_tensor("q_t", [128, NH, S], BF16))
        k_t = ctxB.enter_context(nc.sbuf_tensor("k_t", [128, NH, S], BF16))
        v_t = ctxB.enter_context(nc.sbuf_tensor("v_t", [128, 32, 512], BF16))
        ctxB3 = ExitStack()
        zt_q = ctxB3.enter_context(nc.sbuf_tensor("zt_q", [128, DC, 512], BF16))
        wqf = ctxB3.enter_context(nc.sbuf_tensor("wqf", [128, 2, 512], F32))
        wq_b = ctxB3.enter_context(nc.sbuf_tensor("wq_b", [128, DC, 512], BF16))
        wk_b = ctxB3.enter_context(nc.sbuf_tensor("wk_b", [128, DC, 512], BF16))
        wv_b = ctxB3.enter_context(nc.sbuf_tensor("wv_b", [128, DC, 512], BF16))

        dv_w3 = 0
        nw = 0
        dv_wc = {}
        for (wsrc, wbbuf) in ((wq_t, wq_b), (wk_t, wk_b), (wv_t, wv_b)):
            for c in range(DC):
                p = nw % 2
                fence = FENCE2V if wbbuf is wv_b else FENCE2
                eng, semb = (("scalar", "lw3") if wbbuf is wq_b else ("gpsimd", "lk"))
                ldw = b.dma(eng, f"{semb}{p}", wqf[:, p, :],
                            wsrc[128 * c:128 * (c + 1), :],
                            waits=([("dv", dv_w3)] if nw >= 2 else fence))
                dv_w3 = b.op("vector", "dv",
                             lambda e, p=p, c=c, wbbuf=wbbuf: e.tensor_copy(
                                 out=wbbuf[:, c, :], in_=wqf[:, p, :]),
                             waits=[(f"{semb}{p}", ldw)])
                dv_wc[(id(wbbuf), c)] = dv_w3
                nw += 1
        pe_zread = [0] * DC
        for hq in range(8):
            qq, half = divmod(hq, 2)
            ldz = 0
            for c in range(DC):
                w_ = [("pe", pe_zread[c])] if hq >= 1 else list(FENCE2)
                ldz = b.dma("sync", f"lz{hq % 2}", zt_q[:, c, :],
                            zag_out[half][D * qq + 128 * c:D * qq + 128 * (c + 1), :],
                            waits=w_ + [("cc", cc_ag[half])])
            zq_ready = (f"lz{hq % 2}", ldz)
            for (wb, dst) in ((wq_b, q_t), (wk_b, k_t)):
                pe_h = [0] * NH
                for c in range(DC):
                    for h in range(NH):
                        w_ = []
                        if h == 0 and hq == 0:
                            w_.append(("dv", dv_wc[(id(wb), c)]))
                        if c == 0:
                            w_ = w_ + [zq_ready] + b.use_bank(h)
                            if hq == 0 and wb is wq_b:
                                w_.append(("vy", y_st[-1]))
                        pe_h[h] = b.op("tensor", "pe",
                                       lambda e, c=c, h=h, wb=wb: e.matmul(
                                           ps[h][:, 0:512],
                                           wb[:, c, 128 * h:128 * (h + 1)],
                                           zt_q[:, c, :],
                                           start=(c == 0), stop=(c == DC - 1)),
                                       waits=w_)
                for h in range(NH):
                    dvx = b.op("vector", "dv",
                               lambda e, h=h, hq=hq, dst=dst: e.tensor_copy(
                                   out=dst[:, h, 512 * hq:512 * (hq + 1)],
                                   in_=ps[h][:, 0:512]), waits=[("pe", pe_h[h])])
                    b.done_bank(h, "dv", dvx)
            pe_g = [0] * 4
            for c in range(DC):
                for u in range(4):
                    w_ = []
                    if c == 0:
                        w_ = [zq_ready, ("dv", dv_w3)] + b.use_bank(u)
                    pe_g[u] = b.op("tensor", "pe",
                                   lambda e, c=c, u=u: e.matmul(
                                       ps[u][:, 0:512],
                                       zt_q[:, c, 128 * u:128 * (u + 1)],
                                       wv_b[:, c, 0:512],
                                       start=(c == 0), stop=(c == DC - 1)), waits=w_)
                    pe_zread[c] = pe_g[u]
            for u in range(4):
                dvx = b.op("vector", "dv", lambda e, hq=hq, u=u: e.tensor_copy(
                    out=v_t[:, 4 * hq + u, :], in_=ps[u][:, 0:512]),
                    waits=[("pe", pe_g[u])])
                b.done_bank(u, "dv", dvx)
        PH3_DV = b.cnt["dv"]
        b.flush("ph3_qkv")
        ctxB3.close()

        ctxB4 = ExitStack()
        gate_h = ctxB4.enter_context(nc.sbuf_tensor("gate_h", [128, 2, S], BF16))
        av_sb = ctxB4.enter_context(nc.sbuf_tensor("av_sb", [128, 1, S], BF16))
        p_t = ctxB4.enter_context(nc.sbuf_tensor("p_t", [128, 3, 1024], BF16))
        cs_rows = [ctxB4.enter_context(nc.sbuf_tensor(f"cs_row{i}", [1, S], F32))
                   for i in range(2)]
        cs128 = ctxB4.enter_context(nc.sbuf_tensor("cs128", [128, 32], F32))
        rc128 = ctxB4.enter_context(nc.sbuf_tensor("rc128", [128, 32], F32))
        recip_b = ctxB4.enter_context(nc.sbuf_tensor("recip_b", [128, S], BF16))

        dv_mt = [0] * NH
        pt_pe = [0, 0, 0]
        tail_args = []

        def emit_tail(h, ph, dvc, scg):
            ld1 = b.dma("sync", "lcs", cs128[:, :], cs_rows[ph][0:1, :],
                        waits=[("dv", dvc)])
            dvr = b.op("vector", "dv", lambda e: e.reciprocal(
                out=rc128[:, :], in_=cs128[:, :]),
                waits=[("lcs", ld1)], risky=True)
            ld2 = b.dma("sync", "lc2", cs_rows[ph][0:1, :], rc128[:, :],
                        waits=[("dv", dvr)])
            dv_rb = 0
            for qc in range(NQC):
                bank = qc % 4
                w_ = b.use_bank(bank) + [("lc2", ld2)]
                peb = b.op("tensor", "pe", lambda e, qc=qc, bank=bank, ph=ph: e.matmul(
                    ps[bank][:, 0:512], ones1f[:],
                    cs_rows[ph][0:1, 512 * qc:512 * (qc + 1)],
                    start=True, stop=True), waits=w_)
                dv_rb = b.op("vector", "dv", lambda e, qc=qc, bank=bank: e.tensor_copy(
                    out=recip_b[:, 512 * qc:512 * (qc + 1)], in_=ps[bank][:, 0:512]),
                    waits=[("pe", peb)])
                b.done_bank(bank, "dv", dv_rb)
            dvt = b.op("vector", "dv", lambda e, ph=ph: e.tensor_tensor(
                out=recip_b[:, :], in0=av_sb[:, 0, :], in1=recip_b[:, :], op=ALU.mult),
                waits=[("dv", dv_rb), ("sc", scg)])
            dv_mt[h] = b.op("vector", "dv", lambda e, h=h, ph=ph: e.tensor_tensor(
                out=m_t[:, h, :], in0=recip_b[:, :], in1=gate_h[:, ph, :], op=ALU.mult),
                waits=[("dv", dvt)])
        for h in range(NH):
            ph = h % 2
            scg = 0
            for qc in range(NQC):
                bank = qc % 4
                w_ = b.use_bank(bank)
                if qc == 0:
                    w_ += [("gp", gm), ("dv", GWT_B)]
                    if h == 0:
                        w_.append(("dv", PH3_DV))
                    if h >= 2:
                        w_.append(("dv", dv_mt[h - 2]))
                peg = b.op("tensor", "pe", lambda e, h=h, qc=qc, bank=bank: e.matmul(
                    ps[bank][:, 0:512], gwt_b[:], q_t[:, h, 512 * qc:512 * (qc + 1)],
                    start=True, stop=True), waits=w_)
                scg = b.op("scalar", "sc", lambda e, ph=ph, qc=qc, bank=bank: e.activation(
                    out=gate_h[:, ph, 512 * qc:512 * (qc + 1)], in_=ps[bank][:, 0:512],
                    func=AF.Sigmoid, bias=gb_sb[:, 0:1], scale=1.0),
                    waits=[("pe", peg)] + ([("ld", 32), ("gp", gm)]
                                           if (h == 0 and qc == 0) else []),
                    chain=True)
                b.done_bank(bank, "sc", scg)
            if tail_args:
                emit_tail(*tail_args.pop(0))
            dva = 0
            dvc = 0
            for qc in range(NQC):
                klim = 4 * qc + 4
                npair = klim // 2
                pend = []

                def flush_avcs(h=h):
                    pr, slot, pslot, sc_e = pend.pop(0)
                    ki0, ki1 = 2 * pr, 2 * pr + 1
                    klim_ = pend_klim[0]
                    pe_av = pe_cs = 0
                    for u, ki in ((0, ki0), (1, ki1)):
                        w_ = [("sc", sc_e)]
                        if ki == 0:
                            w_ += b.use_bank(4)
                        pe_av = b.op("tensor", "pe",
                                     lambda e, h=h, ki=ki, pslot=pslot, u=u, klim_=klim_: e.matmul(
                                         ps[4][:, 0:512],
                                         v_t[:, ki, 128 * h:128 * (h + 1)],
                                         p_t[:, pslot, 512 * u:512 * (u + 1)],
                                         start=(ki == 0), stop=(ki == klim_ - 1)),
                                     waits=w_)
                    for u, ki in ((0, ki0), (1, ki1)):
                        w_ = [("sc", sc_e)]
                        if ki == 0:
                            w_ += b.use_bank(5)
                        pe_cs = b.op("tensor", "pe",
                                     lambda e, ki=ki, pslot=pslot, u=u, klim_=klim_: e.matmul(
                                         ps[5][0:1, 0:512], ones_b[:],
                                         p_t[:, pslot, 512 * u:512 * (u + 1)],
                                         start=(ki == 0), stop=(ki == klim_ - 1)),
                                     waits=w_)
                    pt_pe[pslot] = pe_cs
                    return pe_av, pe_cs

                pend_klim = [klim]
                pe_av_last = pe_cs_last = 0
                for pr in range(npair):
                    ki0, ki1 = 2 * pr, 2 * pr + 1
                    off0 = max(0, 128 * ki0 - 512 * qc)
                    off1 = max(0, 128 * ki1 - 512 * qc)
                    slot = pr % 2
                    pslot = pr % 3
                    wS = b.use_bank(2 * slot) + b.use_bank(2 * slot + 1)
                    pes = b.op("tensor", "pe",
                               lambda e, h=h, qc=qc, ki0=ki0, off0=off0, slot=slot: e.matmul(
                                   P[slot][:, off0:512],
                                   k_t[:, h, 128 * ki0:128 * (ki0 + 1)],
                                   q_t[:, h, 512 * qc + off0:512 * (qc + 1)],
                                   start=True, stop=True), waits=wS)
                    pes = b.op("tensor", "pe",
                               lambda e, h=h, qc=qc, ki1=ki1, off1=off1, slot=slot: e.matmul(
                                   P[slot][:, 512 + off1:1024],
                                   k_t[:, h, 128 * ki1:128 * (ki1 + 1)],
                                   q_t[:, h, 512 * qc + off1:512 * (qc + 1)],
                                   start=True, stop=True), waits=wS)
                    dv_ms = 0
                    if ki1 >= 4 * qc:
                        wm = [("pe", pes)]
                        if h == 0 and qc == 0 and pr == 0:
                            wm.append(("gp", gm))
                        if off0 or ki0 >= 4 * qc:
                            dv_ms = b.op("vector", "dv",
                                         lambda e, slot=slot, off0=off0: e.tensor_tensor(
                                             out=P[slot][:, off0:off0 + 128],
                                             in0=P[slot][:, off0:off0 + 128],
                                             in1=mask_add[:], op=ALU.add), waits=wm)
                            wm = []
                        dv_ms = b.op("vector", "dv",
                                     lambda e, slot=slot, off1=off1: e.tensor_tensor(
                                         out=P[slot][:, 512 + off1:512 + off1 + 128],
                                         in0=P[slot][:, 512 + off1:512 + off1 + 128],
                                         in1=mask_add[:], op=ALU.add), waits=wm)
                        if off0:
                            dv_ms = b.op("vector", "dv", lambda e, slot=slot, off0=off0: e.memset(
                                P[slot][:, 0:off0], -1.0e30))
                        if off1:
                            dv_ms = b.op("vector", "dv", lambda e, slot=slot, off1=off1: e.memset(
                                P[slot][:, 512:512 + off1], -1.0e30))
                    w_e = [("pe", pes), ("pe", pt_pe[pslot]), ("dv", dv_ms)]
                    sc_e = b.op("scalar", "sc",
                                lambda e, slot=slot, pslot=pslot: e.activation(
                                    out=p_t[:, pslot, :], in_=P[slot][:, 0:1024],
                                    func=AF.Exp, bias=negC[:, 0:1], scale=SCALE),
                                chain=True, waits=w_e)
                    b.done_bank(2 * slot, "sc", sc_e)
                    b.done_bank(2 * slot + 1, "sc", sc_e)
                    pend.append((pr, slot, pslot, sc_e))
                    if len(pend) > 2:
                        pe_av_last, pe_cs_last = flush_avcs()
                while pend:
                    pe_av_last, pe_cs_last = flush_avcs()
                dva = b.op("vector", "dv", lambda e, ph=ph, qc=qc: e.tensor_copy(
                    out=av_sb[:, 0, 512 * qc:512 * (qc + 1)], in_=ps[4][:, 0:512]),
                    waits=[("pe", pe_av_last)])
                b.done_bank(4, "dv", dva)
                dvc = b.op("vector", "dv", lambda e, qc=qc, ph=ph: e.tensor_copy(
                    out=cs_rows[ph][0:1, 512 * qc:512 * (qc + 1)],
                    in_=ps[5][0:1, 0:512]),
                    waits=[("pe", pe_cs_last)])
                b.done_bank(5, "dv", dvc)
            tail_args.append((h, ph, dvc, scg))
        while tail_args:
            emit_tail(*tail_args.pop(0))
        b.flush("ph4_attn")
        ctxB4.close()
        ctxB.close()

        ctxC = ExitStack()
        x_str2 = ctxC.enter_context(nc.sbuf_tensor("x_str2", [128, TT, D], F32))
        wdeqT2 = ctxC.enter_context(nc.sbuf_tensor("wdeqT2", [128, DC, D], BF16))
        ctxC0 = ExitStack()
        wof = ctxC0.enter_context(nc.sbuf_tensor("wof", [128, 1, D], F32))
        wo_b = ctxC0.enter_context(nc.sbuf_tensor("wo_b", [128, 4, D], BF16))
        rs_buf = ctxC0.enter_context(nc.sbuf_tensor("rs_buf", [128, 2, 2048], BF16))

        dv_wo = 0
        for c in range(4):
            ldw = b.dma("sync", "lwo", wof[:, 0, :], wo_t[128 * c:128 * (c + 1), :],
                        waits=([("dv", dv_wo)] if c >= 1 else [("dv", dv_mt[NH - 1])]))
            dv_wo = b.op("vector", "dv", lambda e, c=c: e.tensor_copy(
                out=wo_b[:, c, :], in_=wof[:, 0, :]), waits=[("lwo", ldw)])
        cc_rs = [0] * 8
        gr_last = [0, 0]
        nqt = 0
        for j in range(TT):
            for g in range(4):
                qt = 8 * g + j
                slot = nqt % 2
                dvo = 0
                for half in range(2):
                    peo = 0
                    for e2p in (2 * half, 2 * half + 1):
                        view = e2p
                        for c in range(4):
                            w_ = []
                            if c == 0:
                                w_ = b.use_bank(view)
                                if nqt == 0 and e2p == 0:
                                    w_ += [("dv", dv_wo), ("dv", dv_mt[NH - 1]),
                                           ("dv", dv_mt[NH - 2])]
                            peo = b.op("tensor", "pe",
                                       lambda e, e2p=e2p, qt=qt, c=c, view=view: e.matmul(
                                           ps[view][:, 0:512],
                                           m_t[:, c, 128 * qt:128 * (qt + 1)],
                                           wo_b[:, c, 512 * e2p:512 * (e2p + 1)],
                                           start=(c == 0), stop=(c == 3)), waits=w_)
                    Pt = P[half]
                    dvo = b.op("vector", "dv", lambda e, Pt=Pt, slot=slot, half=half: e.tensor_copy(
                        out=rs_buf[:, slot, 1024 * half:1024 * (half + 1)],
                        in_=Pt[:, 0:1024]),
                        waits=[("pe", peo), (f"gr{slot}", gr_last[slot])])
                    b.done_bank(2 * half, "dv", dvo)
                    b.done_bank(2 * half + 1, "dv", dvo)
                gr_last[slot] = b.dma("gpsimd" if slot == 0 else "scalar",
                                      f"gr{slot}",
                                      rs_in[j][128 * g:128 * (g + 1), :],
                                      rs_buf[:, slot, :], waits=[("dv", dvo)])
                nqt += 1
            cc_rs[j] = b.op("gpsimd", "cc", lambda e, j=j: e.collective_compute(
                "ReduceScatter", ALU.add, replica_groups=GROUPS,
                ins=[rs_in[j].ap().opt()], outs=[rs_out[j].ap().opt()]),
                waits=[("gr0", gr_last[0]), ("gr1", gr_last[1])], chain=True)
        ld_wd = 0
        for c in range(DC):
            ld_wd = b.dma("scalar", "lwd", wdeqT2[:, c, :], wdeq_spill[c, :, :],
                          waits=[("ts", SW_SPILL), ("dv", dv_mt[NH - 1])])
        if taps:
            gt = b.dma("gpsimd", "gs", dbg_y[:, :], y_spill[:, :],
                       waits=[("gs", GS_SPILL)])
            gt = b.dma("gpsimd", "gs", dbg_wd[:, :, :], wdeq_spill[:, :, :])
            gt = b.dma("gpsimd", "gs", dbg_zo[:, 0:512], zag_out[0][:, :],
                       waits=[("cc", cc_ag[1])])
            gt = b.dma("gpsimd", "gs", dbg_zo[:, 512:1024], zag_out[1][:, :])
            gt = b.dma("gpsimd", "gs", dbg_rs[:, :], rs_out[0][:, :],
                       waits=[("cc", cc_rs[7])])
            b.wait("gpsimd", [("gs", gt)])
        b.flush("ph5_oproj_rs")
        ctxC0.close()
        ctxBm.close()
        y_buf = ctxC.enter_context(nc.sbuf_tensor("y_buf", [128, 2, D], F32))
        rs_sb = ctxC.enter_context(nc.sbuf_tensor("rs_sb", [128, 2, D], BF16))
        xh2 = ctxC.enter_context(nc.sbuf_tensor("xh2", [128, 2, D], F32))
        xq82 = ctxC.enter_context(nc.sbuf_tensor("xq82", [128, 1, D], FP8))
        xdq2 = ctxC.enter_context(nc.sbuf_tensor("xdq2", [128, 2, D], BF16))
        xdT2 = ctxC.enter_context(nc.sbuf_tensor("xdT2", [128, 2, DC, 128], BF16))
        scrC = dict(xh=xh2, xq8=xq82, xdq=xdq2, xdT=xdT2, sq=sq_scrap)

        dv_x2 = [0] * TT
        for t in range(TT):
            p = t % 2
            w_ = [("dv", dv_x2[t - 2])] if t >= 2 else []
            ldy = b.dma("sync", f"ly{p}", y_buf[:, p, :],
                        y_spill[128 * t:128 * (t + 1), :],
                        waits=w_ + [("vy", y_st[-1]), ("dv", dv_mt[NH - 1])])
            ldr = b.dma("sync", f"lr{p}", rs_sb[:, p, :],
                        rs_out[t][:, :], waits=w_ + [("cc", cc_rs[t])])
            dv_x2[t] = b.op("vector", "dv", lambda e, t=t, p=p: e.tensor_tensor(
                out=x_str2[:, t, :], in0=y_buf[:, p, :],
                in1=rs_sb[:, p, :], op=ALU.add),
                waits=[(f"ly{p}", ldy), (f"lr{p}", ldr)])
        ln3 = [emit_ln_stats(x_str2, [t], {t: [("dv", dv_x2[t])]},
                             nm8, ssq8, tb8, sd8, rst8) for t in range(TT)]
        _, st_out = emit_ffn(x_str2, wdeqT2, [("lwd", ld_wd)] * 4,
                             [[("dv", dv_x2[t])] for t in range(TT)], scrC,
                             nm8, rst8, 0, 0, store=out_sh, ln_per_tile=ln3)
        b.wait("gpsimd", [("gs", st_out[-1])])
        b.flush("ph7_ffn2")
        ctxC.close()

    return nc


_cache = {}


def kernel(L0_attn_x, L0_attn_q_w, L0_attn_k_w, L0_attn_v_w, L0_attn_o_w,
           L0_attn_gate_w, L0_attn_gate_b, L0_ffn_weight):
    x = np.ascontiguousarray(np.asarray(L0_attn_x, dtype=np.float32).reshape(8192, 2048))
    qw = np.asarray(L0_attn_q_w, dtype=np.float32)
    kw = np.asarray(L0_attn_k_w, dtype=np.float32)
    vw = np.asarray(L0_attn_v_w, dtype=np.float32)
    ow = np.asarray(L0_attn_o_w, dtype=np.float32)
    gw = np.asarray(L0_attn_gate_w, dtype=np.float32)
    gbv = np.ascontiguousarray(np.asarray(L0_attn_gate_b, dtype=np.float32).reshape(128, 1))
    fw = np.ascontiguousarray(np.asarray(L0_ffn_weight, dtype=np.float32))
    owt = np.ascontiguousarray(ow.T)
    gwt_host = np.ascontiguousarray(gw.T)

    if "nc" not in _cache:
        _cache["nc"] = build()
    nc = _cache["nc"]

    in_maps = []
    for c in range(NCORES):
        bb, r = divmod(c, 4)
        sl = slice(512 * r, 512 * (r + 1))
        in_maps.append({
            "x_sh": np.ascontiguousarray(x[4096 * bb + 1024 * r: 4096 * bb + 1024 * (r + 1)]),
            "wq_t": np.ascontiguousarray(qw[sl, :].T),
            "wk_t": np.ascontiguousarray(kw[sl, :].T),
            "wv_t": np.ascontiguousarray(vw[sl, :].T),
            "wo_t": np.ascontiguousarray(owt[sl, :]),
            "gwt": gwt_host,
            "gb": gbv,
            "wf": np.ascontiguousarray(fw[256 * c:256 * (c + 1)]),
        })
    res = run_bass_kernel_spmd(nc, in_maps, core_ids=list(range(NCORES)))
    out = np.concatenate([np.asarray(res.results[c]["out_sh"]) for c in range(NCORES)],
                         axis=0)
    return out.reshape(2, 4096, 2048).astype(np.float32)


# version 17
# speedup vs baseline: 1.0192x; 1.0192x over previous
import numpy as np
from contextlib import ExitStack

import concourse.bass as bass
import concourse.mybir as mybir
from concourse.bass_utils import run_bass_kernel_spmd

F32 = mybir.dt.float32
BF16 = mybir.dt.bfloat16
FP8 = mybir.dt.float8e4
AF = mybir.ActivationFunctionType
ALU = mybir.AluOpType
AX = mybir.AxisListType

NCORES = 8
TT = 8
D = 2048
DC = 16
S = 4096
NH = 4
QCW = 512
NQC = S // QCW
C_SHIFT = 20.0
SCALE = 0.08838834764831845
INV_D = 1.0 / 2048.0
EPS = 1e-5
GROUPS = [[0, 1, 2, 3], [4, 5, 6, 7]]


class Builder:
    def __init__(self, nc, paranoid=False):
        self.nc = nc
        self.paranoid = paranoid
        self.q = {k: [] for k in ("sync", "gpsimd", "vector", "scalar", "tensor")}
        self.sem = {}
        self.cnt = {}
        self.bank_state = [None] * 8
        self.risky = {}

    def add_sem(self, ctx, name):
        self.sem[name] = ctx.enter_context(self.nc.semaphore(name))
        self.cnt[name] = 0

    def flush(self, blockname):
        with self.nc.Block(blockname) as block:
            for eng, fns in self.q.items():
                if fns:
                    getattr(block, eng)(lambda e, fns=tuple(fns): [f(e) for f in fns])
        self.q = {k: [] for k in self.q}

    def op(self, eng, semname, fn, waits=(), inc=1, chain=False, risky=False):
        w = [(s, v) for (s, v) in waits if v > 0]
        if chain or (self.paranoid and eng == "vector"):
            prev = self.cnt[semname]
            if prev > 0:
                w.append((semname, prev))
        pend = self.risky.pop(eng, None)
        if pend is not None:
            w.append(pend)
        self.cnt[semname] += inc
        if risky:
            self.risky[eng] = (semname, self.cnt[semname])
        sems = self.sem

        def run(e, w=tuple(w), fn=fn, semname=semname, inc=inc):
            seen = {}
            for (s, v) in w:
                if seen.get(s, -1) < v:
                    e.wait_ge(sems[s], v)
                    seen[s] = v
            fn(e).then_inc(sems[semname], inc)
        self.q[eng].append(run)
        return self.cnt[semname]

    def wait(self, eng, waits):
        w = [(s, v) for (s, v) in waits if v > 0]
        sems = self.sem

        def run(e):
            for (s, v) in w:
                e.wait_ge(sems[s], v)
        self.q[eng].append(run)

    def dma(self, eng, semname, out, in_, waits=()):
        return self.op(eng, semname, lambda e: e.dma_start(out=out, in_=in_),
                       waits=waits, inc=16)

    def use_bank(self, k):
        st = self.bank_state[k]
        return [st] if st is not None else []

    def done_bank(self, k, semname, val):
        self.bank_state[k] = (semname, val)


def build(paranoid=False, taps=False):
    nc = bass.Bass("TRN2", target_bir_lowering=False)

    x_sh = nc.dram_tensor("x_sh", [1024, D], F32, kind="ExternalInput")
    wq_t = nc.dram_tensor("wq_t", [D, 512], F32, kind="ExternalInput")
    wk_t = nc.dram_tensor("wk_t", [D, 512], F32, kind="ExternalInput")
    wv_t = nc.dram_tensor("wv_t", [D, 512], F32, kind="ExternalInput")
    wo_t = nc.dram_tensor("wo_t", [512, D], F32, kind="ExternalInput")
    gwt = nc.dram_tensor("gwt", [128, 128], F32, kind="ExternalInput")
    gb = nc.dram_tensor("gb", [128, 1], F32, kind="ExternalInput")
    wf = nc.dram_tensor("wf", [256, D], F32, kind="ExternalInput")
    out_sh = nc.dram_tensor("out_sh", [1024, D], F32, kind="ExternalOutput")

    zag_in = [nc.dram_tensor(f"zag_in{i}", [D, QCW], BF16) for i in range(2)]
    zag_out = [nc.dram_tensor(f"zag_out{i}", [4 * D, QCW], BF16) for i in range(2)]
    rs_in = [nc.dram_tensor(f"rs_in{p}", [4 * 128, D], BF16) for p in range(8)]
    rs_out = [nc.dram_tensor(f"rs_out{p}", [128, D], BF16) for p in range(8)]
    y_spill = nc.dram_tensor("y_spill", [1024, D], F32)
    wag_in = nc.dram_tensor("wag_in", [256, D], BF16)
    wag_out = nc.dram_tensor("wag_out", [D, D], BF16)
    wdeq_spill = nc.dram_tensor("wdeq_spill", [DC, 128, D], BF16)
    if taps:
        dbg_am = nc.dram_tensor("dbg_am", [DC, 128, DC], F32, kind="ExternalOutput")
        dbg_r2 = nc.dram_tensor("dbg_r2", [DC, 128, DC], F32, kind="ExternalOutput")
        dbg_q8 = nc.dram_tensor("dbg_q8", [DC, 128, D], FP8, kind="ExternalOutput")
        dbg_dq = nc.dram_tensor("dbg_dq", [DC, 128, D], BF16, kind="ExternalOutput")
        dbg_y = nc.dram_tensor("dbg_y", [1024, D], F32, kind="ExternalOutput")
        dbg_wd = nc.dram_tensor("dbg_wd", [DC, 128, D], BF16, kind="ExternalOutput")
        dbg_zo = nc.dram_tensor("dbg_zo", [4 * D, 1024], BF16, kind="ExternalOutput")
        dbg_q = nc.dram_tensor("dbg_q", [128, NH * S], BF16, kind="ExternalOutput")
        dbg_k = nc.dram_tensor("dbg_k", [128, NH * S], BF16, kind="ExternalOutput")
        dbg_v = nc.dram_tensor("dbg_v", [128, 32 * 512], BF16, kind="ExternalOutput")
        dbg_m = nc.dram_tensor("dbg_m", [128, NH * S], BF16, kind="ExternalOutput")
        dbg_g = nc.dram_tensor("dbg_g", [128, 2 * S], BF16, kind="ExternalOutput")
        dbg_rs = nc.dram_tensor("dbg_rs", [128, D], BF16, kind="ExternalOutput")

    b = Builder(nc, paranoid=paranoid)
    with ExitStack() as ctx:
        for s in ("ld", "gs", "cc", "pe", "dv", "sc", "gp", "lx",
                  "lwf0", "lwf1", "lz0", "lz1", "lw30", "lw31", "lcs",
                  "lwo", "lwd", "ly0", "ly1", "lr0", "lr1", "gr0", "gr1",
                  "sy", "sw", "vy", "ts", "gz0", "gz1", "gw", "lwn0", "lwn1",
                  "lk0", "lk1", "lc2",
                  "lx0", "lx1"):
            b.add_sem(ctx, s)

        P = [ctx.enter_context(nc.psum_tensor(f"P{k}", [128, 1024], F32))
             for k in range(3)]
        pt = [ctx.enter_context(nc.psum_tensor(f"pt{k}", [128, 1024], BF16))
              for k in range(2)]

        class PV:
            def __init__(self, t, off):
                self.t = t
                self.off = off

            def __getitem__(self, idx):
                rows, cols = idx
                start = self.off + (cols.start or 0)
                stop = self.off + (cols.stop if cols.stop is not None else QCW)
                return self.t[rows, start:stop]

        ps = [PV(P[k // 2], QCW * (k % 2)) for k in range(6)]
        banks = ps + pt

        ident_f = ctx.enter_context(nc.sbuf_tensor("ident_f", [128, 128], F32))
        ident_b = ctx.enter_context(nc.sbuf_tensor("ident_b", [128, 128], BF16))
        mask_add = ctx.enter_context(nc.sbuf_tensor("mask_add", [128, 128], F32))
        ones_b = ctx.enter_context(nc.sbuf_tensor("ones_b", [128, 1], BF16))
        ones1f = ctx.enter_context(nc.sbuf_tensor("ones1f", [1, 128], F32))
        ones1b = ctx.enter_context(nc.sbuf_tensor("ones1b", [1, 128], BF16))
        negC = ctx.enter_context(nc.sbuf_tensor("negC", [128, 1], F32))
        epsb = ctx.enter_context(nc.sbuf_tensor("epsb", [128, 1], F32))
        gb_sb = ctx.enter_context(nc.sbuf_tensor("gb_sb", [128, 1], F32))
        gwt_f = ctx.enter_context(nc.sbuf_tensor("gwt_f", [128, 128], F32))
        gwt_b = ctx.enter_context(nc.sbuf_tensor("gwt_b", [128, 128], BF16))
        rsum = ctx.enter_context(nc.sbuf_tensor("rsum", [128, 2], F32))
        nmean = ctx.enter_context(nc.sbuf_tensor("nmean", [128, 2], F32))
        ssq = ctx.enter_context(nc.sbuf_tensor("ssq", [128, 2], F32))
        sdvv = ctx.enter_context(nc.sbuf_tensor("sdvv", [128, 2], F32))
        rstd = ctx.enter_context(nc.sbuf_tensor("rstd", [128, 2], F32))
        tbx = ctx.enter_context(nc.sbuf_tensor("tbx", [128, 2], F32))
        rsum8 = ctx.enter_context(nc.sbuf_tensor("rsum8", [128, 8], F32))
        nm8 = ctx.enter_context(nc.sbuf_tensor("nm8", [128, 8], F32))
        ssq8 = ctx.enter_context(nc.sbuf_tensor("ssq8", [128, 8], F32))
        tb8 = ctx.enter_context(nc.sbuf_tensor("tb8", [128, 8], F32))
        sd8 = ctx.enter_context(nc.sbuf_tensor("sd8", [128, 8], F32))
        rst8 = ctx.enter_context(nc.sbuf_tensor("rst8", [128, 8], F32))
        nm8b = ctx.enter_context(nc.sbuf_tensor("nm8b", [128, 8], F32))
        ssq8b = ctx.enter_context(nc.sbuf_tensor("ssq8b", [128, 8], F32))
        tb8b = ctx.enter_context(nc.sbuf_tensor("tb8b", [128, 8], F32))
        sd8b = ctx.enter_context(nc.sbuf_tensor("sd8b", [128, 8], F32))
        rst8b = ctx.enter_context(nc.sbuf_tensor("rst8b", [128, 8], F32))

        def emit_ln_stats(xs, tiles, x_waits, NM, SQ, TB, SD, RS):
            for t in tiles:
                b.op("vector", "dv", lambda e, t=t: e.tensor_reduce(
                    out=rsum8[:, t:t + 1], in_=xs[:, t, :], axis=AX.X,
                    op=ALU.add, negate=True), waits=x_waits[t], risky=True)
            sc_q = 0
            for t in tiles:
                sc_q = b.op("scalar", "sc", lambda e, t=t: e.activation(
                    out=sq_scrap[:, 0, :], in_=xs[:, t, :], func=AF.Square,
                    scale=1.0, accum_out=SQ[:, t:t + 1]),
                    waits=list(x_waits[t]), chain=True)
            lo, n = tiles[0], len(tiles)
            b.op("vector", "dv", lambda e, lo=lo, n=n: e.tensor_scalar_mul(
                out=NM[:, lo:lo + n], in0=rsum8[:, lo:lo + n], scalar1=INV_D),
                risky=True)
            b.op("vector", "dv", lambda e, lo=lo, n=n: e.tensor_tensor(
                out=TB[:, lo:lo + n], in0=NM[:, lo:lo + n],
                in1=NM[:, lo:lo + n], op=ALU.mult), risky=True)
            dv_tb = b.op("vector", "dv", lambda e, lo=lo, n=n: e.tensor_scalar(
                out=TB[:, lo:lo + n], in0=TB[:, lo:lo + n], scalar1=-1.0,
                scalar2=EPS, op0=ALU.mult, op1=ALU.add), risky=True)
            sc_sd = 0
            for t in tiles:
                sc_sd = b.op("scalar", "sc", lambda e, t=t: e.activation(
                    out=SD[:, t:t + 1], in_=SQ[:, t:t + 1], func=AF.Sqrt,
                    bias=TB[:, t:t + 1], scale=INV_D),
                    waits=([("dv", dv_tb)] if t == tiles[0] else []), chain=True)
            dv_rs = b.op("vector", "dv", lambda e, lo=lo, n=n: e.reciprocal(
                out=RS[:, lo:lo + n], in_=SD[:, lo:lo + n]),
                waits=[("sc", sc_sd)], risky=True)
            return sc_sd, dv_rs
        nmean2 = ctx.enter_context(nc.sbuf_tensor("nmean2", [128, 2], F32))
        ssq2 = ctx.enter_context(nc.sbuf_tensor("ssq2", [128, 2], F32))
        sdv2 = ctx.enter_context(nc.sbuf_tensor("sdv2", [128, 2], F32))
        rstd2 = ctx.enter_context(nc.sbuf_tensor("rstd2", [128, 2], F32))
        tbx2 = ctx.enter_context(nc.sbuf_tensor("tbx2", [128, 2], F32))
        amax = ctx.enter_context(nc.sbuf_tensor("amax", [128, 2, DC], F32))
        r2 = ctx.enter_context(nc.sbuf_tensor("r2", [128, 2, DC], F32))
        s2 = ctx.enter_context(nc.sbuf_tensor("s2", [128, 2, DC], F32))
        sq_scrap = ctx.enter_context(nc.sbuf_tensor("sq_scrap", [128, 1, D], BF16))

        for (tname, val) in ((ident_f, 0.0), (ident_b, 0.0), (mask_add, 0.0),
                             (ones_b, 1.0), (ones1f, 1.0), (ones1b, 1.0),
                             (negC, -C_SHIFT), (epsb, EPS)):
            b.op("gpsimd", "gp", lambda e, t=tname, v=val: e.memset(t[:], v), chain=True)
        b.op("gpsimd", "gp", lambda e: e.affine_select(
            out=ident_f[:], in_=ident_f[:], compare_op=ALU.not_equal,
            fill=1.0, base=0, pattern=[[-1, 128]], channel_multiplier=1), chain=True)
        b.op("gpsimd", "gp", lambda e: e.affine_select(
            out=ident_b[:], in_=ident_b[:], compare_op=ALU.not_equal,
            fill=1.0, base=0, pattern=[[-1, 128]], channel_multiplier=1), chain=True)
        gm = b.op("gpsimd", "gp", lambda e: e.affine_select(
            out=mask_add[:], in_=mask_add[:], compare_op=ALU.is_ge,
            fill=-1.0e30, base=0, pattern=[[1, 128]], channel_multiplier=-1), chain=True)
        b.dma("sync", "ld", gb_sb[:], gb[:, :])
        lgw = b.dma("sync", "ld", gwt_f[:], gwt[:, :])
        GWT_B = b.op("vector", "dv", lambda e: e.tensor_copy(out=gwt_b[:], in_=gwt_f[:]),
                     waits=[("ld", lgw)])

        ctxA = ExitStack()
        x_str = ctxA.enter_context(nc.sbuf_tensor("x_str", [128, TT, D], F32))
        wdeqT = ctxA.enter_context(nc.sbuf_tensor("wdeqT", [128, DC, D], BF16))
        xh = ctxA.enter_context(nc.sbuf_tensor("xh", [128, 2, D], F32))
        xq8 = ctxA.enter_context(nc.sbuf_tensor("xq8", [128, 1, D], FP8))
        xdq = ctxA.enter_context(nc.sbuf_tensor("xdq", [128, 2, D], BF16))
        xdT = ctxA.enter_context(nc.sbuf_tensor("xdT", [128, 2, DC, 128], BF16))
        scrA = dict(xh=xh, xq8=xq8, xdq=xdq, xdT=xdT, sq=sq_scrap)
        zbuf = ctxA.enter_context(nc.sbuf_tensor("zbuf", [128, 2, 4, 128], BF16))
        zln = ctxA.enter_context(nc.sbuf_tensor("zln", [128, 2, D], BF16))
        ctxA0 = ExitStack()
        wfn = ctxA0.enter_context(nc.sbuf_tensor("wfn", [128, 2, D], F32))
        wq8 = ctxA0.enter_context(nc.sbuf_tensor("wq8", [128, 1, D], FP8))
        wdq = ctxA0.enter_context(nc.sbuf_tensor("wdq", [128, 2, D], BF16))

        ld_x = []
        for t in range(TT):
            sem = "lx0" if t == 0 else ("lx1" if t == 1 else "lx")
            ld_x.append((sem, b.dma("sync", sem, x_str[:, t, :],
                                    x_sh[128 * t:128 * (t + 1), :])))
        ld_x = [(s, v if s != "lx" else b.cnt["lx"]) for (s, v) in ld_x]

        GP_INIT = gm
        pe_wtr = [[0] * 4 for _ in range(DC)]
        dv_wev_flat = [0] * DC
        gw_sp = 0
        for i in range(2):
            p = i % 2
            ldwf = b.dma("sync", f"lwf{p}", wfn[:, p, :], wf[128 * i:128 * (i + 1), :])
            b.op("vector", "dv", lambda e, p=p: e.tensor_reduce(
                out=amax[:, p, :],
                in_=wfn[:, p, :].rearrange("p (c k) -> p c k", k=128),
                axis=AX.X, op=ALU.max, apply_absolute_value=True),
                waits=[(f"lwf{p}", ldwf)], risky=True)
            b.op("vector", "dv", lambda e, p=p: e.reciprocal(
                out=r2[:, p, :], in_=amax[:, p, :]), risky=True)
            b.op("vector", "dv", lambda e, p=p: e.tensor_scalar_mul(
                out=r2[:, p, :], in0=r2[:, p, :], scalar1=224.0), risky=True)
            dv_r2w = b.op("vector", "dv", lambda e, p=p: e.tensor_scalar_mul(
                out=s2[:, p, :], in0=amax[:, p, :], scalar1=1.0 / 224.0), risky=True)
            for blk in range(DC):
                b.op("vector", "dv", lambda e, p=p, blk=blk: e.tensor_scalar(
                    out=wq8[:, 0, 128 * blk:128 * (blk + 1)],
                    in0=wfn[:, p, 128 * blk:128 * (blk + 1)],
                    scalar1=r2[:, p, blk:blk + 1], scalar2=None, op0=ALU.mult))
            dv_wdq = 0
            for blk in range(DC):
                dv_wdq = b.op("vector", "dv", lambda e, p=p, blk=blk: e.tensor_scalar(
                    out=wdq[:, p, 128 * blk:128 * (blk + 1)],
                    in0=wq8[:, 0, 128 * blk:128 * (blk + 1)],
                    scalar1=s2[:, p, blk:blk + 1], scalar2=None, op0=ALU.mult))
            gw_sp = b.dma("gpsimd", "gw", wag_in[128 * i:128 * (i + 1), :],
                          wdq[:, p, :], waits=[("dv", dv_wdq)])
        cc_w = b.op("gpsimd", "cc", lambda e: e.collective_compute(
            "AllGather", ALU.bypass, replica_groups=[list(range(8))],
            ins=[wag_in.ap().opt()], outs=[wag_out.ap().opt()]),
            waits=[("gw", gw_sp)], chain=True)
        for i in range(DC):
            p = i % 2
            ldn = b.dma("sync", f"lwn{p}", wdq[:, p, :],
                        wag_out[128 * i:128 * (i + 1), :],
                        waits=[("cc", cc_w)] + ([("pe", pe_wtr[i - 2][3])]
                                                if i >= 2 else [("gw", gw_sp)]))
            for k in range(4):
                bank = 6 + (k % 2)
                for j in range(4):
                    c = 4 * k + j
                    w_ = [(f"lwn{p}", ldn)] if (k == 0 and j == 0) else []
                    if j == 0:
                        w_ += b.use_bank(bank)
                    if (i == 0 and k == 0 and j == 0) or b.paranoid:
                        w_.append(("gp", GP_INIT))
                    pe_wtr[i][k] = b.op("tensor", "pe", lambda e, p=p, c=c, bank=bank, j=j: e.transpose(
                        banks[bank][:, 128 * j:128 * (j + 1)],
                        wdq[:, p, 128 * c:128 * (c + 1)], ident_b[:]),
                        waits=w_)
                dvev = b.op("vector", "dv", lambda e, i=i, k=k, bank=bank: e.tensor_copy(
                    out=wdeqT[:, 4 * k:4 * (k + 1), 128 * i:128 * (i + 1)],
                    in_=banks[bank][:, 0:512].rearrange("p (j k) -> p j k", k=128)),
                    waits=[("pe", pe_wtr[i][k])])
                b.done_bank(bank, "dv", dvev)
                dv_wev_flat[i] = dvev
        W_DONE = dvev

        def emit_ffn(xs, wT, w_ready, x_ready, scr, NM, RS, ln_sc, ln_dv,
                     store=None, store_sem="gs", hook=None, ln_per_tile=None):
            xh = scr["xh"]; xq8 = scr["xq8"]; xdq = scr["xdq"]
            xdT = scr["xdT"]
            sc_sdv = [0] * TT
            sc_ssqv = [0] * TT
            sc_q8 = [0] * TT
            dv_xdq = [0] * TT
            dv_ev = [[0] * 4 for _ in range(TT)]
            dv_res = [[0] * 4 for _ in range(TT)]
            pe_tr = [[0] * 4 for _ in range(TT)]
            pe_mm = [[0] * 4 for _ in range(TT)]
            st_out = []
            for t in range(TT):
                p = t % 2
                if ln_per_tile is not None:
                    lsc, ldv = ln_per_tile[t]
                else:
                    lsc, ldv = ln_sc, ln_dv
                b.op("vector", "dv", lambda e, t=t, p=p: e.tensor_scalar(
                    out=xh[:, p, :], in0=xs[:, t, :], scalar1=NM[:, t:t + 1],
                    scalar2=RS[:, t:t + 1], op0=ALU.add, op1=ALU.mult),
                    waits=[("dv", ldv), ("sc", lsc)] + list(x_ready[t]))
                b.op("vector", "dv", lambda e, p=p: e.tensor_reduce(
                    out=amax[:, p, :],
                    in_=xh[:, p, :].rearrange("p (c k) -> p c k", k=128),
                    axis=AX.X, op=ALU.max, apply_absolute_value=True), risky=True)
                b.op("vector", "dv", lambda e, p=p: e.reciprocal(
                    out=r2[:, p, :], in_=amax[:, p, :]), risky=True)
                b.op("vector", "dv", lambda e, p=p: e.tensor_scalar_mul(
                    out=r2[:, p, :], in0=r2[:, p, :], scalar1=224.0), risky=True)
                dv_r2 = b.op("vector", "dv", lambda e, p=p: e.tensor_scalar_mul(
                    out=s2[:, p, :], in0=amax[:, p, :], scalar1=1.0 / 224.0), risky=True)
                for blk in range(DC):
                    sc_q8[t] = b.op("vector", "dv", lambda e, p=p, blk=blk: e.tensor_scalar(
                        out=xq8[:, 0, 128 * blk:128 * (blk + 1)],
                        in0=xh[:, p, 128 * blk:128 * (blk + 1)],
                        scalar1=r2[:, p, blk:blk + 1], scalar2=None, op0=ALU.mult))
                for blk in range(DC):
                    w_ = [("pe", pe_tr[t - 2][3])] if (t >= 2 and blk == 0) else []
                    dv_xdq[t] = b.op("vector", "dv", lambda e, p=p, blk=blk: e.tensor_scalar(
                        out=xdq[:, p, 128 * blk:128 * (blk + 1)],
                        in0=xq8[:, 0, 128 * blk:128 * (blk + 1)],
                        scalar1=s2[:, p, blk:blk + 1], scalar2=None, op0=ALU.mult),
                        waits=w_)
                for k in range(4):
                    bank = 6 + (k % 2)
                    for j in range(4):
                        c = 4 * k + j
                        w_ = [("dv", dv_xdq[t])] if (k == 0 and j == 0) else []
                        if j == 0:
                            w_ += b.use_bank(bank)
                        if b.paranoid:
                            w_.append(("gp", gm))
                        pe_tr[t][k] = b.op("tensor", "pe", lambda e, p=p, c=c, bank=bank, j=j: e.transpose(
                            banks[bank][:, 128 * j:128 * (j + 1)],
                            xdq[:, p, 128 * c:128 * (c + 1)], ident_b[:]), waits=w_)
                    w_ev = [("pe", pe_tr[t][k])]
                    if t >= 2 and k == 0:
                        w_ev.append(("pe", pe_mm[t - 2][3]))
                    dv_ev[t][k] = b.op("vector", "dv", lambda e, k=k, p=p, bank=bank: e.tensor_copy(
                        out=xdT[:, p, 4 * k:4 * (k + 1), :],
                        in_=banks[bank][:, 0:512].rearrange("p (j k) -> p j k", k=128)),
                        waits=w_ev)
                    b.done_bank(bank, "dv", dv_ev[t][k])
                def emit_resid(tt):
                    for oc in range(4):
                        dv_res[tt][oc] = b.op("vector", "dv",
                                              lambda e, tt=tt, oc=oc: e.scalar_tensor_tensor(
                                                  out=xs[:, tt, 512 * oc:512 * (oc + 1)],
                                                  in0=ps[oc][:, 0:512], scalar=0.5,
                                                  in1=xs[:, tt, 512 * oc:512 * (oc + 1)],
                                                  op0=ALU.mult, op1=ALU.add),
                                              waits=[("pe", pe_mm[tt][oc])])
                        b.done_bank(oc, "dv", dv_res[tt][oc])
                    if store is not None:
                        eng = "scalar" if store_sem == "vy" else "gpsimd"
                        st_out.append(b.dma(eng, store_sem,
                                            store[128 * tt:128 * (tt + 1), :],
                                            xs[:, tt, :], waits=[("dv", dv_res[tt][3])]))
                if t >= 1:
                    emit_resid(t - 1)
                mm_first = True
                for oc in range(4):
                    for c in range(DC):
                        w_ = []
                        if c == 0:
                            w_ = [("dv", dv_ev[t][3]), w_ready[oc]] + b.use_bank(oc)
                        pe_mm[t][oc] = b.op("tensor", "pe", lambda e, p=p, c=c, oc=oc: e.matmul(
                            ps[oc][:, 0:512], xdT[:, p, c, :],
                            wT[:, c, 512 * oc:512 * (oc + 1)],
                            start=(c == 0), stop=(c == DC - 1)), waits=w_)
                if hook is not None and t >= 1:
                    hook(t - 1, dv_res[t - 1][3])
            emit_resid(TT - 1)
            if hook is not None:
                hook(TT - 1, dv_res[TT - 1][3])
            return dv_res, st_out

        W_OC = [("dv", 0)] * 4
        for oc in range(4):
            W_OC[oc] = ("dv", dv_wev_flat[4 * oc + 3])
        SW_SPILL = 0
        for c in range(DC):
            SW_SPILL = b.dma("gpsimd", "ts", wdeq_spill[c, :, :], wdeqT[:, c, :],
                             waits=[("dv", W_DONE)])
        st2 = {"sc_sd": [0] * TT, "pe_ztr": [0] * TT, "gs_zb": [[0] * 4 for _ in range(TT)],
               "cc_ag": [0, 0], "dv_z": [0] * TT}
        xh_a = scrA["xh"]

        def ln2_hook(t, dv_resv):
            if t != 3 and t != TT - 1:
                return
            half = 0 if t == 3 else 1
            tiles = [4 * half + i for i in range(4)]
            xw = {tt: [("dv", dv_resv)] for tt in tiles}
            sc_sd, dv_rs = emit_ln_stats(x_str, tiles, xw,
                                         nm8b, ssq8b, tb8b, sd8b, rst8b)
            for tt in tiles:
                p = tt % 2
                dv_z = b.op("vector", "dv", lambda e, tt=tt, p=p: e.tensor_scalar(
                    out=zln[:, p, :], in0=x_str[:, tt, :],
                    scalar1=nm8b[:, tt:tt + 1], scalar2=rst8b[:, tt:tt + 1],
                    op0=ALU.add, op1=ALU.mult),
                    waits=([("dv", dv_rs), ("sc", sc_sd)] if tt == tiles[0] else [])
                    + ([("pe", st2["pe_ztr"][tt - 2])] if tt >= 2 else []))
                st2["dv_z"][tt] = dv_z
                for k in range(4):
                    bank = 6 + (k % 2)
                    pe_tr = 0
                    for j in range(4):
                        c = 4 * k + j
                        w_ = [("dv", dv_z)] if (k == 0 and j == 0) else []
                        if j == 0:
                            w_ += b.use_bank(bank)
                        pe_tr = b.op("tensor", "pe", lambda e, p=p, c=c, bank=bank, j=j: e.transpose(
                            banks[bank][:, 128 * j:128 * (j + 1)],
                            zln[:, p, 128 * c:128 * (c + 1)], ident_b[:]), waits=w_)
                    st2["pe_ztr"][tt] = pe_tr
                    zpar = (4 * tt + k) % 2
                    w_ev = [("pe", pe_tr)]
                    if tt >= 1 or k >= 2:
                        pk = (4 * tt + k) - 2
                        w_ev.append((f"gz{zpar}", st2["gs_zb"][pk // 4][pk % 4]))
                    dv_ze = b.op("vector", "dv", lambda e, zpar=zpar, bank=bank: e.tensor_copy(
                        out=zbuf[:, zpar, :, :],
                        in_=banks[bank][:, 0:512].rearrange("p (j k) -> p j k", k=128)),
                        waits=w_ev)
                    b.done_bank(bank, "dv", dv_ze)
                    halfq, tq = divmod(tt, 4)
                    for j in range(4):
                        c = 4 * k + j
                        st2["gs_zb"][tt][k] = b.dma(
                            "gpsimd", f"gz{zpar}",
                            zag_in[halfq][128 * c:128 * (c + 1), 128 * tq:128 * (tq + 1)],
                            zbuf[:, zpar, j, :], waits=([("dv", dv_ze)] if j == 0 else []))
            st2["cc_ag"][half] = b.op("gpsimd", "cc", lambda e, half=half: e.collective_compute(
                "AllGather", ALU.bypass, replica_groups=GROUPS,
                ins=[zag_in[half].ap().opt()], outs=[zag_out[half].ap().opt()]),
                waits=[("gz0", b.cnt["gz0"]), ("gz1", b.cnt["gz1"])], chain=True)

        ln1_sc, ln1_dv = emit_ln_stats(x_str, list(range(TT)),
                                       [[ld_x[t]] for t in range(TT)],
                                       nm8, ssq8, tb8, sd8, rst8)
        dv_res1, y_st = emit_ffn(x_str, wdeqT, W_OC,
                                 [[ld_x[t]] for t in range(TT)], scrA,
                                 nm8, rst8, ln1_sc, ln1_dv,
                                 store=y_spill, store_sem="vy", hook=ln2_hook)
        cc_ag = st2["cc_ag"]
        GS_SPILL = b.cnt["gs"]
        FENCE2 = [(s, b.cnt[s]) for s in ("ts", "pe", "dv", "sc")]
        FENCE2V = FENCE2 + [(s, b.cnt[s]) for s in ("gz0", "gz1")]
        b.flush("ph01_wq_ffn1")
        ctxA0.close()

        ctxA.close()
        ctxBm = ExitStack()
        m_t = ctxBm.enter_context(nc.sbuf_tensor("m_t", [128, NH, S], BF16, side="right"))
        ctxB = ExitStack()
        q_t = ctxB.enter_context(nc.sbuf_tensor("q_t", [128, NH, S], BF16))
        k_t = ctxB.enter_context(nc.sbuf_tensor("k_t", [128, NH, S], BF16))
        v_t = ctxB.enter_context(nc.sbuf_tensor("v_t", [128, 32, 512], BF16))
        ctxB3 = ExitStack()
        zt_q = ctxB3.enter_context(nc.sbuf_tensor("zt_q", [128, DC, 512], BF16))
        wqf = ctxB3.enter_context(nc.sbuf_tensor("wqf", [128, 2, 512], F32))
        wq_b = ctxB3.enter_context(nc.sbuf_tensor("wq_b", [128, DC, 512], BF16))
        wk_b = ctxB3.enter_context(nc.sbuf_tensor("wk_b", [128, DC, 512], BF16))
        wv_b = ctxB3.enter_context(nc.sbuf_tensor("wv_b", [128, DC, 512], BF16))

        dv_w3 = 0
        nw = 0
        dv_wc = {}
        for (wsrc, wbbuf) in ((wq_t, wq_b), (wk_t, wk_b), (wv_t, wv_b)):
            for c in range(DC):
                p = nw % 2
                fence = FENCE2V if wbbuf is wv_b else FENCE2
                eng, semb = (("scalar", "lw3") if wbbuf is wq_b else ("gpsimd", "lk"))
                ldw = b.dma(eng, f"{semb}{p}", wqf[:, p, :],
                            wsrc[128 * c:128 * (c + 1), :],
                            waits=([("dv", dv_w3)] if nw >= 2 else fence))
                dv_w3 = b.op("vector", "dv",
                             lambda e, p=p, c=c, wbbuf=wbbuf: e.tensor_copy(
                                 out=wbbuf[:, c, :], in_=wqf[:, p, :]),
                             waits=[(f"{semb}{p}", ldw)])
                dv_wc[(id(wbbuf), c)] = dv_w3
                nw += 1
        pe_zread = [0] * DC
        for hq in range(8):
            qq, half = divmod(hq, 2)
            ldz = 0
            for c in range(DC):
                w_ = [("pe", pe_zread[c])] if hq >= 1 else list(FENCE2)
                ldz = b.dma("sync", f"lz{hq % 2}", zt_q[:, c, :],
                            zag_out[half][D * qq + 128 * c:D * qq + 128 * (c + 1), :],
                            waits=w_ + [("cc", cc_ag[half])])
            zq_ready = (f"lz{hq % 2}", ldz)
            for (wb, dst) in ((wq_b, q_t), (wk_b, k_t)):
                pe_h = [0] * NH
                for c in range(DC):
                    for h in range(NH):
                        w_ = []
                        if h == 0 and hq == 0:
                            w_.append(("dv", dv_wc[(id(wb), c)]))
                        if c == 0:
                            w_ = w_ + [zq_ready] + b.use_bank(h)
                            if hq == 0 and wb is wq_b:
                                w_.append(("vy", y_st[-1]))
                        pe_h[h] = b.op("tensor", "pe",
                                       lambda e, c=c, h=h, wb=wb: e.matmul(
                                           ps[h][:, 0:512],
                                           wb[:, c, 128 * h:128 * (h + 1)],
                                           zt_q[:, c, :],
                                           start=(c == 0), stop=(c == DC - 1)),
                                       waits=w_)
                for h in range(NH):
                    dvx = b.op("vector", "dv",
                               lambda e, h=h, hq=hq, dst=dst: e.tensor_copy(
                                   out=dst[:, h, 512 * hq:512 * (hq + 1)],
                                   in_=ps[h][:, 0:512]), waits=[("pe", pe_h[h])])
                    b.done_bank(h, "dv", dvx)
            pe_g = [0] * 4
            for c in range(DC):
                for u in range(4):
                    w_ = []
                    if c == 0:
                        w_ = [zq_ready, ("dv", dv_w3)] + b.use_bank(u)
                    pe_g[u] = b.op("tensor", "pe",
                                   lambda e, c=c, u=u: e.matmul(
                                       ps[u][:, 0:512],
                                       zt_q[:, c, 128 * u:128 * (u + 1)],
                                       wv_b[:, c, 0:512],
                                       start=(c == 0), stop=(c == DC - 1)), waits=w_)
                    pe_zread[c] = pe_g[u]
            for u in range(4):
                dvx = b.op("vector", "dv", lambda e, hq=hq, u=u: e.tensor_copy(
                    out=v_t[:, 4 * hq + u, :], in_=ps[u][:, 0:512]),
                    waits=[("pe", pe_g[u])])
                b.done_bank(u, "dv", dvx)
        PH3_DV = b.cnt["dv"]
        b.flush("ph3_qkv")
        ctxB3.close()

        ctxB4 = ExitStack()
        gate_h = ctxB4.enter_context(nc.sbuf_tensor("gate_h", [128, 2, S], BF16))
        av_sb = ctxB4.enter_context(nc.sbuf_tensor("av_sb", [128, 1, S], BF16))
        p_t = ctxB4.enter_context(nc.sbuf_tensor("p_t", [128, 3, 1024], BF16))
        cs_rows = [ctxB4.enter_context(nc.sbuf_tensor(f"cs_row{i}", [1, S], BF16))
                   for i in range(2)]
        cs128 = ctxB4.enter_context(nc.sbuf_tensor("cs128", [128, 32], BF16))
        rc128 = ctxB4.enter_context(nc.sbuf_tensor("rc128", [128, 32], BF16))
        recip_b = ctxB4.enter_context(nc.sbuf_tensor("recip_b", [128, S], BF16))

        dv_mt = [0] * NH
        pt_pe = [0, 0, 0]
        tail_args = []

        def emit_tail(h, ph, dvc, scg):
            ld1 = b.dma("sync", "lcs", cs128[:, :], cs_rows[ph][0:1, :],
                        waits=[("dv", dvc)])
            def _recip(e):
                with nc.allow_low_precision(reason="bf16 norm factor, matches recip_b precision"):
                    return e.reciprocal(out=rc128[:, :], in_=cs128[:, :])
            dvr = b.op("vector", "dv", _recip,
                waits=[("lcs", ld1)], risky=True)
            ld2 = b.dma("sync", "lc2", cs_rows[ph][0:1, :], rc128[:, :],
                        waits=[("dv", dvr)])
            dv_rb = 0
            for qc in range(NQC):
                bank = qc % 4
                w_ = b.use_bank(bank) + [("lc2", ld2)]
                peb = b.op("tensor", "pe", lambda e, qc=qc, bank=bank, ph=ph: e.matmul(
                    ps[bank][:, 0:512], ones1b[:],
                    cs_rows[ph][0:1, 512 * qc:512 * (qc + 1)],
                    start=True, stop=True), waits=w_)
                dv_rb = b.op("vector", "dv", lambda e, qc=qc, bank=bank: e.tensor_copy(
                    out=recip_b[:, 512 * qc:512 * (qc + 1)], in_=ps[bank][:, 0:512]),
                    waits=[("pe", peb)])
                b.done_bank(bank, "dv", dv_rb)
            dvt = b.op("vector", "dv", lambda e, ph=ph: e.tensor_tensor(
                out=recip_b[:, :], in0=av_sb[:, 0, :], in1=recip_b[:, :], op=ALU.mult),
                waits=[("dv", dv_rb), ("sc", scg)])
            dv_mt[h] = b.op("vector", "dv", lambda e, h=h, ph=ph: e.tensor_tensor(
                out=m_t[:, h, :], in0=recip_b[:, :], in1=gate_h[:, ph, :], op=ALU.mult),
                waits=[("dv", dvt)])
        for h in range(NH):
            ph = h % 2
            scg = 0
            for qc in range(NQC):
                bank = qc % 4
                w_ = b.use_bank(bank)
                if qc == 0:
                    w_ += [("gp", gm), ("dv", GWT_B)]
                    if h == 0:
                        w_.append(("dv", PH3_DV))
                    if h >= 2:
                        w_.append(("dv", dv_mt[h - 2]))
                peg = b.op("tensor", "pe", lambda e, h=h, qc=qc, bank=bank: e.matmul(
                    ps[bank][:, 0:512], gwt_b[:], q_t[:, h, 512 * qc:512 * (qc + 1)],
                    start=True, stop=True), waits=w_)
                scg = b.op("scalar", "sc", lambda e, ph=ph, qc=qc, bank=bank: e.activation(
                    out=gate_h[:, ph, 512 * qc:512 * (qc + 1)], in_=ps[bank][:, 0:512],
                    func=AF.Sigmoid, bias=gb_sb[:, 0:1], scale=1.0),
                    waits=[("pe", peg)] + ([("ld", 32), ("gp", gm)]
                                           if (h == 0 and qc == 0) else []),
                    chain=True)
                b.done_bank(bank, "sc", scg)
            if tail_args:
                emit_tail(*tail_args.pop(0))
            dva = 0
            dvc = 0
            for qc in range(NQC):
                klim = 4 * qc + 4
                npair = klim // 2
                pend = []

                def flush_avcs(h=h):
                    pr, slot, pslot, sc_e = pend.pop(0)
                    ki0, ki1 = 2 * pr, 2 * pr + 1
                    klim_ = pend_klim[0]
                    pe_av = pe_cs = 0
                    for u, ki in ((0, ki0), (1, ki1)):
                        w_ = [("sc", sc_e)]
                        if ki == 0:
                            w_ += b.use_bank(4)
                        pe_av = b.op("tensor", "pe",
                                     lambda e, h=h, ki=ki, pslot=pslot, u=u, klim_=klim_: e.matmul(
                                         ps[4][:, 0:512],
                                         v_t[:, ki, 128 * h:128 * (h + 1)],
                                         p_t[:, pslot, 512 * u:512 * (u + 1)],
                                         start=(ki == 0), stop=(ki == klim_ - 1)),
                                     waits=w_)
                    for u, ki in ((0, ki0), (1, ki1)):
                        w_ = [("sc", sc_e)]
                        if ki == 0:
                            w_ += b.use_bank(5)
                        pe_cs = b.op("tensor", "pe",
                                     lambda e, ki=ki, pslot=pslot, u=u, klim_=klim_: e.matmul(
                                         ps[5][0:1, 0:512], ones_b[:],
                                         p_t[:, pslot, 512 * u:512 * (u + 1)],
                                         start=(ki == 0), stop=(ki == klim_ - 1)),
                                     waits=w_)
                    pt_pe[pslot] = pe_cs
                    return pe_av, pe_cs

                pend_klim = [klim]
                pe_av_last = pe_cs_last = 0
                for pr in range(npair):
                    ki0, ki1 = 2 * pr, 2 * pr + 1
                    off0 = max(0, 128 * ki0 - 512 * qc)
                    off1 = max(0, 128 * ki1 - 512 * qc)
                    slot = pr % 2
                    pslot = pr % 3
                    wS = b.use_bank(2 * slot) + b.use_bank(2 * slot + 1)
                    pes = b.op("tensor", "pe",
                               lambda e, h=h, qc=qc, ki0=ki0, off0=off0, slot=slot: e.matmul(
                                   P[slot][:, off0:512],
                                   k_t[:, h, 128 * ki0:128 * (ki0 + 1)],
                                   q_t[:, h, 512 * qc + off0:512 * (qc + 1)],
                                   start=True, stop=True), waits=wS)
                    pes = b.op("tensor", "pe",
                               lambda e, h=h, qc=qc, ki1=ki1, off1=off1, slot=slot: e.matmul(
                                   P[slot][:, 512 + off1:1024],
                                   k_t[:, h, 128 * ki1:128 * (ki1 + 1)],
                                   q_t[:, h, 512 * qc + off1:512 * (qc + 1)],
                                   start=True, stop=True), waits=wS)
                    dv_ms = 0
                    if ki1 >= 4 * qc:
                        wm = [("pe", pes)]
                        if h == 0 and qc == 0 and pr == 0:
                            wm.append(("gp", gm))
                        if off0 or ki0 >= 4 * qc:
                            dv_ms = b.op("vector", "dv",
                                         lambda e, slot=slot, off0=off0: e.tensor_tensor(
                                             out=P[slot][:, off0:off0 + 128],
                                             in0=P[slot][:, off0:off0 + 128],
                                             in1=mask_add[:], op=ALU.add), waits=wm)
                            wm = []
                        dv_ms = b.op("vector", "dv",
                                     lambda e, slot=slot, off1=off1: e.tensor_tensor(
                                         out=P[slot][:, 512 + off1:512 + off1 + 128],
                                         in0=P[slot][:, 512 + off1:512 + off1 + 128],
                                         in1=mask_add[:], op=ALU.add), waits=wm)
                        if off0:
                            dv_ms = b.op("vector", "dv", lambda e, slot=slot, off0=off0: e.memset(
                                P[slot][:, 0:off0], -1.0e30))
                        if off1:
                            dv_ms = b.op("vector", "dv", lambda e, slot=slot, off1=off1: e.memset(
                                P[slot][:, 512:512 + off1], -1.0e30))
                    w_e = [("pe", pes), ("pe", pt_pe[pslot]), ("dv", dv_ms)]
                    sc_e = b.op("scalar", "sc",
                                lambda e, slot=slot, pslot=pslot: e.activation(
                                    out=p_t[:, pslot, :], in_=P[slot][:, 0:1024],
                                    func=AF.Exp, bias=negC[:, 0:1], scale=SCALE),
                                chain=True, waits=w_e)
                    b.done_bank(2 * slot, "sc", sc_e)
                    b.done_bank(2 * slot + 1, "sc", sc_e)
                    pend.append((pr, slot, pslot, sc_e))
                    if len(pend) > 2:
                        pe_av_last, pe_cs_last = flush_avcs()
                while pend:
                    pe_av_last, pe_cs_last = flush_avcs()
                dva = b.op("vector", "dv", lambda e, ph=ph, qc=qc: e.tensor_copy(
                    out=av_sb[:, 0, 512 * qc:512 * (qc + 1)], in_=ps[4][:, 0:512]),
                    waits=[("pe", pe_av_last)])
                b.done_bank(4, "dv", dva)
                dvc = b.op("vector", "dv", lambda e, qc=qc, ph=ph: e.tensor_copy(
                    out=cs_rows[ph][0:1, 512 * qc:512 * (qc + 1)],
                    in_=ps[5][0:1, 0:512]),
                    waits=[("pe", pe_cs_last)])
                b.done_bank(5, "dv", dvc)
            tail_args.append((h, ph, dvc, scg))
        while tail_args:
            emit_tail(*tail_args.pop(0))
        b.flush("ph4_attn")
        ctxB4.close()
        ctxB.close()

        ctxC = ExitStack()
        x_str2 = ctxC.enter_context(nc.sbuf_tensor("x_str2", [128, TT, D], F32))
        wdeqT2 = ctxC.enter_context(nc.sbuf_tensor("wdeqT2", [128, DC, D], BF16))
        ctxC0 = ExitStack()
        wof = ctxC0.enter_context(nc.sbuf_tensor("wof", [128, 1, D], F32))
        wo_b = ctxC0.enter_context(nc.sbuf_tensor("wo_b", [128, 4, D], BF16))
        rs_buf = ctxC0.enter_context(nc.sbuf_tensor("rs_buf", [128, 2, 2048], BF16))

        dv_wo = 0
        for c in range(4):
            ldw = b.dma("sync", "lwo", wof[:, 0, :], wo_t[128 * c:128 * (c + 1), :],
                        waits=([("dv", dv_wo)] if c >= 1 else [("dv", dv_mt[NH - 1])]))
            dv_wo = b.op("vector", "dv", lambda e, c=c: e.tensor_copy(
                out=wo_b[:, c, :], in_=wof[:, 0, :]), waits=[("lwo", ldw)])
        cc_rs = [0] * 8
        gr_last = [0, 0]
        nqt = 0
        for j in range(TT):
            for g in range(4):
                qt = 8 * g + j
                slot = nqt % 2
                dvo = 0
                for half in range(2):
                    peo = 0
                    for e2p in (2 * half, 2 * half + 1):
                        view = e2p
                        for c in range(4):
                            w_ = []
                            if c == 0:
                                w_ = b.use_bank(view)
                                if nqt == 0 and e2p == 0:
                                    w_ += [("dv", dv_wo), ("dv", dv_mt[NH - 1]),
                                           ("dv", dv_mt[NH - 2])]
                            peo = b.op("tensor", "pe",
                                       lambda e, e2p=e2p, qt=qt, c=c, view=view: e.matmul(
                                           ps[view][:, 0:512],
                                           m_t[:, c, 128 * qt:128 * (qt + 1)],
                                           wo_b[:, c, 512 * e2p:512 * (e2p + 1)],
                                           start=(c == 0), stop=(c == 3)), waits=w_)
                    Pt = P[half]
                    dvo = b.op("vector", "dv", lambda e, Pt=Pt, slot=slot, half=half: e.tensor_copy(
                        out=rs_buf[:, slot, 1024 * half:1024 * (half + 1)],
                        in_=Pt[:, 0:1024]),
                        waits=[("pe", peo), (f"gr{slot}", gr_last[slot])])
                    b.done_bank(2 * half, "dv", dvo)
                    b.done_bank(2 * half + 1, "dv", dvo)
                gr_last[slot] = b.dma("gpsimd" if slot == 0 else "scalar",
                                      f"gr{slot}",
                                      rs_in[j][128 * g:128 * (g + 1), :],
                                      rs_buf[:, slot, :], waits=[("dv", dvo)])
                nqt += 1
            cc_rs[j] = b.op("gpsimd", "cc", lambda e, j=j: e.collective_compute(
                "ReduceScatter", ALU.add, replica_groups=GROUPS,
                ins=[rs_in[j].ap().opt()], outs=[rs_out[j].ap().opt()]),
                waits=[("gr0", gr_last[0]), ("gr1", gr_last[1])], chain=True)
        ld_wd = 0
        for c in range(DC):
            ld_wd = b.dma("scalar", "lwd", wdeqT2[:, c, :], wdeq_spill[c, :, :],
                          waits=[("ts", SW_SPILL), ("dv", dv_mt[NH - 1])])
        if taps:
            gt = b.dma("gpsimd", "gs", dbg_y[:, :], y_spill[:, :],
                       waits=[("gs", GS_SPILL)])
            gt = b.dma("gpsimd", "gs", dbg_wd[:, :, :], wdeq_spill[:, :, :])
            gt = b.dma("gpsimd", "gs", dbg_zo[:, 0:512], zag_out[0][:, :],
                       waits=[("cc", cc_ag[1])])
            gt = b.dma("gpsimd", "gs", dbg_zo[:, 512:1024], zag_out[1][:, :])
            gt = b.dma("gpsimd", "gs", dbg_rs[:, :], rs_out[0][:, :],
                       waits=[("cc", cc_rs[7])])
            b.wait("gpsimd", [("gs", gt)])
        b.flush("ph5_oproj_rs")
        ctxC0.close()
        ctxBm.close()
        y_buf = ctxC.enter_context(nc.sbuf_tensor("y_buf", [128, 2, D], F32))
        rs_sb = ctxC.enter_context(nc.sbuf_tensor("rs_sb", [128, 2, D], BF16))
        xh2 = ctxC.enter_context(nc.sbuf_tensor("xh2", [128, 2, D], F32))
        xq82 = ctxC.enter_context(nc.sbuf_tensor("xq82", [128, 1, D], FP8))
        xdq2 = ctxC.enter_context(nc.sbuf_tensor("xdq2", [128, 2, D], BF16))
        xdT2 = ctxC.enter_context(nc.sbuf_tensor("xdT2", [128, 2, DC, 128], BF16))
        scrC = dict(xh=xh2, xq8=xq82, xdq=xdq2, xdT=xdT2, sq=sq_scrap)

        dv_x2 = [0] * TT
        for t in range(TT):
            p = t % 2
            w_ = [("dv", dv_x2[t - 2])] if t >= 2 else []
            ldy = b.dma("sync", f"ly{p}", y_buf[:, p, :],
                        y_spill[128 * t:128 * (t + 1), :],
                        waits=w_ + [("vy", y_st[-1]), ("dv", dv_mt[NH - 1])])
            ldr = b.dma("sync", f"lr{p}", rs_sb[:, p, :],
                        rs_out[t][:, :], waits=w_ + [("cc", cc_rs[t])])
            dv_x2[t] = b.op("vector", "dv", lambda e, t=t, p=p: e.tensor_tensor(
                out=x_str2[:, t, :], in0=y_buf[:, p, :],
                in1=rs_sb[:, p, :], op=ALU.add),
                waits=[(f"ly{p}", ldy), (f"lr{p}", ldr)])
        ln3 = [emit_ln_stats(x_str2, [t], {t: [("dv", dv_x2[t])]},
                             nm8, ssq8, tb8, sd8, rst8) for t in range(TT)]
        _, st_out = emit_ffn(x_str2, wdeqT2, [("lwd", ld_wd)] * 4,
                             [[("dv", dv_x2[t])] for t in range(TT)], scrC,
                             nm8, rst8, 0, 0, store=out_sh, ln_per_tile=ln3)
        b.wait("gpsimd", [("gs", st_out[-1])])
        b.flush("ph7_ffn2")
        ctxC.close()

    return nc


_cache = {}


def kernel(L0_attn_x, L0_attn_q_w, L0_attn_k_w, L0_attn_v_w, L0_attn_o_w,
           L0_attn_gate_w, L0_attn_gate_b, L0_ffn_weight):
    x = np.ascontiguousarray(np.asarray(L0_attn_x, dtype=np.float32).reshape(8192, 2048))
    qw = np.asarray(L0_attn_q_w, dtype=np.float32)
    kw = np.asarray(L0_attn_k_w, dtype=np.float32)
    vw = np.asarray(L0_attn_v_w, dtype=np.float32)
    ow = np.asarray(L0_attn_o_w, dtype=np.float32)
    gw = np.asarray(L0_attn_gate_w, dtype=np.float32)
    gbv = np.ascontiguousarray(np.asarray(L0_attn_gate_b, dtype=np.float32).reshape(128, 1))
    fw = np.ascontiguousarray(np.asarray(L0_ffn_weight, dtype=np.float32))
    owt = np.ascontiguousarray(ow.T)
    gwt_host = np.ascontiguousarray(gw.T)

    if "nc" not in _cache:
        _cache["nc"] = build()
    nc = _cache["nc"]

    in_maps = []
    for c in range(NCORES):
        bb, r = divmod(c, 4)
        sl = slice(512 * r, 512 * (r + 1))
        in_maps.append({
            "x_sh": np.ascontiguousarray(x[4096 * bb + 1024 * r: 4096 * bb + 1024 * (r + 1)]),
            "wq_t": np.ascontiguousarray(qw[sl, :].T),
            "wk_t": np.ascontiguousarray(kw[sl, :].T),
            "wv_t": np.ascontiguousarray(vw[sl, :].T),
            "wo_t": np.ascontiguousarray(owt[sl, :]),
            "gwt": gwt_host,
            "gb": gbv,
            "wf": np.ascontiguousarray(fw[256 * c:256 * (c + 1)]),
        })
    res = run_bass_kernel_spmd(nc, in_maps, core_ids=list(range(NCORES)))
    out = np.concatenate([np.asarray(res.results[c]["out_sh"]) for c in range(NCORES)],
                         axis=0)
    return out.reshape(2, 4096, 2048).astype(np.float32)


# version 19
# speedup vs baseline: 1.0358x; 1.0163x over previous
import numpy as np
from contextlib import ExitStack

import concourse.bass as bass
import concourse.mybir as mybir
from concourse.bass_utils import run_bass_kernel_spmd

F32 = mybir.dt.float32
BF16 = mybir.dt.bfloat16
FP8 = mybir.dt.float8e4
AF = mybir.ActivationFunctionType
ALU = mybir.AluOpType
AX = mybir.AxisListType

NCORES = 8
TT = 8
D = 2048
DC = 16
S = 4096
NH = 4
QCW = 512
NQC = S // QCW
C_SHIFT = 20.0
SCALE = 0.08838834764831845
INV_D = 1.0 / 2048.0
EPS = 1e-5
GROUPS = [[0, 1, 2, 3], [4, 5, 6, 7]]


class Builder:
    def __init__(self, nc, paranoid=False):
        self.nc = nc
        self.paranoid = paranoid
        self.q = {k: [] for k in ("sync", "gpsimd", "vector", "scalar", "tensor")}
        self.sem = {}
        self.cnt = {}
        self.bank_state = [None] * 8
        self.risky = {}

    def add_sem(self, ctx, name):
        self.sem[name] = ctx.enter_context(self.nc.semaphore(name))
        self.cnt[name] = 0

    def flush(self, blockname):
        with self.nc.Block(blockname) as block:
            for eng, fns in self.q.items():
                if fns:
                    getattr(block, eng)(lambda e, fns=tuple(fns): [f(e) for f in fns])
        self.q = {k: [] for k in self.q}

    def op(self, eng, semname, fn, waits=(), inc=1, chain=False, risky=False):
        w = [(s, v) for (s, v) in waits if v > 0]
        if chain or (self.paranoid and eng == "vector"):
            prev = self.cnt[semname]
            if prev > 0:
                w.append((semname, prev))
        pend = self.risky.pop(eng, None)
        if pend is not None:
            w.append(pend)
        self.cnt[semname] += inc
        if risky:
            self.risky[eng] = (semname, self.cnt[semname])
        sems = self.sem

        def run(e, w=tuple(w), fn=fn, semname=semname, inc=inc):
            seen = {}
            for (s, v) in w:
                if seen.get(s, -1) < v:
                    e.wait_ge(sems[s], v)
                    seen[s] = v
            fn(e).then_inc(sems[semname], inc)
        self.q[eng].append(run)
        return self.cnt[semname]

    def wait(self, eng, waits):
        w = [(s, v) for (s, v) in waits if v > 0]
        sems = self.sem

        def run(e):
            for (s, v) in w:
                e.wait_ge(sems[s], v)
        self.q[eng].append(run)

    def dma(self, eng, semname, out, in_, waits=()):
        return self.op(eng, semname, lambda e: e.dma_start(out=out, in_=in_),
                       waits=waits, inc=16)

    def use_bank(self, k):
        st = self.bank_state[k]
        return [st] if st is not None else []

    def done_bank(self, k, semname, val):
        self.bank_state[k] = (semname, val)


def build(paranoid=False, taps=False):
    nc = bass.Bass("TRN2", target_bir_lowering=False)

    x_sh = nc.dram_tensor("x_sh", [1024, D], F32, kind="ExternalInput")
    wq_t = nc.dram_tensor("wq_t", [D, 512], F32, kind="ExternalInput")
    wk_t = nc.dram_tensor("wk_t", [D, 512], F32, kind="ExternalInput")
    wv_t = nc.dram_tensor("wv_t", [D, 512], F32, kind="ExternalInput")
    wo_t = nc.dram_tensor("wo_t", [512, D], F32, kind="ExternalInput")
    gwt = nc.dram_tensor("gwt", [128, 128], F32, kind="ExternalInput")
    gb = nc.dram_tensor("gb", [128, 1], F32, kind="ExternalInput")
    wf = nc.dram_tensor("wf", [256, D], F32, kind="ExternalInput")
    out_sh = nc.dram_tensor("out_sh", [1024, D], F32, kind="ExternalOutput")

    zag_in = [nc.dram_tensor(f"zag_in{i}", [D, QCW], BF16) for i in range(2)]
    zag_out = [nc.dram_tensor(f"zag_out{i}", [4 * D, QCW], BF16) for i in range(2)]
    rs_in = [nc.dram_tensor(f"rs_in{p}", [4 * 128, D], BF16) for p in range(8)]
    rs_out = [nc.dram_tensor(f"rs_out{p}", [128, D], BF16) for p in range(8)]
    y_spill = nc.dram_tensor("y_spill", [1024, D], F32)
    wag_in = nc.dram_tensor("wag_in", [256, D], BF16)
    wag_out = nc.dram_tensor("wag_out", [D, D], BF16)
    wdeq_spill = nc.dram_tensor("wdeq_spill", [DC, 128, D], BF16)
    wq_sp = nc.dram_tensor("wq_sp", [D, 512], BF16)
    wk_sp = nc.dram_tensor("wk_sp", [D, 512], BF16)
    wv_sp = nc.dram_tensor("wv_sp", [D, 512], BF16)
    wo_sp = nc.dram_tensor("wo_sp", [512, D], BF16)
    if taps:
        dbg_am = nc.dram_tensor("dbg_am", [DC, 128, DC], F32, kind="ExternalOutput")
        dbg_r2 = nc.dram_tensor("dbg_r2", [DC, 128, DC], F32, kind="ExternalOutput")
        dbg_q8 = nc.dram_tensor("dbg_q8", [DC, 128, D], FP8, kind="ExternalOutput")
        dbg_dq = nc.dram_tensor("dbg_dq", [DC, 128, D], BF16, kind="ExternalOutput")
        dbg_y = nc.dram_tensor("dbg_y", [1024, D], F32, kind="ExternalOutput")
        dbg_wd = nc.dram_tensor("dbg_wd", [DC, 128, D], BF16, kind="ExternalOutput")
        dbg_zo = nc.dram_tensor("dbg_zo", [4 * D, 1024], BF16, kind="ExternalOutput")
        dbg_q = nc.dram_tensor("dbg_q", [128, NH * S], BF16, kind="ExternalOutput")
        dbg_k = nc.dram_tensor("dbg_k", [128, NH * S], BF16, kind="ExternalOutput")
        dbg_v = nc.dram_tensor("dbg_v", [128, 32 * 512], BF16, kind="ExternalOutput")
        dbg_m = nc.dram_tensor("dbg_m", [128, NH * S], BF16, kind="ExternalOutput")
        dbg_g = nc.dram_tensor("dbg_g", [128, 2 * S], BF16, kind="ExternalOutput")
        dbg_rs = nc.dram_tensor("dbg_rs", [128, D], BF16, kind="ExternalOutput")

    b = Builder(nc, paranoid=paranoid)
    with ExitStack() as ctx:
        for s in ("ld", "gs", "cc", "pe", "dv", "sc", "gp", "lx",
                  "lwf0", "lwf1", "lz0", "lz1", "lw30", "lw31", "lcs",
                  "lwo", "lwd", "ly0", "ly1", "lr0", "lr1", "gr0", "gr1",
                  "sy", "sw", "vy", "ts", "gz0", "gz1", "gw", "lwn0", "lwn1",
                  "lk0", "lk1", "lc2",
                  "lsp0", "lsp1", "lsp2", "lsp3", "lsp4", "lsp5", "lsp6",
                  "lsp7", "ssp0", "ssp1", "lw3", "lk3", "lv3", "gr2", "gr3",
                  "lwd1", "lwd2", "lwd3",
                  "lx0", "lx1"):
            b.add_sem(ctx, s)

        P = [ctx.enter_context(nc.psum_tensor(f"P{k}", [128, 1024], F32))
             for k in range(3)]
        pt = [ctx.enter_context(nc.psum_tensor(f"pt{k}", [128, 1024], BF16))
              for k in range(2)]

        class PV:
            def __init__(self, t, off):
                self.t = t
                self.off = off

            def __getitem__(self, idx):
                rows, cols = idx
                start = self.off + (cols.start or 0)
                stop = self.off + (cols.stop if cols.stop is not None else QCW)
                return self.t[rows, start:stop]

        ps = [PV(P[k // 2], QCW * (k % 2)) for k in range(6)]
        banks = ps + pt

        ident_f = ctx.enter_context(nc.sbuf_tensor("ident_f", [128, 128], F32))
        ident_b = ctx.enter_context(nc.sbuf_tensor("ident_b", [128, 128], BF16))
        mask_add = ctx.enter_context(nc.sbuf_tensor("mask_add", [128, 128], F32))
        ones_b = ctx.enter_context(nc.sbuf_tensor("ones_b", [128, 1], BF16))
        ones1f = ctx.enter_context(nc.sbuf_tensor("ones1f", [1, 128], F32))
        ones1b = ctx.enter_context(nc.sbuf_tensor("ones1b", [1, 128], BF16))
        negC = ctx.enter_context(nc.sbuf_tensor("negC", [128, 1], F32))
        epsb = ctx.enter_context(nc.sbuf_tensor("epsb", [128, 1], F32))
        gb_sb = ctx.enter_context(nc.sbuf_tensor("gb_sb", [128, 1], F32))
        gwt_f = ctx.enter_context(nc.sbuf_tensor("gwt_f", [128, 128], F32))
        gwt_b = ctx.enter_context(nc.sbuf_tensor("gwt_b", [128, 128], BF16))
        rsum = ctx.enter_context(nc.sbuf_tensor("rsum", [128, 2], F32))
        nmean = ctx.enter_context(nc.sbuf_tensor("nmean", [128, 2], F32))
        ssq = ctx.enter_context(nc.sbuf_tensor("ssq", [128, 2], F32))
        sdvv = ctx.enter_context(nc.sbuf_tensor("sdvv", [128, 2], F32))
        rstd = ctx.enter_context(nc.sbuf_tensor("rstd", [128, 2], F32))
        tbx = ctx.enter_context(nc.sbuf_tensor("tbx", [128, 2], F32))
        rsum8 = ctx.enter_context(nc.sbuf_tensor("rsum8", [128, 8], F32))
        nm8 = ctx.enter_context(nc.sbuf_tensor("nm8", [128, 8], F32))
        ssq8 = ctx.enter_context(nc.sbuf_tensor("ssq8", [128, 8], F32))
        tb8 = ctx.enter_context(nc.sbuf_tensor("tb8", [128, 8], F32))
        sd8 = ctx.enter_context(nc.sbuf_tensor("sd8", [128, 8], F32))
        rst8 = ctx.enter_context(nc.sbuf_tensor("rst8", [128, 8], F32))
        nm8b = ctx.enter_context(nc.sbuf_tensor("nm8b", [128, 8], F32))
        ssq8b = ctx.enter_context(nc.sbuf_tensor("ssq8b", [128, 8], F32))
        tb8b = ctx.enter_context(nc.sbuf_tensor("tb8b", [128, 8], F32))
        sd8b = ctx.enter_context(nc.sbuf_tensor("sd8b", [128, 8], F32))
        rst8b = ctx.enter_context(nc.sbuf_tensor("rst8b", [128, 8], F32))

        def emit_ln_stats(xs, tiles, x_waits, NM, SQ, TB, SD, RS):
            for t in tiles:
                b.op("vector", "dv", lambda e, t=t: e.tensor_reduce(
                    out=rsum8[:, t:t + 1], in_=xs[:, t, :], axis=AX.X,
                    op=ALU.add, negate=True), waits=x_waits[t], risky=True)
            sc_q = 0
            for t in tiles:
                sc_q = b.op("scalar", "sc", lambda e, t=t: e.activation(
                    out=sq_scrap[:, 0, :], in_=xs[:, t, :], func=AF.Square,
                    scale=1.0, accum_out=SQ[:, t:t + 1]),
                    waits=list(x_waits[t]), chain=True)
            lo, n = tiles[0], len(tiles)
            b.op("vector", "dv", lambda e, lo=lo, n=n: e.tensor_scalar_mul(
                out=NM[:, lo:lo + n], in0=rsum8[:, lo:lo + n], scalar1=INV_D),
                risky=True)
            b.op("vector", "dv", lambda e, lo=lo, n=n: e.tensor_tensor(
                out=TB[:, lo:lo + n], in0=NM[:, lo:lo + n],
                in1=NM[:, lo:lo + n], op=ALU.mult), risky=True)
            dv_tb = b.op("vector", "dv", lambda e, lo=lo, n=n: e.tensor_scalar(
                out=TB[:, lo:lo + n], in0=TB[:, lo:lo + n], scalar1=-1.0,
                scalar2=EPS, op0=ALU.mult, op1=ALU.add), risky=True)
            sc_sd = 0
            for t in tiles:
                sc_sd = b.op("scalar", "sc", lambda e, t=t: e.activation(
                    out=SD[:, t:t + 1], in_=SQ[:, t:t + 1], func=AF.Sqrt,
                    bias=TB[:, t:t + 1], scale=INV_D),
                    waits=([("dv", dv_tb)] if t == tiles[0] else []), chain=True)
            dv_rs = b.op("vector", "dv", lambda e, lo=lo, n=n: e.reciprocal(
                out=RS[:, lo:lo + n], in_=SD[:, lo:lo + n]),
                waits=[("sc", sc_sd)], risky=True)
            return sc_sd, dv_rs
        nmean2 = ctx.enter_context(nc.sbuf_tensor("nmean2", [128, 2], F32))
        ssq2 = ctx.enter_context(nc.sbuf_tensor("ssq2", [128, 2], F32))
        sdv2 = ctx.enter_context(nc.sbuf_tensor("sdv2", [128, 2], F32))
        rstd2 = ctx.enter_context(nc.sbuf_tensor("rstd2", [128, 2], F32))
        tbx2 = ctx.enter_context(nc.sbuf_tensor("tbx2", [128, 2], F32))
        amax = ctx.enter_context(nc.sbuf_tensor("amax", [128, 2, DC], F32))
        r2 = ctx.enter_context(nc.sbuf_tensor("r2", [128, 2, DC], F32))
        s2 = ctx.enter_context(nc.sbuf_tensor("s2", [128, 2, DC], F32))
        sq_scrap = ctx.enter_context(nc.sbuf_tensor("sq_scrap", [128, 1, D], BF16))

        for (tname, val) in ((ident_f, 0.0), (ident_b, 0.0), (mask_add, 0.0),
                             (ones_b, 1.0), (ones1f, 1.0), (ones1b, 1.0),
                             (negC, -C_SHIFT), (epsb, EPS)):
            b.op("gpsimd", "gp", lambda e, t=tname, v=val: e.memset(t[:], v), chain=True)
        b.op("gpsimd", "gp", lambda e: e.affine_select(
            out=ident_f[:], in_=ident_f[:], compare_op=ALU.not_equal,
            fill=1.0, base=0, pattern=[[-1, 128]], channel_multiplier=1), chain=True)
        b.op("gpsimd", "gp", lambda e: e.affine_select(
            out=ident_b[:], in_=ident_b[:], compare_op=ALU.not_equal,
            fill=1.0, base=0, pattern=[[-1, 128]], channel_multiplier=1), chain=True)
        gm = b.op("gpsimd", "gp", lambda e: e.affine_select(
            out=mask_add[:], in_=mask_add[:], compare_op=ALU.is_ge,
            fill=-1.0e30, base=0, pattern=[[1, 128]], channel_multiplier=-1), chain=True)
        b.dma("sync", "ld", gb_sb[:], gb[:, :])
        lgw = b.dma("sync", "ld", gwt_f[:], gwt[:, :])
        GWT_B = b.op("vector", "dv", lambda e: e.tensor_copy(out=gwt_b[:], in_=gwt_f[:]),
                     waits=[("ld", lgw)])

        ctxA = ExitStack()
        x_str = ctxA.enter_context(nc.sbuf_tensor("x_str", [128, TT, D], F32))
        wdeqT = ctxA.enter_context(nc.sbuf_tensor("wdeqT", [128, DC, D], BF16))
        xh = ctxA.enter_context(nc.sbuf_tensor("xh", [128, 2, D], F32))
        xq8 = ctxA.enter_context(nc.sbuf_tensor("xq8", [128, 1, D], FP8))
        xdq = ctxA.enter_context(nc.sbuf_tensor("xdq", [128, 2, D], BF16))
        xdT = ctxA.enter_context(nc.sbuf_tensor("xdT", [128, 2, DC, 128], BF16))
        scrA = dict(xh=xh, xq8=xq8, xdq=xdq, xdT=xdT, sq=sq_scrap)
        zbuf = ctxA.enter_context(nc.sbuf_tensor("zbuf", [128, 2, 4, 128], BF16))
        zln = ctxA.enter_context(nc.sbuf_tensor("zln", [128, 2, D], BF16))
        ctxA0 = ExitStack()
        wfn = ctxA0.enter_context(nc.sbuf_tensor("wfn", [128, 2, D], F32))
        wq8 = ctxA0.enter_context(nc.sbuf_tensor("wq8", [128, 1, D], FP8))
        wdq = ctxA0.enter_context(nc.sbuf_tensor("wdq", [128, 2, D], BF16))

        ld_x = []
        for t in range(TT):
            sem = "lx0" if t == 0 else ("lx1" if t == 1 else "lx")
            ld_x.append((sem, b.dma("sync", sem, x_str[:, t, :],
                                    x_sh[128 * t:128 * (t + 1), :])))
        ld_x = [(s, v if s != "lx" else b.cnt["lx"]) for (s, v) in ld_x]

        GP_INIT = gm
        pe_wtr = [[0] * 4 for _ in range(DC)]
        dv_wev_flat = [0] * DC
        gw_sp = 0
        for i in range(2):
            p = i % 2
            ldwf = b.dma("sync", f"lwf{p}", wfn[:, p, :], wf[128 * i:128 * (i + 1), :])
            b.op("vector", "dv", lambda e, p=p: e.tensor_reduce(
                out=amax[:, p, :],
                in_=wfn[:, p, :].rearrange("p (c k) -> p c k", k=128),
                axis=AX.X, op=ALU.max, apply_absolute_value=True),
                waits=[(f"lwf{p}", ldwf)], risky=True)
            b.op("vector", "dv", lambda e, p=p: e.reciprocal(
                out=r2[:, p, :], in_=amax[:, p, :]), risky=True)
            b.op("vector", "dv", lambda e, p=p: e.tensor_scalar_mul(
                out=r2[:, p, :], in0=r2[:, p, :], scalar1=224.0), risky=True)
            dv_r2w = b.op("vector", "dv", lambda e, p=p: e.tensor_scalar_mul(
                out=s2[:, p, :], in0=amax[:, p, :], scalar1=1.0 / 224.0), risky=True)
            for blk in range(DC):
                b.op("vector", "dv", lambda e, p=p, blk=blk: e.tensor_scalar(
                    out=wq8[:, 0, 128 * blk:128 * (blk + 1)],
                    in0=wfn[:, p, 128 * blk:128 * (blk + 1)],
                    scalar1=r2[:, p, blk:blk + 1], scalar2=None, op0=ALU.mult))
            dv_wdq = 0
            for blk in range(DC):
                dv_wdq = b.op("vector", "dv", lambda e, p=p, blk=blk: e.tensor_scalar(
                    out=wdq[:, p, 128 * blk:128 * (blk + 1)],
                    in0=wq8[:, 0, 128 * blk:128 * (blk + 1)],
                    scalar1=s2[:, p, blk:blk + 1], scalar2=None, op0=ALU.mult))
            gw_sp = b.dma("gpsimd", "gw", wag_in[128 * i:128 * (i + 1), :],
                          wdq[:, p, :], waits=[("dv", dv_wdq)])
        cc_w = b.op("gpsimd", "cc", lambda e: e.collective_compute(
            "AllGather", ALU.bypass, replica_groups=[list(range(8))],
            ins=[wag_in.ap().opt()], outs=[wag_out.ap().opt()]),
            waits=[("gw", gw_sp)], chain=True)
        for i in range(DC):
            p = i % 2
            ldn = b.dma("sync", f"lwn{p}", wdq[:, p, :],
                        wag_out[128 * i:128 * (i + 1), :],
                        waits=[("cc", cc_w)] + ([("pe", pe_wtr[i - 2][3])]
                                                if i >= 2 else [("gw", gw_sp)]))
            for k in range(4):
                bank = 6 + (k % 2)
                for j in range(4):
                    c = 4 * k + j
                    w_ = [(f"lwn{p}", ldn)] if (k == 0 and j == 0) else []
                    if j == 0:
                        w_ += b.use_bank(bank)
                    if (i == 0 and k == 0 and j == 0) or b.paranoid:
                        w_.append(("gp", GP_INIT))
                    pe_wtr[i][k] = b.op("tensor", "pe", lambda e, p=p, c=c, bank=bank, j=j: e.transpose(
                        banks[bank][:, 128 * j:128 * (j + 1)],
                        wdq[:, p, 128 * c:128 * (c + 1)], ident_b[:]),
                        waits=w_)
                dvev = b.op("vector", "dv", lambda e, i=i, k=k, bank=bank: e.tensor_copy(
                    out=wdeqT[:, 4 * k:4 * (k + 1), 128 * i:128 * (i + 1)],
                    in_=banks[bank][:, 0:512].rearrange("p (j k) -> p j k", k=128)),
                    waits=[("pe", pe_wtr[i][k])])
                b.done_bank(bank, "dv", dvev)
                dv_wev_flat[i] = dvev
        W_DONE = dvev

        def emit_ffn(xs, wT, w_ready, x_ready, scr, NM, RS, ln_sc, ln_dv,
                     store=None, store_sem="gs", hook=None, ln_per_tile=None):
            xh = scr["xh"]; xq8 = scr["xq8"]; xdq = scr["xdq"]
            xdT = scr["xdT"]
            sc_sdv = [0] * TT
            sc_ssqv = [0] * TT
            sc_q8 = [0] * TT
            dv_xdq = [0] * TT
            dv_ev = [[0] * 4 for _ in range(TT)]
            dv_res = [[0] * 4 for _ in range(TT)]
            pe_tr = [[0] * 4 for _ in range(TT)]
            pe_mm = [[0] * 4 for _ in range(TT)]
            st_out = []
            for t in range(TT):
                p = t % 2
                if ln_per_tile is not None:
                    lsc, ldv = ln_per_tile[t]
                else:
                    lsc, ldv = ln_sc, ln_dv
                b.op("vector", "dv", lambda e, t=t, p=p: e.tensor_scalar(
                    out=xh[:, p, :], in0=xs[:, t, :], scalar1=NM[:, t:t + 1],
                    scalar2=RS[:, t:t + 1], op0=ALU.add, op1=ALU.mult),
                    waits=[("dv", ldv), ("sc", lsc)] + list(x_ready[t]))
                b.op("vector", "dv", lambda e, p=p: e.tensor_reduce(
                    out=amax[:, p, :],
                    in_=xh[:, p, :].rearrange("p (c k) -> p c k", k=128),
                    axis=AX.X, op=ALU.max, apply_absolute_value=True), risky=True)
                b.op("vector", "dv", lambda e, p=p: e.reciprocal(
                    out=r2[:, p, :], in_=amax[:, p, :]), risky=True)
                b.op("vector", "dv", lambda e, p=p: e.tensor_scalar_mul(
                    out=r2[:, p, :], in0=r2[:, p, :], scalar1=224.0), risky=True)
                dv_r2 = b.op("vector", "dv", lambda e, p=p: e.tensor_scalar_mul(
                    out=s2[:, p, :], in0=amax[:, p, :], scalar1=1.0 / 224.0), risky=True)
                for blk in range(DC):
                    sc_q8[t] = b.op("vector", "dv", lambda e, p=p, blk=blk: e.tensor_scalar(
                        out=xq8[:, 0, 128 * blk:128 * (blk + 1)],
                        in0=xh[:, p, 128 * blk:128 * (blk + 1)],
                        scalar1=r2[:, p, blk:blk + 1], scalar2=None, op0=ALU.mult))
                for blk in range(DC):
                    w_ = [("pe", pe_tr[t - 2][3])] if (t >= 2 and blk == 0) else []
                    dv_xdq[t] = b.op("vector", "dv", lambda e, p=p, blk=blk: e.tensor_scalar(
                        out=xdq[:, p, 128 * blk:128 * (blk + 1)],
                        in0=xq8[:, 0, 128 * blk:128 * (blk + 1)],
                        scalar1=s2[:, p, blk:blk + 1], scalar2=None, op0=ALU.mult),
                        waits=w_)
                for k in range(4):
                    bank = 6 + (k % 2)
                    for j in range(4):
                        c = 4 * k + j
                        w_ = [("dv", dv_xdq[t])] if (k == 0 and j == 0) else []
                        if j == 0:
                            w_ += b.use_bank(bank)
                        if b.paranoid:
                            w_.append(("gp", gm))
                        pe_tr[t][k] = b.op("tensor", "pe", lambda e, p=p, c=c, bank=bank, j=j: e.transpose(
                            banks[bank][:, 128 * j:128 * (j + 1)],
                            xdq[:, p, 128 * c:128 * (c + 1)], ident_b[:]), waits=w_)
                    w_ev = [("pe", pe_tr[t][k])]
                    if t >= 2 and k == 0:
                        w_ev.append(("pe", pe_mm[t - 2][3]))
                    dv_ev[t][k] = b.op("vector", "dv", lambda e, k=k, p=p, bank=bank: e.tensor_copy(
                        out=xdT[:, p, 4 * k:4 * (k + 1), :],
                        in_=banks[bank][:, 0:512].rearrange("p (j k) -> p j k", k=128)),
                        waits=w_ev)
                    b.done_bank(bank, "dv", dv_ev[t][k])
                def emit_resid(tt):
                    for oc in range(4):
                        dv_res[tt][oc] = b.op("vector", "dv",
                                              lambda e, tt=tt, oc=oc: e.scalar_tensor_tensor(
                                                  out=xs[:, tt, 512 * oc:512 * (oc + 1)],
                                                  in0=ps[oc][:, 0:512], scalar=0.5,
                                                  in1=xs[:, tt, 512 * oc:512 * (oc + 1)],
                                                  op0=ALU.mult, op1=ALU.add),
                                              waits=[("pe", pe_mm[tt][oc])])
                        b.done_bank(oc, "dv", dv_res[tt][oc])
                    if store is not None:
                        eng = "scalar" if store_sem == "vy" else "gpsimd"
                        st_out.append(b.dma(eng, store_sem,
                                            store[128 * tt:128 * (tt + 1), :],
                                            xs[:, tt, :], waits=[("dv", dv_res[tt][3])]))
                if t >= 1:
                    emit_resid(t - 1)
                mm_first = True
                for oc in range(4):
                    for c in range(DC):
                        w_ = []
                        if c == 0:
                            w_ = [("dv", dv_ev[t][3]), w_ready[oc]] + b.use_bank(oc)
                        pe_mm[t][oc] = b.op("tensor", "pe", lambda e, p=p, c=c, oc=oc: e.matmul(
                            ps[oc][:, 0:512], xdT[:, p, c, :],
                            wT[:, c, 512 * oc:512 * (oc + 1)],
                            start=(c == 0), stop=(c == DC - 1)), waits=w_)
                if hook is not None and t >= 1:
                    hook(t - 1, dv_res[t - 1][3])
            emit_resid(TT - 1)
            if hook is not None:
                hook(TT - 1, dv_res[TT - 1][3])
            return dv_res, st_out

        W_OC = [("dv", 0)] * 4
        for oc in range(4):
            W_OC[oc] = ("dv", dv_wev_flat[4 * oc + 3])
        SW_SPILL = 0
        for c in range(DC):
            SW_SPILL = b.dma("gpsimd", "ts", wdeq_spill[c, :, :], wdeqT[:, c, :],
                             waits=[("dv", W_DONE)])
        st2 = {"sc_sd": [0] * TT, "pe_ztr": [0] * TT, "gs_zb": [[0] * 4 for _ in range(TT)],
               "cc_ag": [0, 0], "dv_z": [0] * TT}
        xh_a = scrA["xh"]

        def ln2_hook(t, dv_resv):
            if t != 3 and t != TT - 1:
                return
            half = 0 if t == 3 else 1
            tiles = [4 * half + i for i in range(4)]
            xw = {tt: [("dv", dv_resv)] for tt in tiles}
            sc_sd, dv_rs = emit_ln_stats(x_str, tiles, xw,
                                         nm8b, ssq8b, tb8b, sd8b, rst8b)
            for tt in tiles:
                p = tt % 2
                dv_z = b.op("vector", "dv", lambda e, tt=tt, p=p: e.tensor_scalar(
                    out=zln[:, p, :], in0=x_str[:, tt, :],
                    scalar1=nm8b[:, tt:tt + 1], scalar2=rst8b[:, tt:tt + 1],
                    op0=ALU.add, op1=ALU.mult),
                    waits=([("dv", dv_rs), ("sc", sc_sd)] if tt == tiles[0] else [])
                    + ([("pe", st2["pe_ztr"][tt - 2])] if tt >= 2 else []))
                st2["dv_z"][tt] = dv_z
                for k in range(4):
                    bank = 6 + (k % 2)
                    pe_tr = 0
                    for j in range(4):
                        c = 4 * k + j
                        w_ = [("dv", dv_z)] if (k == 0 and j == 0) else []
                        if j == 0:
                            w_ += b.use_bank(bank)
                        pe_tr = b.op("tensor", "pe", lambda e, p=p, c=c, bank=bank, j=j: e.transpose(
                            banks[bank][:, 128 * j:128 * (j + 1)],
                            zln[:, p, 128 * c:128 * (c + 1)], ident_b[:]), waits=w_)
                    st2["pe_ztr"][tt] = pe_tr
                    zpar = (4 * tt + k) % 2
                    w_ev = [("pe", pe_tr)]
                    if tt >= 1 or k >= 2:
                        pk = (4 * tt + k) - 2
                        w_ev.append((f"gz{zpar}", st2["gs_zb"][pk // 4][pk % 4]))
                    dv_ze = b.op("vector", "dv", lambda e, zpar=zpar, bank=bank: e.tensor_copy(
                        out=zbuf[:, zpar, :, :],
                        in_=banks[bank][:, 0:512].rearrange("p (j k) -> p j k", k=128)),
                        waits=w_ev)
                    b.done_bank(bank, "dv", dv_ze)
                    halfq, tq = divmod(tt, 4)
                    for j in range(4):
                        c = 4 * k + j
                        st2["gs_zb"][tt][k] = b.dma(
                            "gpsimd", f"gz{zpar}",
                            zag_in[halfq][128 * c:128 * (c + 1), 128 * tq:128 * (tq + 1)],
                            zbuf[:, zpar, j, :], waits=([("dv", dv_ze)] if j == 0 else []))
            st2["cc_ag"][half] = b.op("gpsimd", "cc", lambda e, half=half: e.collective_compute(
                "AllGather", ALU.bypass, replica_groups=GROUPS,
                ins=[zag_in[half].ap().opt()], outs=[zag_out[half].ap().opt()]),
                waits=[("gz0", b.cnt["gz0"]), ("gz1", b.cnt["gz1"])], chain=True)

        spill_jobs = []
        for (wsrc, wdst) in ((wq_t, wq_sp), (wk_t, wk_sp), (wv_t, wv_sp)):
            for c in range(DC):
                spill_jobs.append((wsrc[128 * c:128 * (c + 1), :],
                                   wdst[128 * c:128 * (c + 1), :]))
        for c in range(4):
            for u in range(4):
                spill_jobs.append(
                    (wo_t[128 * c:128 * (c + 1), 512 * u:512 * (u + 1)],
                     wo_sp[128 * c:128 * (c + 1), 512 * u:512 * (u + 1)]))
        SP = {"k": 0, "lms": {}, "cms": {}, "sms": {}}
        wspf = [wfn[:, s // 4, 512 * (s % 4):512 * (s % 4 + 1)] for s in range(8)]
        wspb = [wdq[:, 0, 512 * p:512 * (p + 1)] for p in range(2)]

        def spill_casts(k):
            for i in range(8 * k, min(8 * k + 8, len(spill_jobs))):
                _, dst = spill_jobs[i]
                s = i % 8
                wc = [(f"lsp{s}", SP["lms"][i])]
                if i >= 2:
                    wc.append((f"ssp{i % 2}", SP["sms"][i - 2]))
                if i == 0:
                    wc.append(("gw", gw_sp))
                cv = b.op("vector", "dv", lambda e, s=s, i=i: e.tensor_copy(
                    out=wspb[i % 2], in_=wspf[s]), waits=wc)
                SP["cms"][i] = cv
                SP["sms"][i] = b.dma("sync", f"ssp{i % 2}", dst,
                                     wspb[i % 2], waits=[("dv", cv)])

        def spill_loads(k):
            for i in range(8 * k, min(8 * k + 8, len(spill_jobs))):
                src, _ = spill_jobs[i]
                s = i % 8
                wl = [("dv", SP["cms"].get(i - 8, 0))]
                if i == 0:
                    wl.append(("dv", W_DONE))
                SP["lms"][i] = b.dma("scalar", f"lsp{s}", wspf[s], src,
                                     waits=wl)

        def spill_step():
            k = SP["k"]
            SP["k"] = k + 1
            spill_casts(k)
            spill_loads(k + 1)

        def hook_all(t, dvr):
            ln2_hook(t, dvr)
            spill_step()

        spill_loads(0)
        ln1_sc, ln1_dv = emit_ln_stats(x_str, list(range(TT)),
                                       [[ld_x[t]] for t in range(TT)],
                                       nm8, ssq8, tb8, sd8, rst8)
        dv_res1, y_st = emit_ffn(x_str, wdeqT, W_OC,
                                 [[ld_x[t]] for t in range(TT)], scrA,
                                 nm8, rst8, ln1_sc, ln1_dv,
                                 store=y_spill, store_sem="vy", hook=hook_all)
        SPW = [("ssp0", SP["sms"][len(spill_jobs) - 2]),
               ("ssp1", SP["sms"][len(spill_jobs) - 1])]
        cc_ag = st2["cc_ag"]
        GS_SPILL = b.cnt["gs"]
        FENCE2 = [(s, b.cnt[s]) for s in ("ts", "pe", "dv", "sc")]
        FENCE2V = FENCE2 + [(s, b.cnt[s]) for s in ("gz0", "gz1")]
        ctxA0.close()

        ctxA.close()
        ctxBm = ExitStack()
        m_t = ctxBm.enter_context(nc.sbuf_tensor("m_t", [128, NH, S], BF16, side="right"))
        ctxB = ExitStack()
        q_t = ctxB.enter_context(nc.sbuf_tensor("q_t", [128, NH, S], BF16))
        k_t = ctxB.enter_context(nc.sbuf_tensor("k_t", [128, NH, S], BF16))
        v_t = ctxB.enter_context(nc.sbuf_tensor("v_t", [128, 32, 512], BF16))
        ctxB3 = ExitStack()
        zt_q = ctxB3.enter_context(nc.sbuf_tensor("zt_q", [128, DC, 512], BF16))
        wq_b = ctxB3.enter_context(nc.sbuf_tensor("wq_b", [128, DC, 512], BF16))
        wk_b = ctxB3.enter_context(nc.sbuf_tensor("wk_b", [128, DC, 512], BF16))
        wv_b = ctxB3.enter_context(nc.sbuf_tensor("wv_b", [128, DC, 512], BF16))

        for (wsrc, wbbuf, semw) in ((wq_sp, wq_b, "lw3"), (wk_sp, wk_b, "lk3"),
                                    (wv_sp, wv_b, "lv3")):
            for c in range(DC):
                w_ = []
                if wbbuf is wq_b and c == 0:
                    w_ = list(FENCE2V) + SPW + [("vy", y_st[-1])]
                b.dma("scalar", semw, wbbuf[:, c, :],
                      wsrc[128 * c:128 * (c + 1), :], waits=w_)
        pe_zread = [0] * DC
        for hq in range(8):
            qq, half = divmod(hq, 2)
            ldz = 0
            for c in range(DC):
                w_ = [("pe", pe_zread[c])] if hq >= 1 else (
                    list(FENCE2) + SPW + [("vy", y_st[-1])])
                ldz = b.dma("sync", f"lz{hq % 2}", zt_q[:, c, :],
                            zag_out[half][D * qq + 128 * c:D * qq + 128 * (c + 1), :],
                            waits=w_ + [("cc", cc_ag[half])])
            zq_ready = (f"lz{hq % 2}", ldz)
            for (wb, dst) in ((wq_b, q_t), (wk_b, k_t)):
                pe_h = [0] * NH
                for c in range(DC):
                    for h in range(NH):
                        w_ = []
                        if h == 0 and hq == 0 and c == 0:
                            w_.append(("lw3" if wb is wq_b else "lk3", 16 * DC))
                        if c == 0:
                            w_ = w_ + [zq_ready] + b.use_bank(h)
                        pe_h[h] = b.op("tensor", "pe",
                                       lambda e, c=c, h=h, wb=wb: e.matmul(
                                           ps[h][:, 0:512],
                                           wb[:, c, 128 * h:128 * (h + 1)],
                                           zt_q[:, c, :],
                                           start=(c == 0), stop=(c == DC - 1)),
                                       waits=w_)
                for h in range(NH):
                    dvx = b.op("vector", "dv",
                               lambda e, h=h, hq=hq, dst=dst: e.tensor_copy(
                                   out=dst[:, h, 512 * hq:512 * (hq + 1)],
                                   in_=ps[h][:, 0:512]), waits=[("pe", pe_h[h])])
                    b.done_bank(h, "dv", dvx)
            pe_g = [0] * 4
            for c in range(DC):
                for u in range(4):
                    w_ = []
                    if c == 0:
                        w_ = [zq_ready, ("lv3", 16 * DC)] + b.use_bank(u)
                    pe_g[u] = b.op("tensor", "pe",
                                   lambda e, c=c, u=u: e.matmul(
                                       ps[u][:, 0:512],
                                       zt_q[:, c, 128 * u:128 * (u + 1)],
                                       wv_b[:, c, 0:512],
                                       start=(c == 0), stop=(c == DC - 1)), waits=w_)
                    pe_zread[c] = pe_g[u]
            for u in range(4):
                dvx = b.op("vector", "dv", lambda e, hq=hq, u=u: e.tensor_copy(
                    out=v_t[:, 4 * hq + u, :], in_=ps[u][:, 0:512]),
                    waits=[("pe", pe_g[u])])
                b.done_bank(u, "dv", dvx)
        PH3_DV = b.cnt["dv"]
        ctxB3.close()

        ctxB4 = ExitStack()
        gate_h = ctxB4.enter_context(nc.sbuf_tensor("gate_h", [128, 2, S], BF16))
        av_sb = ctxB4.enter_context(nc.sbuf_tensor("av_sb", [128, 1, S], BF16))
        p_t = ctxB4.enter_context(nc.sbuf_tensor("p_t", [128, 3, 1024], BF16))
        cs_rows = [ctxB4.enter_context(nc.sbuf_tensor(f"cs_row{i}", [1, S], BF16))
                   for i in range(2)]
        cs128 = ctxB4.enter_context(nc.sbuf_tensor("cs128", [128, 32], BF16))
        rc128 = ctxB4.enter_context(nc.sbuf_tensor("rc128", [128, 32], BF16))
        recip_b = ctxB4.enter_context(nc.sbuf_tensor("recip_b", [128, S], BF16))

        dv_mt = [0] * NH
        pt_pe = [0, 0, 0]
        tail_args = []

        def emit_tail(h, ph, dvc, scg):
            ld1 = b.dma("sync", "lcs", cs128[:, :], cs_rows[ph][0:1, :],
                        waits=[("dv", dvc)])
            def _recip(e):
                with nc.allow_low_precision(reason="bf16 norm factor, matches recip_b precision"):
                    return e.reciprocal(out=rc128[:, :], in_=cs128[:, :])
            dvr = b.op("vector", "dv", _recip,
                waits=[("lcs", ld1)], risky=True)
            ld2 = b.dma("sync", "lc2", cs_rows[ph][0:1, :], rc128[:, :],
                        waits=[("dv", dvr)])
            dv_rb = 0
            for qc in range(NQC):
                bank = qc % 4
                w_ = b.use_bank(bank) + [("lc2", ld2)]
                peb = b.op("tensor", "pe", lambda e, qc=qc, bank=bank, ph=ph: e.matmul(
                    ps[bank][:, 0:512], ones1b[:],
                    cs_rows[ph][0:1, 512 * qc:512 * (qc + 1)],
                    start=True, stop=True), waits=w_)
                dv_rb = b.op("vector", "dv", lambda e, qc=qc, bank=bank: e.tensor_copy(
                    out=recip_b[:, 512 * qc:512 * (qc + 1)], in_=ps[bank][:, 0:512]),
                    waits=[("pe", peb)])
                b.done_bank(bank, "dv", dv_rb)
            dvt = b.op("vector", "dv", lambda e, ph=ph: e.tensor_tensor(
                out=recip_b[:, :], in0=av_sb[:, 0, :], in1=recip_b[:, :], op=ALU.mult),
                waits=[("dv", dv_rb), ("sc", scg)])
            dv_mt[h] = b.op("vector", "dv", lambda e, h=h, ph=ph: e.tensor_tensor(
                out=m_t[:, h, :], in0=recip_b[:, :], in1=gate_h[:, ph, :], op=ALU.mult),
                waits=[("dv", dvt)])
        for h in range(NH):
            ph = h % 2
            scg = 0
            for qc in range(NQC):
                bank = qc % 4
                w_ = b.use_bank(bank)
                if qc == 0:
                    w_ += [("gp", gm), ("dv", GWT_B)]
                    if h == 0:
                        w_.append(("dv", PH3_DV))
                    if h >= 2:
                        w_.append(("dv", dv_mt[h - 2]))
                peg = b.op("tensor", "pe", lambda e, h=h, qc=qc, bank=bank: e.matmul(
                    ps[bank][:, 0:512], gwt_b[:], q_t[:, h, 512 * qc:512 * (qc + 1)],
                    start=True, stop=True), waits=w_)
                scg = b.op("scalar", "sc", lambda e, ph=ph, qc=qc, bank=bank: e.activation(
                    out=gate_h[:, ph, 512 * qc:512 * (qc + 1)], in_=ps[bank][:, 0:512],
                    func=AF.Sigmoid, bias=gb_sb[:, 0:1], scale=1.0),
                    waits=[("pe", peg)] + ([("ld", 32), ("gp", gm)]
                                           if (h == 0 and qc == 0) else []),
                    chain=True)
                b.done_bank(bank, "sc", scg)
            if tail_args:
                emit_tail(*tail_args.pop(0))
            dva = 0
            dvc = 0
            for qc in range(NQC):
                klim = 4 * qc + 4
                npair = klim // 2
                pend = []

                def flush_avcs(h=h):
                    pr, slot, pslot, sc_e = pend.pop(0)
                    ki0, ki1 = 2 * pr, 2 * pr + 1
                    klim_ = pend_klim[0]
                    pe_av = pe_cs = 0
                    for u, ki in ((0, ki0), (1, ki1)):
                        w_ = [("sc", sc_e)]
                        if ki == 0:
                            w_ += b.use_bank(4)
                        pe_av = b.op("tensor", "pe",
                                     lambda e, h=h, ki=ki, pslot=pslot, u=u, klim_=klim_: e.matmul(
                                         ps[4][:, 0:512],
                                         v_t[:, ki, 128 * h:128 * (h + 1)],
                                         p_t[:, pslot, 512 * u:512 * (u + 1)],
                                         start=(ki == 0), stop=(ki == klim_ - 1)),
                                     waits=w_)
                    for u, ki in ((0, ki0), (1, ki1)):
                        w_ = [("sc", sc_e)]
                        if ki == 0:
                            w_ += b.use_bank(5)
                        pe_cs = b.op("tensor", "pe",
                                     lambda e, ki=ki, pslot=pslot, u=u, klim_=klim_: e.matmul(
                                         ps[5][0:1, 0:512], ones_b[:],
                                         p_t[:, pslot, 512 * u:512 * (u + 1)],
                                         start=(ki == 0), stop=(ki == klim_ - 1)),
                                     waits=w_)
                    pt_pe[pslot] = pe_cs
                    return pe_av, pe_cs

                pend_klim = [klim]
                pe_av_last = pe_cs_last = 0
                for pr in range(npair):
                    ki0, ki1 = 2 * pr, 2 * pr + 1
                    off0 = max(0, 128 * ki0 - 512 * qc)
                    off1 = max(0, 128 * ki1 - 512 * qc)
                    slot = pr % 2
                    pslot = pr % 3
                    wS = b.use_bank(2 * slot) + b.use_bank(2 * slot + 1)
                    pes = b.op("tensor", "pe",
                               lambda e, h=h, qc=qc, ki0=ki0, off0=off0, slot=slot: e.matmul(
                                   P[slot][:, off0:512],
                                   k_t[:, h, 128 * ki0:128 * (ki0 + 1)],
                                   q_t[:, h, 512 * qc + off0:512 * (qc + 1)],
                                   start=True, stop=True), waits=wS)
                    pes = b.op("tensor", "pe",
                               lambda e, h=h, qc=qc, ki1=ki1, off1=off1, slot=slot: e.matmul(
                                   P[slot][:, 512 + off1:1024],
                                   k_t[:, h, 128 * ki1:128 * (ki1 + 1)],
                                   q_t[:, h, 512 * qc + off1:512 * (qc + 1)],
                                   start=True, stop=True), waits=wS)
                    dv_ms = 0
                    if ki1 >= 4 * qc:
                        wm = [("pe", pes)]
                        if h == 0 and qc == 0 and pr == 0:
                            wm.append(("gp", gm))
                        if off0 or ki0 >= 4 * qc:
                            dv_ms = b.op("vector", "dv",
                                         lambda e, slot=slot, off0=off0: e.tensor_tensor(
                                             out=P[slot][:, off0:off0 + 128],
                                             in0=P[slot][:, off0:off0 + 128],
                                             in1=mask_add[:], op=ALU.add), waits=wm)
                            wm = []
                        dv_ms = b.op("vector", "dv",
                                     lambda e, slot=slot, off1=off1: e.tensor_tensor(
                                         out=P[slot][:, 512 + off1:512 + off1 + 128],
                                         in0=P[slot][:, 512 + off1:512 + off1 + 128],
                                         in1=mask_add[:], op=ALU.add), waits=wm)
                        if off0:
                            dv_ms = b.op("vector", "dv", lambda e, slot=slot, off0=off0: e.memset(
                                P[slot][:, 0:off0], -1.0e30))
                        if off1:
                            dv_ms = b.op("vector", "dv", lambda e, slot=slot, off1=off1: e.memset(
                                P[slot][:, 512:512 + off1], -1.0e30))
                    w_e = [("pe", pes), ("pe", pt_pe[pslot]), ("dv", dv_ms)]
                    sc_e = b.op("scalar", "sc",
                                lambda e, slot=slot, pslot=pslot: e.activation(
                                    out=p_t[:, pslot, :], in_=P[slot][:, 0:1024],
                                    func=AF.Exp, bias=negC[:, 0:1], scale=SCALE),
                                chain=True, waits=w_e)
                    b.done_bank(2 * slot, "sc", sc_e)
                    b.done_bank(2 * slot + 1, "sc", sc_e)
                    pend.append((pr, slot, pslot, sc_e))
                    if len(pend) > 2:
                        pe_av_last, pe_cs_last = flush_avcs()
                while pend:
                    pe_av_last, pe_cs_last = flush_avcs()
                dva = b.op("vector", "dv", lambda e, ph=ph, qc=qc: e.tensor_copy(
                    out=av_sb[:, 0, 512 * qc:512 * (qc + 1)], in_=ps[4][:, 0:512]),
                    waits=[("pe", pe_av_last)])
                b.done_bank(4, "dv", dva)
                dvc = b.op("vector", "dv", lambda e, qc=qc, ph=ph: e.tensor_copy(
                    out=cs_rows[ph][0:1, 512 * qc:512 * (qc + 1)],
                    in_=ps[5][0:1, 0:512]),
                    waits=[("pe", pe_cs_last)])
                b.done_bank(5, "dv", dvc)
            tail_args.append((h, ph, dvc, scg))
        while tail_args:
            emit_tail(*tail_args.pop(0))
        ctxB4.close()
        ctxB.close()

        ctxC = ExitStack()
        x_str2 = ctxC.enter_context(nc.sbuf_tensor("x_str2", [128, TT, D], F32))
        wdeqT2 = ctxC.enter_context(nc.sbuf_tensor("wdeqT2", [128, DC, D], BF16))
        ctxC0 = ExitStack()
        wo_b = ctxC0.enter_context(nc.sbuf_tensor("wo_b", [128, 4, D], BF16))
        rs_buf = ctxC0.enter_context(nc.sbuf_tensor("rs_buf", [128, 4, 2048], BF16))

        for c in range(4):
            b.dma("sync", "lwo", wo_b[:, c, :], wo_sp[128 * c:128 * (c + 1), :],
                  waits=(([("dv", dv_mt[NH - 1])] + SPW) if c == 0 else []))
        cc_rs = [0] * 8
        gr_last = [0, 0, 0, 0]
        nqt = 0
        for j in range(TT):
            for g in range(4):
                qt = 8 * g + j
                slot = nqt % 4
                dvo = 0
                for half in range(2):
                    peo = 0
                    for e2p in (2 * half, 2 * half + 1):
                        view = e2p
                        for c in range(4):
                            w_ = []
                            if c == 0:
                                w_ = b.use_bank(view)
                                if nqt == 0 and e2p == 0:
                                    w_ += [("lwo", 64), ("dv", dv_mt[NH - 1]),
                                           ("dv", dv_mt[NH - 2])]
                            peo = b.op("tensor", "pe",
                                       lambda e, e2p=e2p, qt=qt, c=c, view=view: e.matmul(
                                           ps[view][:, 0:512],
                                           m_t[:, c, 128 * qt:128 * (qt + 1)],
                                           wo_b[:, c, 512 * e2p:512 * (e2p + 1)],
                                           start=(c == 0), stop=(c == 3)), waits=w_)
                    Pt = P[half]
                    dvo = b.op("vector", "dv", lambda e, Pt=Pt, slot=slot, half=half: e.tensor_copy(
                        out=rs_buf[:, slot, 1024 * half:1024 * (half + 1)],
                        in_=Pt[:, 0:1024]),
                        waits=[("pe", peo), (f"gr{slot}", gr_last[slot])])
                    b.done_bank(2 * half, "dv", dvo)
                    b.done_bank(2 * half + 1, "dv", dvo)
                gr_last[slot] = b.dma("gpsimd" if slot % 2 == 0 else "scalar",
                                      f"gr{slot}",
                                      rs_in[j][128 * g:128 * (g + 1), :],
                                      rs_buf[:, slot, :], waits=[("dv", dvo)])
                nqt += 1
            cc_rs[j] = b.op("gpsimd", "cc", lambda e, j=j: e.collective_compute(
                "ReduceScatter", ALU.add, replica_groups=GROUPS,
                ins=[rs_in[j].ap().opt()], outs=[rs_out[j].ap().opt()]),
                waits=[(f"gr{s}", gr_last[s]) for s in range(4)], chain=True)
        PE_OPROJ_END = b.cnt["pe"]
        W2_OC = []
        for oc in range(4):
            sem = "lwd" if oc == 0 else f"lwd{oc}"
            for c in range(DC):
                ld_wd = b.dma("sync", sem, wdeqT2[:, c, 512 * oc:512 * (oc + 1)],
                              wdeq_spill[c, :, 512 * oc:512 * (oc + 1)],
                              waits=([("ts", SW_SPILL), ("dv", dv_mt[NH - 1])]
                                     if (oc == 0 and c == 0) else []))
            W2_OC.append((sem, ld_wd))
        if taps:
            gt = b.dma("gpsimd", "gs", dbg_y[:, :], y_spill[:, :],
                       waits=[("gs", GS_SPILL)])
            gt = b.dma("gpsimd", "gs", dbg_wd[:, :, :], wdeq_spill[:, :, :])
            gt = b.dma("gpsimd", "gs", dbg_zo[:, 0:512], zag_out[0][:, :],
                       waits=[("cc", cc_ag[1])])
            gt = b.dma("gpsimd", "gs", dbg_zo[:, 512:1024], zag_out[1][:, :])
            gt = b.dma("gpsimd", "gs", dbg_rs[:, :], rs_out[0][:, :],
                       waits=[("cc", cc_rs[7])])
            b.wait("gpsimd", [("gs", gt)])
        ctxC0.close()
        ctxBm.close()
        y_buf = ctxC.enter_context(nc.sbuf_tensor("y_buf", [128, 2, D], F32))
        rs_sb = ctxC.enter_context(nc.sbuf_tensor("rs_sb", [128, 2, D], BF16))
        xh2 = ctxC.enter_context(nc.sbuf_tensor("xh2", [128, 2, D], F32))
        xq82 = ctxC.enter_context(nc.sbuf_tensor("xq82", [128, 1, D], FP8))
        xdq2 = ctxC.enter_context(nc.sbuf_tensor("xdq2", [128, 2, D], BF16))
        xdT2 = ctxC.enter_context(nc.sbuf_tensor("xdT2", [128, 2, DC, 128], BF16))
        scrC = dict(xh=xh2, xq8=xq82, xdq=xdq2, xdT=xdT2, sq=sq_scrap)

        dv_x2 = [0] * TT
        for t in range(TT):
            p = t % 2
            w_ = [("dv", dv_x2[t - 2])] if t >= 2 else []
            hz = [] if t else [("pe", PE_OPROJ_END)] + [
                (f"gr{s}", gr_last[s]) for s in range(4)]
            ldy = b.dma("sync", f"ly{p}", y_buf[:, p, :],
                        y_spill[128 * t:128 * (t + 1), :],
                        waits=w_ + hz + [("vy", y_st[-1]), ("dv", dv_mt[NH - 1])])
            ldr = b.dma("sync", f"lr{p}", rs_sb[:, p, :],
                        rs_out[t][:, :], waits=w_ + [("cc", cc_rs[t])])
            dv_x2[t] = b.op("vector", "dv", lambda e, t=t, p=p: e.tensor_tensor(
                out=x_str2[:, t, :], in0=y_buf[:, p, :],
                in1=rs_sb[:, p, :], op=ALU.add),
                waits=[(f"ly{p}", ldy), (f"lr{p}", ldr)])
        ln3 = [emit_ln_stats(x_str2, [t], {t: [("dv", dv_x2[t])]},
                             nm8, ssq8, tb8, sd8, rst8) for t in range(TT)]
        _, st_out = emit_ffn(x_str2, wdeqT2, W2_OC,
                             [[("dv", dv_x2[t])] for t in range(TT)], scrC,
                             nm8, rst8, 0, 0, store=out_sh, store_sem="vy",
                             ln_per_tile=ln3)
        b.wait("scalar", [("vy", st_out[-1])])
        b.flush("all")
        ctxC.close()

    return nc


_cache = {}


def kernel(L0_attn_x, L0_attn_q_w, L0_attn_k_w, L0_attn_v_w, L0_attn_o_w,
           L0_attn_gate_w, L0_attn_gate_b, L0_ffn_weight):
    x = np.ascontiguousarray(np.asarray(L0_attn_x, dtype=np.float32).reshape(8192, 2048))
    qw = np.asarray(L0_attn_q_w, dtype=np.float32)
    kw = np.asarray(L0_attn_k_w, dtype=np.float32)
    vw = np.asarray(L0_attn_v_w, dtype=np.float32)
    ow = np.asarray(L0_attn_o_w, dtype=np.float32)
    gw = np.asarray(L0_attn_gate_w, dtype=np.float32)
    gbv = np.ascontiguousarray(np.asarray(L0_attn_gate_b, dtype=np.float32).reshape(128, 1))
    fw = np.ascontiguousarray(np.asarray(L0_ffn_weight, dtype=np.float32))
    owt = np.ascontiguousarray(ow.T)
    gwt_host = np.ascontiguousarray(gw.T)

    if "nc" not in _cache:
        _cache["nc"] = build()
    nc = _cache["nc"]

    in_maps = []
    for c in range(NCORES):
        bb, r = divmod(c, 4)
        sl = slice(512 * r, 512 * (r + 1))
        in_maps.append({
            "x_sh": np.ascontiguousarray(x[4096 * bb + 1024 * r: 4096 * bb + 1024 * (r + 1)]),
            "wq_t": np.ascontiguousarray(qw[sl, :].T),
            "wk_t": np.ascontiguousarray(kw[sl, :].T),
            "wv_t": np.ascontiguousarray(vw[sl, :].T),
            "wo_t": np.ascontiguousarray(owt[sl, :]),
            "gwt": gwt_host,
            "gb": gbv,
            "wf": np.ascontiguousarray(fw[256 * c:256 * (c + 1)]),
        })
    res = run_bass_kernel_spmd(nc, in_maps, core_ids=list(range(NCORES)))
    out = np.concatenate([np.asarray(res.results[c]["out_sh"]) for c in range(NCORES)],
                         axis=0)
    return out.reshape(2, 4096, 2048).astype(np.float32)


# version 20
# speedup vs baseline: 1.0482x; 1.0120x over previous
import numpy as np
from contextlib import ExitStack

import concourse.bass as bass
import concourse.mybir as mybir
from concourse.bass_utils import run_bass_kernel_spmd

F32 = mybir.dt.float32
BF16 = mybir.dt.bfloat16
FP8 = mybir.dt.float8e4
AF = mybir.ActivationFunctionType
ALU = mybir.AluOpType
AX = mybir.AxisListType

NCORES = 8
TT = 8
D = 2048
DC = 16
S = 4096
NH = 4
QCW = 512
NQC = S // QCW
C_SHIFT = 20.0
SCALE = 0.08838834764831845
INV_D = 1.0 / 2048.0
EPS = 1e-5
GROUPS = [[0, 1, 2, 3], [4, 5, 6, 7]]


class Builder:
    def __init__(self, nc, paranoid=False):
        self.nc = nc
        self.paranoid = paranoid
        self.q = {k: [] for k in ("sync", "gpsimd", "vector", "scalar", "tensor")}
        self.sem = {}
        self.cnt = {}
        self.bank_state = [None] * 8
        self.risky = {}

    def add_sem(self, ctx, name):
        self.sem[name] = ctx.enter_context(self.nc.semaphore(name))
        self.cnt[name] = 0

    def flush(self, blockname):
        with self.nc.Block(blockname) as block:
            for eng, fns in self.q.items():
                if fns:
                    getattr(block, eng)(lambda e, fns=tuple(fns): [f(e) for f in fns])
        self.q = {k: [] for k in self.q}

    def op(self, eng, semname, fn, waits=(), inc=1, chain=False, risky=False):
        w = [(s, v) for (s, v) in waits if v > 0]
        if chain or (self.paranoid and eng == "vector"):
            prev = self.cnt[semname]
            if prev > 0:
                w.append((semname, prev))
        pend = self.risky.pop(eng, None)
        if pend is not None:
            w.append(pend)
        self.cnt[semname] += inc
        if risky:
            self.risky[eng] = (semname, self.cnt[semname])
        sems = self.sem

        def run(e, w=tuple(w), fn=fn, semname=semname, inc=inc):
            seen = {}
            for (s, v) in w:
                if seen.get(s, -1) < v:
                    e.wait_ge(sems[s], v)
                    seen[s] = v
            fn(e).then_inc(sems[semname], inc)
        self.q[eng].append(run)
        return self.cnt[semname]

    def wait(self, eng, waits):
        w = [(s, v) for (s, v) in waits if v > 0]
        sems = self.sem

        def run(e):
            for (s, v) in w:
                e.wait_ge(sems[s], v)
        self.q[eng].append(run)

    def dma(self, eng, semname, out, in_, waits=()):
        return self.op(eng, semname, lambda e: e.dma_start(out=out, in_=in_),
                       waits=waits, inc=16)

    def use_bank(self, k):
        st = self.bank_state[k]
        return [st] if st is not None else []

    def done_bank(self, k, semname, val):
        self.bank_state[k] = (semname, val)


def build(paranoid=False, taps=False):
    nc = bass.Bass("TRN2", target_bir_lowering=False)

    x_sh = nc.dram_tensor("x_sh", [1024, D], F32, kind="ExternalInput")
    wq_t = nc.dram_tensor("wq_t", [D, 512], F32, kind="ExternalInput")
    wk_t = nc.dram_tensor("wk_t", [D, 512], F32, kind="ExternalInput")
    wv_t = nc.dram_tensor("wv_t", [D, 512], F32, kind="ExternalInput")
    wo_t = nc.dram_tensor("wo_t", [512, D], F32, kind="ExternalInput")
    gwt = nc.dram_tensor("gwt", [128, 128], F32, kind="ExternalInput")
    gb = nc.dram_tensor("gb", [128, 1], F32, kind="ExternalInput")
    wf = nc.dram_tensor("wf", [256, D], F32, kind="ExternalInput")
    out_sh = nc.dram_tensor("out_sh", [1024, D], F32, kind="ExternalOutput")

    zag_in = [nc.dram_tensor(f"zag_in{i}", [D, QCW], BF16) for i in range(2)]
    zag_out = [nc.dram_tensor(f"zag_out{i}", [4 * D, QCW], BF16) for i in range(2)]
    rs_in = [nc.dram_tensor(f"rs_in{p}", [4 * 512, D], BF16) for p in range(2)]
    rs_out = [nc.dram_tensor(f"rs_out{p}", [512, D], BF16) for p in range(2)]
    y_spill = nc.dram_tensor("y_spill", [1024, D], F32)
    wag_in = nc.dram_tensor("wag_in", [256, D], BF16)
    wag_out = nc.dram_tensor("wag_out", [D, D], BF16)
    wdeq_spill = nc.dram_tensor("wdeq_spill", [DC, 128, D], BF16)
    wq_sp = nc.dram_tensor("wq_sp", [D, 512], BF16)
    wk_sp = nc.dram_tensor("wk_sp", [D, 512], BF16)
    wv_sp = nc.dram_tensor("wv_sp", [D, 512], BF16)
    wo_sp = nc.dram_tensor("wo_sp", [512, D], BF16)
    if taps:
        dbg_am = nc.dram_tensor("dbg_am", [DC, 128, DC], F32, kind="ExternalOutput")
        dbg_r2 = nc.dram_tensor("dbg_r2", [DC, 128, DC], F32, kind="ExternalOutput")
        dbg_q8 = nc.dram_tensor("dbg_q8", [DC, 128, D], FP8, kind="ExternalOutput")
        dbg_dq = nc.dram_tensor("dbg_dq", [DC, 128, D], BF16, kind="ExternalOutput")
        dbg_y = nc.dram_tensor("dbg_y", [1024, D], F32, kind="ExternalOutput")
        dbg_wd = nc.dram_tensor("dbg_wd", [DC, 128, D], BF16, kind="ExternalOutput")
        dbg_zo = nc.dram_tensor("dbg_zo", [4 * D, 1024], BF16, kind="ExternalOutput")
        dbg_q = nc.dram_tensor("dbg_q", [128, NH * S], BF16, kind="ExternalOutput")
        dbg_k = nc.dram_tensor("dbg_k", [128, NH * S], BF16, kind="ExternalOutput")
        dbg_v = nc.dram_tensor("dbg_v", [128, 32 * 512], BF16, kind="ExternalOutput")
        dbg_m = nc.dram_tensor("dbg_m", [128, NH * S], BF16, kind="ExternalOutput")
        dbg_g = nc.dram_tensor("dbg_g", [128, 2 * S], BF16, kind="ExternalOutput")
        dbg_rs = nc.dram_tensor("dbg_rs", [128, D], BF16, kind="ExternalOutput")

    b = Builder(nc, paranoid=paranoid)
    with ExitStack() as ctx:
        for s in ("ld", "gs", "cc", "pe", "dv", "sc", "gp", "lx",
                  "lwf0", "lwf1", "lz0", "lz1", "lw30", "lw31", "lcs",
                  "lwo", "lwd", "ly0", "ly1", "lr0", "lr1", "gr0", "gr1",
                  "sy", "sw", "vy", "ts", "gz0", "gz1", "gw", "lwn0", "lwn1",
                  "lk0", "lk1", "lc2",
                  "lsp0", "lsp1", "lsp2", "lsp3", "lsp4", "lsp5", "lsp6",
                  "lsp7", "ssp0", "ssp1", "lw3", "lk3", "lv3", "gr2", "gr3",
                  "lwd1", "lwd2", "lwd3",
                  "lx0", "lx1"):
            b.add_sem(ctx, s)

        P = [ctx.enter_context(nc.psum_tensor(f"P{k}", [128, 1024], F32))
             for k in range(3)]
        pt = [ctx.enter_context(nc.psum_tensor(f"pt{k}", [128, 1024], BF16))
              for k in range(2)]

        class PV:
            def __init__(self, t, off):
                self.t = t
                self.off = off

            def __getitem__(self, idx):
                rows, cols = idx
                start = self.off + (cols.start or 0)
                stop = self.off + (cols.stop if cols.stop is not None else QCW)
                return self.t[rows, start:stop]

        ps = [PV(P[k // 2], QCW * (k % 2)) for k in range(6)]
        banks = ps + pt

        ident_f = ctx.enter_context(nc.sbuf_tensor("ident_f", [128, 128], F32))
        ident_b = ctx.enter_context(nc.sbuf_tensor("ident_b", [128, 128], BF16))
        mask_add = ctx.enter_context(nc.sbuf_tensor("mask_add", [128, 128], F32))
        ones_b = ctx.enter_context(nc.sbuf_tensor("ones_b", [128, 1], BF16))
        ones1f = ctx.enter_context(nc.sbuf_tensor("ones1f", [1, 128], F32))
        ones1b = ctx.enter_context(nc.sbuf_tensor("ones1b", [1, 128], BF16))
        negC = ctx.enter_context(nc.sbuf_tensor("negC", [128, 1], F32))
        epsb = ctx.enter_context(nc.sbuf_tensor("epsb", [128, 1], F32))
        gb_sb = ctx.enter_context(nc.sbuf_tensor("gb_sb", [128, 1], F32))
        gwt_f = ctx.enter_context(nc.sbuf_tensor("gwt_f", [128, 128], F32))
        gwt_b = ctx.enter_context(nc.sbuf_tensor("gwt_b", [128, 128], BF16))
        rsum = ctx.enter_context(nc.sbuf_tensor("rsum", [128, 2], F32))
        nmean = ctx.enter_context(nc.sbuf_tensor("nmean", [128, 2], F32))
        ssq = ctx.enter_context(nc.sbuf_tensor("ssq", [128, 2], F32))
        sdvv = ctx.enter_context(nc.sbuf_tensor("sdvv", [128, 2], F32))
        rstd = ctx.enter_context(nc.sbuf_tensor("rstd", [128, 2], F32))
        tbx = ctx.enter_context(nc.sbuf_tensor("tbx", [128, 2], F32))
        rsum8 = ctx.enter_context(nc.sbuf_tensor("rsum8", [128, 8], F32))
        nm8 = ctx.enter_context(nc.sbuf_tensor("nm8", [128, 8], F32))
        ssq8 = ctx.enter_context(nc.sbuf_tensor("ssq8", [128, 8], F32))
        tb8 = ctx.enter_context(nc.sbuf_tensor("tb8", [128, 8], F32))
        sd8 = ctx.enter_context(nc.sbuf_tensor("sd8", [128, 8], F32))
        rst8 = ctx.enter_context(nc.sbuf_tensor("rst8", [128, 8], F32))
        nm8b = ctx.enter_context(nc.sbuf_tensor("nm8b", [128, 8], F32))
        ssq8b = ctx.enter_context(nc.sbuf_tensor("ssq8b", [128, 8], F32))
        tb8b = ctx.enter_context(nc.sbuf_tensor("tb8b", [128, 8], F32))
        sd8b = ctx.enter_context(nc.sbuf_tensor("sd8b", [128, 8], F32))
        rst8b = ctx.enter_context(nc.sbuf_tensor("rst8b", [128, 8], F32))

        def emit_ln_stats(xs, tiles, x_waits, NM, SQ, TB, SD, RS):
            for t in tiles:
                b.op("vector", "dv", lambda e, t=t: e.tensor_reduce(
                    out=rsum8[:, t:t + 1], in_=xs[:, t, :], axis=AX.X,
                    op=ALU.add, negate=True), waits=x_waits[t], risky=True)
            sc_q = 0
            for t in tiles:
                sc_q = b.op("scalar", "sc", lambda e, t=t: e.activation(
                    out=sq_scrap[:, 0, :], in_=xs[:, t, :], func=AF.Square,
                    scale=1.0, accum_out=SQ[:, t:t + 1]),
                    waits=list(x_waits[t]), chain=True)
            lo, n = tiles[0], len(tiles)
            b.op("vector", "dv", lambda e, lo=lo, n=n: e.tensor_scalar_mul(
                out=NM[:, lo:lo + n], in0=rsum8[:, lo:lo + n], scalar1=INV_D),
                risky=True)
            b.op("vector", "dv", lambda e, lo=lo, n=n: e.tensor_tensor(
                out=TB[:, lo:lo + n], in0=NM[:, lo:lo + n],
                in1=NM[:, lo:lo + n], op=ALU.mult), risky=True)
            dv_tb = b.op("vector", "dv", lambda e, lo=lo, n=n: e.tensor_scalar(
                out=TB[:, lo:lo + n], in0=TB[:, lo:lo + n], scalar1=-1.0,
                scalar2=EPS, op0=ALU.mult, op1=ALU.add), risky=True)
            sc_sd = 0
            for t in tiles:
                sc_sd = b.op("scalar", "sc", lambda e, t=t: e.activation(
                    out=SD[:, t:t + 1], in_=SQ[:, t:t + 1], func=AF.Sqrt,
                    bias=TB[:, t:t + 1], scale=INV_D),
                    waits=([("dv", dv_tb)] if t == tiles[0] else []), chain=True)
            dv_rs = b.op("vector", "dv", lambda e, lo=lo, n=n: e.reciprocal(
                out=RS[:, lo:lo + n], in_=SD[:, lo:lo + n]),
                waits=[("sc", sc_sd)], risky=True)
            return sc_sd, dv_rs
        nmean2 = ctx.enter_context(nc.sbuf_tensor("nmean2", [128, 2], F32))
        ssq2 = ctx.enter_context(nc.sbuf_tensor("ssq2", [128, 2], F32))
        sdv2 = ctx.enter_context(nc.sbuf_tensor("sdv2", [128, 2], F32))
        rstd2 = ctx.enter_context(nc.sbuf_tensor("rstd2", [128, 2], F32))
        tbx2 = ctx.enter_context(nc.sbuf_tensor("tbx2", [128, 2], F32))
        amax = ctx.enter_context(nc.sbuf_tensor("amax", [128, 2, DC], F32))
        r2 = ctx.enter_context(nc.sbuf_tensor("r2", [128, 2, DC], F32))
        s2 = ctx.enter_context(nc.sbuf_tensor("s2", [128, 2, DC], F32))
        sq_scrap = ctx.enter_context(nc.sbuf_tensor("sq_scrap", [128, 1, D], BF16))

        for (tname, val) in ((ident_f, 0.0), (ident_b, 0.0), (mask_add, 0.0),
                             (ones_b, 1.0), (ones1f, 1.0), (ones1b, 1.0),
                             (negC, -C_SHIFT), (epsb, EPS)):
            b.op("gpsimd", "gp", lambda e, t=tname, v=val: e.memset(t[:], v), chain=True)
        b.op("gpsimd", "gp", lambda e: e.affine_select(
            out=ident_f[:], in_=ident_f[:], compare_op=ALU.not_equal,
            fill=1.0, base=0, pattern=[[-1, 128]], channel_multiplier=1), chain=True)
        b.op("gpsimd", "gp", lambda e: e.affine_select(
            out=ident_b[:], in_=ident_b[:], compare_op=ALU.not_equal,
            fill=1.0, base=0, pattern=[[-1, 128]], channel_multiplier=1), chain=True)
        gm = b.op("gpsimd", "gp", lambda e: e.affine_select(
            out=mask_add[:], in_=mask_add[:], compare_op=ALU.is_ge,
            fill=-1.0e30, base=0, pattern=[[1, 128]], channel_multiplier=-1), chain=True)
        b.dma("sync", "ld", gb_sb[:], gb[:, :])
        lgw = b.dma("sync", "ld", gwt_f[:], gwt[:, :])
        GWT_B = b.op("vector", "dv", lambda e: e.tensor_copy(out=gwt_b[:], in_=gwt_f[:]),
                     waits=[("ld", lgw)])

        ctxA = ExitStack()
        x_str = ctxA.enter_context(nc.sbuf_tensor("x_str", [128, TT, D], F32))
        wdeqT = ctxA.enter_context(nc.sbuf_tensor("wdeqT", [128, DC, D], BF16))
        xh = ctxA.enter_context(nc.sbuf_tensor("xh", [128, 2, D], F32))
        xq8 = ctxA.enter_context(nc.sbuf_tensor("xq8", [128, 1, D], FP8))
        xdq = ctxA.enter_context(nc.sbuf_tensor("xdq", [128, 2, D], BF16))
        xdT = ctxA.enter_context(nc.sbuf_tensor("xdT", [128, 2, DC, 128], BF16))
        scrA = dict(xh=xh, xq8=xq8, xdq=xdq, xdT=xdT, sq=sq_scrap)
        zbuf = ctxA.enter_context(nc.sbuf_tensor("zbuf", [128, 2, 4, 128], BF16))
        zln = ctxA.enter_context(nc.sbuf_tensor("zln", [128, 2, D], BF16))
        ctxA0 = ExitStack()
        wfn = ctxA0.enter_context(nc.sbuf_tensor("wfn", [128, 2, D], F32))
        wq8 = ctxA0.enter_context(nc.sbuf_tensor("wq8", [128, 1, D], FP8))
        wdq = ctxA0.enter_context(nc.sbuf_tensor("wdq", [128, 2, D], BF16))

        ld_x = []
        for t in range(TT):
            sem = "lx0" if t == 0 else ("lx1" if t == 1 else "lx")
            ld_x.append((sem, b.dma("sync", sem, x_str[:, t, :],
                                    x_sh[128 * t:128 * (t + 1), :])))
        ld_x = [(s, v if s != "lx" else b.cnt["lx"]) for (s, v) in ld_x]

        GP_INIT = gm
        pe_wtr = [[0] * 4 for _ in range(DC)]
        dv_wev_flat = [0] * DC
        gw_sp = 0
        for i in range(2):
            p = i % 2
            ldwf = b.dma("sync", f"lwf{p}", wfn[:, p, :], wf[128 * i:128 * (i + 1), :])
            b.op("vector", "dv", lambda e, p=p: e.tensor_reduce(
                out=amax[:, p, :],
                in_=wfn[:, p, :].rearrange("p (c k) -> p c k", k=128),
                axis=AX.X, op=ALU.max, apply_absolute_value=True),
                waits=[(f"lwf{p}", ldwf)], risky=True)
            b.op("vector", "dv", lambda e, p=p: e.reciprocal(
                out=r2[:, p, :], in_=amax[:, p, :]), risky=True)
            b.op("vector", "dv", lambda e, p=p: e.tensor_scalar_mul(
                out=r2[:, p, :], in0=r2[:, p, :], scalar1=224.0), risky=True)
            dv_r2w = b.op("vector", "dv", lambda e, p=p: e.tensor_scalar_mul(
                out=s2[:, p, :], in0=amax[:, p, :], scalar1=1.0 / 224.0), risky=True)
            for blk in range(DC):
                b.op("vector", "dv", lambda e, p=p, blk=blk: e.tensor_scalar(
                    out=wq8[:, 0, 128 * blk:128 * (blk + 1)],
                    in0=wfn[:, p, 128 * blk:128 * (blk + 1)],
                    scalar1=r2[:, p, blk:blk + 1], scalar2=None, op0=ALU.mult))
            dv_wdq = 0
            for blk in range(DC):
                dv_wdq = b.op("vector", "dv", lambda e, p=p, blk=blk: e.tensor_scalar(
                    out=wdq[:, p, 128 * blk:128 * (blk + 1)],
                    in0=wq8[:, 0, 128 * blk:128 * (blk + 1)],
                    scalar1=s2[:, p, blk:blk + 1], scalar2=None, op0=ALU.mult))
            gw_sp = b.dma("gpsimd", "gw", wag_in[128 * i:128 * (i + 1), :],
                          wdq[:, p, :], waits=[("dv", dv_wdq)])
        cc_w = b.op("gpsimd", "cc", lambda e: e.collective_compute(
            "AllGather", ALU.bypass, replica_groups=[list(range(8))],
            ins=[wag_in.ap().opt()], outs=[wag_out.ap().opt()]),
            waits=[("gw", gw_sp)], chain=True)
        for i in range(DC):
            p = i % 2
            ldn = b.dma("sync", f"lwn{p}", wdq[:, p, :],
                        wag_out[128 * i:128 * (i + 1), :],
                        waits=[("cc", cc_w)] + ([("pe", pe_wtr[i - 2][3])]
                                                if i >= 2 else [("gw", gw_sp)]))
            for k in range(4):
                bank = 6 + (k % 2)
                for j in range(4):
                    c = 4 * k + j
                    w_ = [(f"lwn{p}", ldn)] if (k == 0 and j == 0) else []
                    if j == 0:
                        w_ += b.use_bank(bank)
                    if (i == 0 and k == 0 and j == 0) or b.paranoid:
                        w_.append(("gp", GP_INIT))
                    pe_wtr[i][k] = b.op("tensor", "pe", lambda e, p=p, c=c, bank=bank, j=j: e.transpose(
                        banks[bank][:, 128 * j:128 * (j + 1)],
                        wdq[:, p, 128 * c:128 * (c + 1)], ident_b[:]),
                        waits=w_)
                dvev = b.op("vector", "dv", lambda e, i=i, k=k, bank=bank: e.tensor_copy(
                    out=wdeqT[:, 4 * k:4 * (k + 1), 128 * i:128 * (i + 1)],
                    in_=banks[bank][:, 0:512].rearrange("p (j k) -> p j k", k=128)),
                    waits=[("pe", pe_wtr[i][k])])
                b.done_bank(bank, "dv", dvev)
                dv_wev_flat[i] = dvev
        W_DONE = dvev

        def emit_ffn(xs, wT, w_ready, x_ready, scr, NM, RS, ln_sc, ln_dv,
                     store=None, store_sem="gs", hook=None, ln_per_tile=None):
            xh = scr["xh"]; xq8 = scr["xq8"]; xdq = scr["xdq"]
            xdT = scr["xdT"]
            sc_sdv = [0] * TT
            sc_ssqv = [0] * TT
            sc_q8 = [0] * TT
            dv_xdq = [0] * TT
            dv_ev = [[0] * 4 for _ in range(TT)]
            dv_res = [[0] * 4 for _ in range(TT)]
            pe_tr = [[0] * 4 for _ in range(TT)]
            pe_mm = [[0] * 4 for _ in range(TT)]
            st_out = []
            for t in range(TT):
                p = t % 2
                if ln_per_tile is not None:
                    lsc, ldv = ln_per_tile[t]
                else:
                    lsc, ldv = ln_sc, ln_dv
                b.op("vector", "dv", lambda e, t=t, p=p: e.tensor_scalar(
                    out=xh[:, p, :], in0=xs[:, t, :], scalar1=NM[:, t:t + 1],
                    scalar2=RS[:, t:t + 1], op0=ALU.add, op1=ALU.mult),
                    waits=[("dv", ldv), ("sc", lsc)] + list(x_ready[t]))
                b.op("vector", "dv", lambda e, p=p: e.tensor_reduce(
                    out=amax[:, p, :],
                    in_=xh[:, p, :].rearrange("p (c k) -> p c k", k=128),
                    axis=AX.X, op=ALU.max, apply_absolute_value=True), risky=True)
                b.op("vector", "dv", lambda e, p=p: e.reciprocal(
                    out=r2[:, p, :], in_=amax[:, p, :]), risky=True)
                b.op("vector", "dv", lambda e, p=p: e.tensor_scalar_mul(
                    out=r2[:, p, :], in0=r2[:, p, :], scalar1=224.0), risky=True)
                dv_r2 = b.op("vector", "dv", lambda e, p=p: e.tensor_scalar_mul(
                    out=s2[:, p, :], in0=amax[:, p, :], scalar1=1.0 / 224.0), risky=True)
                for blk in range(DC):
                    sc_q8[t] = b.op("vector", "dv", lambda e, p=p, blk=blk: e.tensor_scalar(
                        out=xq8[:, 0, 128 * blk:128 * (blk + 1)],
                        in0=xh[:, p, 128 * blk:128 * (blk + 1)],
                        scalar1=r2[:, p, blk:blk + 1], scalar2=None, op0=ALU.mult))
                for blk in range(DC):
                    w_ = [("pe", pe_tr[t - 2][3])] if (t >= 2 and blk == 0) else []
                    dv_xdq[t] = b.op("vector", "dv", lambda e, p=p, blk=blk: e.tensor_scalar(
                        out=xdq[:, p, 128 * blk:128 * (blk + 1)],
                        in0=xq8[:, 0, 128 * blk:128 * (blk + 1)],
                        scalar1=s2[:, p, blk:blk + 1], scalar2=None, op0=ALU.mult),
                        waits=w_)
                for k in range(4):
                    bank = 6 + (k % 2)
                    for j in range(4):
                        c = 4 * k + j
                        w_ = [("dv", dv_xdq[t])] if (k == 0 and j == 0) else []
                        if j == 0:
                            w_ += b.use_bank(bank)
                        if b.paranoid:
                            w_.append(("gp", gm))
                        pe_tr[t][k] = b.op("tensor", "pe", lambda e, p=p, c=c, bank=bank, j=j: e.transpose(
                            banks[bank][:, 128 * j:128 * (j + 1)],
                            xdq[:, p, 128 * c:128 * (c + 1)], ident_b[:]), waits=w_)
                    w_ev = [("pe", pe_tr[t][k])]
                    if t >= 2 and k == 0:
                        w_ev.append(("pe", pe_mm[t - 2][3]))
                    dv_ev[t][k] = b.op("vector", "dv", lambda e, k=k, p=p, bank=bank: e.tensor_copy(
                        out=xdT[:, p, 4 * k:4 * (k + 1), :],
                        in_=banks[bank][:, 0:512].rearrange("p (j k) -> p j k", k=128)),
                        waits=w_ev)
                    b.done_bank(bank, "dv", dv_ev[t][k])
                def emit_resid(tt):
                    for oc in range(4):
                        dv_res[tt][oc] = b.op("vector", "dv",
                                              lambda e, tt=tt, oc=oc: e.scalar_tensor_tensor(
                                                  out=xs[:, tt, 512 * oc:512 * (oc + 1)],
                                                  in0=ps[oc][:, 0:512], scalar=0.5,
                                                  in1=xs[:, tt, 512 * oc:512 * (oc + 1)],
                                                  op0=ALU.mult, op1=ALU.add),
                                              waits=[("pe", pe_mm[tt][oc])])
                        b.done_bank(oc, "dv", dv_res[tt][oc])
                    if store is not None:
                        eng = "scalar" if store_sem == "vy" else "gpsimd"
                        st_out.append(b.dma(eng, store_sem,
                                            store[128 * tt:128 * (tt + 1), :],
                                            xs[:, tt, :], waits=[("dv", dv_res[tt][3])]))
                if t >= 1:
                    emit_resid(t - 1)
                mm_first = True
                for oc in range(4):
                    for c in range(DC):
                        w_ = []
                        if c == 0:
                            w_ = [("dv", dv_ev[t][3]), w_ready[oc]] + b.use_bank(oc)
                        pe_mm[t][oc] = b.op("tensor", "pe", lambda e, p=p, c=c, oc=oc: e.matmul(
                            ps[oc][:, 0:512], xdT[:, p, c, :],
                            wT[:, c, 512 * oc:512 * (oc + 1)],
                            start=(c == 0), stop=(c == DC - 1)), waits=w_)
                if hook is not None and t >= 1:
                    hook(t - 1, dv_res[t - 1][3])
            emit_resid(TT - 1)
            if hook is not None:
                hook(TT - 1, dv_res[TT - 1][3])
            return dv_res, st_out

        W_OC = [("dv", 0)] * 4
        for oc in range(4):
            W_OC[oc] = ("dv", dv_wev_flat[4 * oc + 3])
        SW_SPILL = 0
        for c in range(DC):
            SW_SPILL = b.dma("gpsimd", "ts", wdeq_spill[c, :, :], wdeqT[:, c, :],
                             waits=[("dv", W_DONE)])
        st2 = {"sc_sd": [0] * TT, "pe_ztr": [0] * TT, "gs_zb": [[0] * 4 for _ in range(TT)],
               "cc_ag": [0, 0], "dv_z": [0] * TT}
        xh_a = scrA["xh"]

        def ln2_hook(t, dv_resv):
            if t != 3 and t != TT - 1:
                return
            half = 0 if t == 3 else 1
            tiles = [4 * half + i for i in range(4)]
            xw = {tt: [("dv", dv_resv)] for tt in tiles}
            sc_sd, dv_rs = emit_ln_stats(x_str, tiles, xw,
                                         nm8b, ssq8b, tb8b, sd8b, rst8b)
            for tt in tiles:
                p = tt % 2
                dv_z = b.op("vector", "dv", lambda e, tt=tt, p=p: e.tensor_scalar(
                    out=zln[:, p, :], in0=x_str[:, tt, :],
                    scalar1=nm8b[:, tt:tt + 1], scalar2=rst8b[:, tt:tt + 1],
                    op0=ALU.add, op1=ALU.mult),
                    waits=([("dv", dv_rs), ("sc", sc_sd)] if tt == tiles[0] else [])
                    + ([("pe", st2["pe_ztr"][tt - 2])] if tt >= 2 else []))
                st2["dv_z"][tt] = dv_z
                for k in range(4):
                    bank = 6 + (k % 2)
                    pe_tr = 0
                    for j in range(4):
                        c = 4 * k + j
                        w_ = [("dv", dv_z)] if (k == 0 and j == 0) else []
                        if j == 0:
                            w_ += b.use_bank(bank)
                        pe_tr = b.op("tensor", "pe", lambda e, p=p, c=c, bank=bank, j=j: e.transpose(
                            banks[bank][:, 128 * j:128 * (j + 1)],
                            zln[:, p, 128 * c:128 * (c + 1)], ident_b[:]), waits=w_)
                    st2["pe_ztr"][tt] = pe_tr
                    zpar = (4 * tt + k) % 2
                    w_ev = [("pe", pe_tr)]
                    if tt >= 1 or k >= 2:
                        pk = (4 * tt + k) - 2
                        w_ev.append((f"gz{zpar}", st2["gs_zb"][pk // 4][pk % 4]))
                    dv_ze = b.op("vector", "dv", lambda e, zpar=zpar, bank=bank: e.tensor_copy(
                        out=zbuf[:, zpar, :, :],
                        in_=banks[bank][:, 0:512].rearrange("p (j k) -> p j k", k=128)),
                        waits=w_ev)
                    b.done_bank(bank, "dv", dv_ze)
                    halfq, tq = divmod(tt, 4)
                    for j in range(4):
                        c = 4 * k + j
                        st2["gs_zb"][tt][k] = b.dma(
                            "gpsimd", f"gz{zpar}",
                            zag_in[halfq][128 * c:128 * (c + 1), 128 * tq:128 * (tq + 1)],
                            zbuf[:, zpar, j, :], waits=([("dv", dv_ze)] if j == 0 else []))
            st2["cc_ag"][half] = b.op("gpsimd", "cc", lambda e, half=half: e.collective_compute(
                "AllGather", ALU.bypass, replica_groups=GROUPS,
                ins=[zag_in[half].ap().opt()], outs=[zag_out[half].ap().opt()]),
                waits=[("gz0", b.cnt["gz0"]), ("gz1", b.cnt["gz1"])], chain=True)

        spill_jobs = []
        for (wsrc, wdst) in ((wq_t, wq_sp), (wk_t, wk_sp), (wv_t, wv_sp)):
            for c in range(DC):
                spill_jobs.append((wsrc[128 * c:128 * (c + 1), :],
                                   wdst[128 * c:128 * (c + 1), :]))
        for c in range(4):
            for u in range(4):
                spill_jobs.append(
                    (wo_t[128 * c:128 * (c + 1), 512 * u:512 * (u + 1)],
                     wo_sp[128 * c:128 * (c + 1), 512 * u:512 * (u + 1)]))
        SP = {"k": 0, "lms": {}, "cms": {}, "sms": {}}
        wspf = [wfn[:, s // 4, 512 * (s % 4):512 * (s % 4 + 1)] for s in range(8)]
        wspb = [wdq[:, 0, 512 * p:512 * (p + 1)] for p in range(2)]

        def spill_casts(k):
            for i in range(8 * k, min(8 * k + 8, len(spill_jobs))):
                _, dst = spill_jobs[i]
                s = i % 8
                wc = [(f"lsp{s}", SP["lms"][i])]
                if i >= 2:
                    wc.append((f"ssp{i % 2}", SP["sms"][i - 2]))
                if i == 0:
                    wc.append(("gw", gw_sp))
                cv = b.op("vector", "dv", lambda e, s=s, i=i: e.tensor_copy(
                    out=wspb[i % 2], in_=wspf[s]), waits=wc)
                SP["cms"][i] = cv
                SP["sms"][i] = b.dma("sync", f"ssp{i % 2}", dst,
                                     wspb[i % 2], waits=[("dv", cv)])

        def spill_loads(k):
            for i in range(8 * k, min(8 * k + 8, len(spill_jobs))):
                src, _ = spill_jobs[i]
                s = i % 8
                wl = [("dv", SP["cms"].get(i - 8, 0))]
                if i == 0:
                    wl.append(("dv", W_DONE))
                SP["lms"][i] = b.dma("scalar", f"lsp{s}", wspf[s], src,
                                     waits=wl)

        def spill_step():
            k = SP["k"]
            SP["k"] = k + 1
            spill_casts(k)
            spill_loads(k + 1)

        def hook_all(t, dvr):
            ln2_hook(t, dvr)
            spill_step()

        spill_loads(0)
        ln1_sc, ln1_dv = emit_ln_stats(x_str, list(range(TT)),
                                       [[ld_x[t]] for t in range(TT)],
                                       nm8, ssq8, tb8, sd8, rst8)
        dv_res1, y_st = emit_ffn(x_str, wdeqT, W_OC,
                                 [[ld_x[t]] for t in range(TT)], scrA,
                                 nm8, rst8, ln1_sc, ln1_dv,
                                 store=y_spill, store_sem="vy", hook=hook_all)
        SPW = [("ssp0", SP["sms"][len(spill_jobs) - 2]),
               ("ssp1", SP["sms"][len(spill_jobs) - 1])]
        cc_ag = st2["cc_ag"]
        GS_SPILL = b.cnt["gs"]
        FENCE2 = [(s, b.cnt[s]) for s in ("ts", "pe", "dv", "sc")]
        FENCE2V = FENCE2 + [(s, b.cnt[s]) for s in ("gz0", "gz1")]
        ctxA0.close()

        ctxA.close()
        ctxBm = ExitStack()
        m_t = ctxBm.enter_context(nc.sbuf_tensor("m_t", [128, NH, S], BF16, side="right"))
        ctxB = ExitStack()
        q_t = ctxB.enter_context(nc.sbuf_tensor("q_t", [128, NH, S], BF16))
        k_t = ctxB.enter_context(nc.sbuf_tensor("k_t", [128, NH, S], BF16))
        v_t = ctxB.enter_context(nc.sbuf_tensor("v_t", [128, 32, 512], BF16))
        ctxB3 = ExitStack()
        zt_q = ctxB3.enter_context(nc.sbuf_tensor("zt_q", [128, DC, 512], BF16))
        wq_b = ctxB3.enter_context(nc.sbuf_tensor("wq_b", [128, DC, 512], BF16))
        wk_b = ctxB3.enter_context(nc.sbuf_tensor("wk_b", [128, DC, 512], BF16))
        wv_b = ctxB3.enter_context(nc.sbuf_tensor("wv_b", [128, DC, 512], BF16))

        for (wsrc, wbbuf, semw) in ((wq_sp, wq_b, "lw3"), (wk_sp, wk_b, "lk3"),
                                    (wv_sp, wv_b, "lv3")):
            for c in range(DC):
                w_ = []
                if wbbuf is wq_b and c == 0:
                    w_ = list(FENCE2V) + SPW + [("vy", y_st[-1])]
                b.dma("scalar", semw, wbbuf[:, c, :],
                      wsrc[128 * c:128 * (c + 1), :], waits=w_)
        pe_zread = [0] * DC
        for hq in range(8):
            qq, half = divmod(hq, 2)
            ldz = 0
            for c in range(DC):
                w_ = [("pe", pe_zread[c])] if hq >= 1 else (
                    list(FENCE2) + SPW + [("vy", y_st[-1])])
                ldz = b.dma("sync", f"lz{hq % 2}", zt_q[:, c, :],
                            zag_out[half][D * qq + 128 * c:D * qq + 128 * (c + 1), :],
                            waits=w_ + [("cc", cc_ag[half])])
            zq_ready = (f"lz{hq % 2}", ldz)
            for (wb, dst) in ((wq_b, q_t), (wk_b, k_t)):
                pe_h = [0] * NH
                for c in range(DC):
                    for h in range(NH):
                        w_ = []
                        if h == 0 and hq == 0 and c == 0:
                            w_.append(("lw3" if wb is wq_b else "lk3", 16 * DC))
                        if c == 0:
                            w_ = w_ + [zq_ready] + b.use_bank(h)
                        pe_h[h] = b.op("tensor", "pe",
                                       lambda e, c=c, h=h, wb=wb: e.matmul(
                                           ps[h][:, 0:512],
                                           wb[:, c, 128 * h:128 * (h + 1)],
                                           zt_q[:, c, :],
                                           start=(c == 0), stop=(c == DC - 1)),
                                       waits=w_)
                for h in range(NH):
                    dvx = b.op("vector", "dv",
                               lambda e, h=h, hq=hq, dst=dst: e.tensor_copy(
                                   out=dst[:, h, 512 * hq:512 * (hq + 1)],
                                   in_=ps[h][:, 0:512]), waits=[("pe", pe_h[h])])
                    b.done_bank(h, "dv", dvx)
            pe_g = [0] * 4
            for c in range(DC):
                for u in range(4):
                    w_ = []
                    if c == 0:
                        w_ = [zq_ready, ("lv3", 16 * DC)] + b.use_bank(u)
                    pe_g[u] = b.op("tensor", "pe",
                                   lambda e, c=c, u=u: e.matmul(
                                       ps[u][:, 0:512],
                                       zt_q[:, c, 128 * u:128 * (u + 1)],
                                       wv_b[:, c, 0:512],
                                       start=(c == 0), stop=(c == DC - 1)), waits=w_)
                    pe_zread[c] = pe_g[u]
            for u in range(4):
                dvx = b.op("vector", "dv", lambda e, hq=hq, u=u: e.tensor_copy(
                    out=v_t[:, 4 * hq + u, :], in_=ps[u][:, 0:512]),
                    waits=[("pe", pe_g[u])])
                b.done_bank(u, "dv", dvx)
        PH3_DV = b.cnt["dv"]
        ctxB3.close()

        ctxB4 = ExitStack()
        gate_h = ctxB4.enter_context(nc.sbuf_tensor("gate_h", [128, 2, S], BF16))
        av_sb = ctxB4.enter_context(nc.sbuf_tensor("av_sb", [128, 1, S], BF16))
        p_t = ctxB4.enter_context(nc.sbuf_tensor("p_t", [128, 3, 1024], BF16))
        cs_rows = [ctxB4.enter_context(nc.sbuf_tensor(f"cs_row{i}", [1, S], BF16))
                   for i in range(2)]
        cs128 = ctxB4.enter_context(nc.sbuf_tensor("cs128", [128, 32], BF16))
        rc128 = ctxB4.enter_context(nc.sbuf_tensor("rc128", [128, 32], BF16))
        recip_b = ctxB4.enter_context(nc.sbuf_tensor("recip_b", [128, S], BF16))

        dv_mt = [0] * NH
        pt_pe = [0, 0, 0]
        tail_args = []

        def emit_tail(h, ph, dvc, scg):
            ld1 = b.dma("sync", "lcs", cs128[:, :], cs_rows[ph][0:1, :],
                        waits=[("dv", dvc)])
            def _recip(e):
                with nc.allow_low_precision(reason="bf16 norm factor, matches recip_b precision"):
                    return e.reciprocal(out=rc128[:, :], in_=cs128[:, :])
            dvr = b.op("vector", "dv", _recip,
                waits=[("lcs", ld1)], risky=True)
            ld2 = b.dma("sync", "lc2", cs_rows[ph][0:1, :], rc128[:, :],
                        waits=[("dv", dvr)])
            dv_rb = 0
            for qc in range(NQC):
                bank = qc % 4
                w_ = b.use_bank(bank) + [("lc2", ld2)]
                peb = b.op("tensor", "pe", lambda e, qc=qc, bank=bank, ph=ph: e.matmul(
                    ps[bank][:, 0:512], ones1b[:],
                    cs_rows[ph][0:1, 512 * qc:512 * (qc + 1)],
                    start=True, stop=True), waits=w_)
                dv_rb = b.op("vector", "dv", lambda e, qc=qc, bank=bank: e.tensor_copy(
                    out=recip_b[:, 512 * qc:512 * (qc + 1)], in_=ps[bank][:, 0:512]),
                    waits=[("pe", peb)])
                b.done_bank(bank, "dv", dv_rb)
            dvt = b.op("vector", "dv", lambda e, ph=ph: e.tensor_tensor(
                out=recip_b[:, :], in0=av_sb[:, 0, :], in1=recip_b[:, :], op=ALU.mult),
                waits=[("dv", dv_rb), ("sc", scg)])
            dv_mt[h] = b.op("vector", "dv", lambda e, h=h, ph=ph: e.tensor_tensor(
                out=m_t[:, h, :], in0=recip_b[:, :], in1=gate_h[:, ph, :], op=ALU.mult),
                waits=[("dv", dvt)])
        for h in range(NH):
            ph = h % 2
            scg = 0
            for qc in range(NQC):
                bank = qc % 4
                w_ = b.use_bank(bank)
                if qc == 0:
                    w_ += [("gp", gm), ("dv", GWT_B)]
                    if h == 0:
                        w_.append(("dv", PH3_DV))
                    if h >= 2:
                        w_.append(("dv", dv_mt[h - 2]))
                peg = b.op("tensor", "pe", lambda e, h=h, qc=qc, bank=bank: e.matmul(
                    ps[bank][:, 0:512], gwt_b[:], q_t[:, h, 512 * qc:512 * (qc + 1)],
                    start=True, stop=True), waits=w_)
                scg = b.op("scalar", "sc", lambda e, ph=ph, qc=qc, bank=bank: e.activation(
                    out=gate_h[:, ph, 512 * qc:512 * (qc + 1)], in_=ps[bank][:, 0:512],
                    func=AF.Sigmoid, bias=gb_sb[:, 0:1], scale=1.0),
                    waits=[("pe", peg)] + ([("ld", 32), ("gp", gm)]
                                           if (h == 0 and qc == 0) else []),
                    chain=True)
                b.done_bank(bank, "sc", scg)
            if tail_args:
                emit_tail(*tail_args.pop(0))
            dva = 0
            dvc = 0
            for qc in range(NQC):
                klim = 4 * qc + 4
                npair = klim // 2
                pend = []

                def flush_avcs(h=h):
                    pr, slot, pslot, sc_e = pend.pop(0)
                    ki0, ki1 = 2 * pr, 2 * pr + 1
                    klim_ = pend_klim[0]
                    pe_av = pe_cs = 0
                    for u, ki in ((0, ki0), (1, ki1)):
                        w_ = [("sc", sc_e)]
                        if ki == 0:
                            w_ += b.use_bank(4)
                        pe_av = b.op("tensor", "pe",
                                     lambda e, h=h, ki=ki, pslot=pslot, u=u, klim_=klim_: e.matmul(
                                         ps[4][:, 0:512],
                                         v_t[:, ki, 128 * h:128 * (h + 1)],
                                         p_t[:, pslot, 512 * u:512 * (u + 1)],
                                         start=(ki == 0), stop=(ki == klim_ - 1)),
                                     waits=w_)
                    for u, ki in ((0, ki0), (1, ki1)):
                        w_ = [("sc", sc_e)]
                        if ki == 0:
                            w_ += b.use_bank(5)
                        pe_cs = b.op("tensor", "pe",
                                     lambda e, ki=ki, pslot=pslot, u=u, klim_=klim_: e.matmul(
                                         ps[5][0:1, 0:512], ones_b[:],
                                         p_t[:, pslot, 512 * u:512 * (u + 1)],
                                         start=(ki == 0), stop=(ki == klim_ - 1)),
                                     waits=w_)
                    pt_pe[pslot] = pe_cs
                    return pe_av, pe_cs

                pend_klim = [klim]
                pe_av_last = pe_cs_last = 0
                for pr in range(npair):
                    ki0, ki1 = 2 * pr, 2 * pr + 1
                    off0 = max(0, 128 * ki0 - 512 * qc)
                    off1 = max(0, 128 * ki1 - 512 * qc)
                    slot = pr % 2
                    pslot = pr % 3
                    wS = b.use_bank(2 * slot) + b.use_bank(2 * slot + 1)
                    pes = b.op("tensor", "pe",
                               lambda e, h=h, qc=qc, ki0=ki0, off0=off0, slot=slot: e.matmul(
                                   P[slot][:, off0:512],
                                   k_t[:, h, 128 * ki0:128 * (ki0 + 1)],
                                   q_t[:, h, 512 * qc + off0:512 * (qc + 1)],
                                   start=True, stop=True), waits=wS)
                    pes = b.op("tensor", "pe",
                               lambda e, h=h, qc=qc, ki1=ki1, off1=off1, slot=slot: e.matmul(
                                   P[slot][:, 512 + off1:1024],
                                   k_t[:, h, 128 * ki1:128 * (ki1 + 1)],
                                   q_t[:, h, 512 * qc + off1:512 * (qc + 1)],
                                   start=True, stop=True), waits=wS)
                    dv_ms = 0
                    if ki1 >= 4 * qc:
                        wm = [("pe", pes)]
                        if h == 0 and qc == 0 and pr == 0:
                            wm.append(("gp", gm))
                        if off0 or ki0 >= 4 * qc:
                            dv_ms = b.op("vector", "dv",
                                         lambda e, slot=slot, off0=off0: e.tensor_tensor(
                                             out=P[slot][:, off0:off0 + 128],
                                             in0=P[slot][:, off0:off0 + 128],
                                             in1=mask_add[:], op=ALU.add), waits=wm)
                            wm = []
                        dv_ms = b.op("vector", "dv",
                                     lambda e, slot=slot, off1=off1: e.tensor_tensor(
                                         out=P[slot][:, 512 + off1:512 + off1 + 128],
                                         in0=P[slot][:, 512 + off1:512 + off1 + 128],
                                         in1=mask_add[:], op=ALU.add), waits=wm)
                        if off0:
                            dv_ms = b.op("vector", "dv", lambda e, slot=slot, off0=off0: e.memset(
                                P[slot][:, 0:off0], -1.0e30))
                        if off1:
                            dv_ms = b.op("vector", "dv", lambda e, slot=slot, off1=off1: e.memset(
                                P[slot][:, 512:512 + off1], -1.0e30))
                    w_e = [("pe", pes), ("pe", pt_pe[pslot]), ("dv", dv_ms)]
                    sc_e = b.op("scalar", "sc",
                                lambda e, slot=slot, pslot=pslot: e.activation(
                                    out=p_t[:, pslot, :], in_=P[slot][:, 0:1024],
                                    func=AF.Exp, bias=negC[:, 0:1], scale=SCALE),
                                chain=True, waits=w_e)
                    b.done_bank(2 * slot, "sc", sc_e)
                    b.done_bank(2 * slot + 1, "sc", sc_e)
                    pend.append((pr, slot, pslot, sc_e))
                    if len(pend) > 2:
                        pe_av_last, pe_cs_last = flush_avcs()
                while pend:
                    pe_av_last, pe_cs_last = flush_avcs()
                dva = b.op("vector", "dv", lambda e, ph=ph, qc=qc: e.tensor_copy(
                    out=av_sb[:, 0, 512 * qc:512 * (qc + 1)], in_=ps[4][:, 0:512]),
                    waits=[("pe", pe_av_last)])
                b.done_bank(4, "dv", dva)
                dvc = b.op("vector", "dv", lambda e, qc=qc, ph=ph: e.tensor_copy(
                    out=cs_rows[ph][0:1, 512 * qc:512 * (qc + 1)],
                    in_=ps[5][0:1, 0:512]),
                    waits=[("pe", pe_cs_last)])
                b.done_bank(5, "dv", dvc)
            tail_args.append((h, ph, dvc, scg))
        while tail_args:
            emit_tail(*tail_args.pop(0))
        ctxB4.close()
        ctxB.close()

        ctxC = ExitStack()
        x_str2 = ctxC.enter_context(nc.sbuf_tensor("x_str2", [128, TT, D], F32))
        wdeqT2 = ctxC.enter_context(nc.sbuf_tensor("wdeqT2", [128, DC, D], BF16))
        ctxC0 = ExitStack()
        wo_b = ctxC0.enter_context(nc.sbuf_tensor("wo_b", [128, 4, D], BF16))
        rs_buf = ctxC0.enter_context(nc.sbuf_tensor("rs_buf", [128, 4, 2048], BF16))

        for c in range(4):
            b.dma("sync", "lwo", wo_b[:, c, :], wo_sp[128 * c:128 * (c + 1), :],
                  waits=(([("dv", dv_mt[NH - 1])] + SPW) if c == 0 else []))
        cc_rs = [0, 0]
        gr_last = [0, 0, 0, 0]
        nqt = 0
        for j in range(TT):
            for g in range(4):
                qt = 8 * g + j
                slot = nqt % 4
                dvo = 0
                for half in range(2):
                    peo = 0
                    for e2p in (2 * half, 2 * half + 1):
                        view = e2p
                        for c in range(4):
                            w_ = []
                            if c == 0:
                                w_ = b.use_bank(view)
                                if nqt == 0 and e2p == 0:
                                    w_ += [("lwo", 64), ("dv", dv_mt[NH - 1]),
                                           ("dv", dv_mt[NH - 2])]
                            peo = b.op("tensor", "pe",
                                       lambda e, e2p=e2p, qt=qt, c=c, view=view: e.matmul(
                                           ps[view][:, 0:512],
                                           m_t[:, c, 128 * qt:128 * (qt + 1)],
                                           wo_b[:, c, 512 * e2p:512 * (e2p + 1)],
                                           start=(c == 0), stop=(c == 3)), waits=w_)
                    Pt = P[half]
                    dvo = b.op("vector", "dv", lambda e, Pt=Pt, slot=slot, half=half: e.tensor_copy(
                        out=rs_buf[:, slot, 1024 * half:1024 * (half + 1)],
                        in_=Pt[:, 0:1024]),
                        waits=[("pe", peo), (f"gr{slot}", gr_last[slot])])
                    b.done_bank(2 * half, "dv", dvo)
                    b.done_bank(2 * half + 1, "dv", dvo)
                hv = j // 4
                gr_last[slot] = b.dma("gpsimd" if slot % 2 == 0 else "scalar",
                                      f"gr{slot}",
                                      rs_in[hv][512 * g + 128 * (j % 4):
                                                512 * g + 128 * (j % 4) + 128, :],
                                      rs_buf[:, slot, :], waits=[("dv", dvo)])
                nqt += 1
            if j % 4 == 3:
                cc_rs[j // 4] = b.op("gpsimd", "cc",
                                     lambda e, hv=j // 4: e.collective_compute(
                                         "ReduceScatter", ALU.add,
                                         replica_groups=GROUPS,
                                         ins=[rs_in[hv].ap().opt()],
                                         outs=[rs_out[hv].ap().opt()]),
                                     waits=[(f"gr{s}", gr_last[s])
                                            for s in range(4)], chain=True)
        PE_OPROJ_END = b.cnt["pe"]
        W2_OC = []
        for oc in range(4):
            sem = "lwd" if oc == 0 else f"lwd{oc}"
            for c in range(DC):
                ld_wd = b.dma("sync", sem, wdeqT2[:, c, 512 * oc:512 * (oc + 1)],
                              wdeq_spill[c, :, 512 * oc:512 * (oc + 1)],
                              waits=([("ts", SW_SPILL), ("dv", dv_mt[NH - 1])]
                                     if (oc == 0 and c == 0) else []))
            W2_OC.append((sem, ld_wd))
        if taps:
            gt = b.dma("gpsimd", "gs", dbg_y[:, :], y_spill[:, :],
                       waits=[("gs", GS_SPILL)])
            gt = b.dma("gpsimd", "gs", dbg_wd[:, :, :], wdeq_spill[:, :, :])
            gt = b.dma("gpsimd", "gs", dbg_zo[:, 0:512], zag_out[0][:, :],
                       waits=[("cc", cc_ag[1])])
            gt = b.dma("gpsimd", "gs", dbg_zo[:, 512:1024], zag_out[1][:, :])
            gt = b.dma("gpsimd", "gs", dbg_rs[:, :], rs_out[0][0:128, :],
                       waits=[("cc", cc_rs[1])])
            b.wait("gpsimd", [("gs", gt)])
        ctxC0.close()
        ctxBm.close()
        y_buf = ctxC.enter_context(nc.sbuf_tensor("y_buf", [128, 2, D], F32))
        rs_sb = ctxC.enter_context(nc.sbuf_tensor("rs_sb", [128, 2, D], BF16))
        xh2 = ctxC.enter_context(nc.sbuf_tensor("xh2", [128, 2, D], F32))
        xq82 = ctxC.enter_context(nc.sbuf_tensor("xq82", [128, 1, D], FP8))
        xdq2 = ctxC.enter_context(nc.sbuf_tensor("xdq2", [128, 2, D], BF16))
        xdT2 = ctxC.enter_context(nc.sbuf_tensor("xdT2", [128, 2, DC, 128], BF16))
        scrC = dict(xh=xh2, xq8=xq82, xdq=xdq2, xdT=xdT2, sq=sq_scrap)

        dv_x2 = [0] * TT
        for t in range(TT):
            p = t % 2
            w_ = [("dv", dv_x2[t - 2])] if t >= 2 else []
            hz = [] if t else [("pe", PE_OPROJ_END)] + [
                (f"gr{s}", gr_last[s]) for s in range(4)]
            ldy = b.dma("sync", f"ly{p}", y_buf[:, p, :],
                        y_spill[128 * t:128 * (t + 1), :],
                        waits=w_ + hz + [("vy", y_st[-1]), ("dv", dv_mt[NH - 1])])
            ldr = b.dma("sync", f"lr{p}", rs_sb[:, p, :],
                        rs_out[t // 4][128 * (t % 4):128 * (t % 4) + 128, :],
                        waits=w_ + [("cc", cc_rs[t // 4])])
            dv_x2[t] = b.op("vector", "dv", lambda e, t=t, p=p: e.tensor_tensor(
                out=x_str2[:, t, :], in0=y_buf[:, p, :],
                in1=rs_sb[:, p, :], op=ALU.add),
                waits=[(f"ly{p}", ldy), (f"lr{p}", ldr)])
        ln3 = [emit_ln_stats(x_str2, [t], {t: [("dv", dv_x2[t])]},
                             nm8, ssq8, tb8, sd8, rst8) for t in range(TT)]
        _, st_out = emit_ffn(x_str2, wdeqT2, W2_OC,
                             [[("dv", dv_x2[t])] for t in range(TT)], scrC,
                             nm8, rst8, 0, 0, store=out_sh, store_sem="vy",
                             ln_per_tile=ln3)
        b.wait("scalar", [("vy", st_out[-1])])
        b.flush("all")
        ctxC.close()

    return nc


_cache = {}


def kernel(L0_attn_x, L0_attn_q_w, L0_attn_k_w, L0_attn_v_w, L0_attn_o_w,
           L0_attn_gate_w, L0_attn_gate_b, L0_ffn_weight):
    x = np.ascontiguousarray(np.asarray(L0_attn_x, dtype=np.float32).reshape(8192, 2048))
    qw = np.asarray(L0_attn_q_w, dtype=np.float32)
    kw = np.asarray(L0_attn_k_w, dtype=np.float32)
    vw = np.asarray(L0_attn_v_w, dtype=np.float32)
    ow = np.asarray(L0_attn_o_w, dtype=np.float32)
    gw = np.asarray(L0_attn_gate_w, dtype=np.float32)
    gbv = np.ascontiguousarray(np.asarray(L0_attn_gate_b, dtype=np.float32).reshape(128, 1))
    fw = np.ascontiguousarray(np.asarray(L0_ffn_weight, dtype=np.float32))
    owt = np.ascontiguousarray(ow.T)
    gwt_host = np.ascontiguousarray(gw.T)

    if "nc" not in _cache:
        _cache["nc"] = build()
    nc = _cache["nc"]

    in_maps = []
    for c in range(NCORES):
        bb, r = divmod(c, 4)
        sl = slice(512 * r, 512 * (r + 1))
        in_maps.append({
            "x_sh": np.ascontiguousarray(x[4096 * bb + 1024 * r: 4096 * bb + 1024 * (r + 1)]),
            "wq_t": np.ascontiguousarray(qw[sl, :].T),
            "wk_t": np.ascontiguousarray(kw[sl, :].T),
            "wv_t": np.ascontiguousarray(vw[sl, :].T),
            "wo_t": np.ascontiguousarray(owt[sl, :]),
            "gwt": gwt_host,
            "gb": gbv,
            "wf": np.ascontiguousarray(fw[256 * c:256 * (c + 1)]),
        })
    res = run_bass_kernel_spmd(nc, in_maps, core_ids=list(range(NCORES)))
    out = np.concatenate([np.asarray(res.results[c]["out_sh"]) for c in range(NCORES)],
                         axis=0)
    return out.reshape(2, 4096, 2048).astype(np.float32)


# version 21
# speedup vs baseline: 1.0991x; 1.0485x over previous
import numpy as np
from contextlib import ExitStack

import concourse.bass as bass
import concourse.mybir as mybir
from concourse.bass_utils import run_bass_kernel_spmd

F32 = mybir.dt.float32
BF16 = mybir.dt.bfloat16
FP8 = mybir.dt.float8e4
AF = mybir.ActivationFunctionType
ALU = mybir.AluOpType
AX = mybir.AxisListType

NCORES = 8
TT = 8
D = 2048
DC = 16
S = 4096
NH = 4
QCW = 512
NQC = S // QCW
C_SHIFT = 20.0
SCALE = 0.08838834764831845
INV_D = 1.0 / 2048.0
EPS = 1e-5
GROUPS = [[0, 1, 2, 3], [4, 5, 6, 7]]


class Builder:
    def __init__(self, nc, paranoid=False):
        self.nc = nc
        self.paranoid = paranoid
        self.q = {k: [] for k in ("sync", "gpsimd", "vector", "scalar", "tensor")}
        self.sem = {}
        self.cnt = {}
        self.bank_state = [None] * 8
        self.risky = {}

    def add_sem(self, ctx, name):
        self.sem[name] = ctx.enter_context(self.nc.semaphore(name))
        self.cnt[name] = 0

    def flush(self, blockname):
        with self.nc.Block(blockname) as block:
            for eng, fns in self.q.items():
                if fns:
                    getattr(block, eng)(lambda e, fns=tuple(fns): [f(e) for f in fns])
        self.q = {k: [] for k in self.q}

    def op(self, eng, semname, fn, waits=(), inc=1, chain=False, risky=False):
        w = [(s, v) for (s, v) in waits if v > 0]
        if chain or (self.paranoid and eng == "vector"):
            prev = self.cnt[semname]
            if prev > 0:
                w.append((semname, prev))
        pend = self.risky.pop(eng, None)
        if pend is not None:
            w.append(pend)
        self.cnt[semname] += inc
        if risky:
            self.risky[eng] = (semname, self.cnt[semname])
        sems = self.sem

        def run(e, w=tuple(w), fn=fn, semname=semname, inc=inc):
            seen = {}
            for (s, v) in w:
                if seen.get(s, -1) < v:
                    e.wait_ge(sems[s], v)
                    seen[s] = v
            fn(e).then_inc(sems[semname], inc)
        self.q[eng].append(run)
        return self.cnt[semname]

    def wait(self, eng, waits):
        w = [(s, v) for (s, v) in waits if v > 0]
        sems = self.sem

        def run(e):
            for (s, v) in w:
                e.wait_ge(sems[s], v)
        self.q[eng].append(run)

    def dma(self, eng, semname, out, in_, waits=()):
        return self.op(eng, semname, lambda e: e.dma_start(out=out, in_=in_),
                       waits=waits, inc=16)

    def use_bank(self, k):
        st = self.bank_state[k]
        return [st] if st is not None else []

    def done_bank(self, k, semname, val):
        self.bank_state[k] = (semname, val)


def build(paranoid=False, taps=False):
    nc = bass.Bass("TRN2", target_bir_lowering=False)

    x_sh = nc.dram_tensor("x_sh", [1024, D], F32, kind="ExternalInput")
    wq_t = nc.dram_tensor("wq_t", [D, 512], F32, kind="ExternalInput")
    wk_t = nc.dram_tensor("wk_t", [D, 512], F32, kind="ExternalInput")
    wv_t = nc.dram_tensor("wv_t", [D, 512], F32, kind="ExternalInput")
    wo_t = nc.dram_tensor("wo_t", [512, D], F32, kind="ExternalInput")
    gwt = nc.dram_tensor("gwt", [128, 128], F32, kind="ExternalInput")
    gb = nc.dram_tensor("gb", [128, 1], F32, kind="ExternalInput")
    wf = nc.dram_tensor("wf", [256, D], F32, kind="ExternalInput")
    out_sh = nc.dram_tensor("out_sh", [1024, D], F32, kind="ExternalOutput")

    zag_in = [nc.dram_tensor(f"zag_in{i}", [D, QCW], BF16) for i in range(2)]
    zag_out = [nc.dram_tensor(f"zag_out{i}", [4 * D, QCW], BF16) for i in range(2)]
    rs_in = [nc.dram_tensor(f"rs_in{p}", [4 * 512, D], BF16) for p in range(2)]
    rs_out = [nc.dram_tensor(f"rs_out{p}", [512, D], BF16) for p in range(2)]
    y_spill = nc.dram_tensor("y_spill", [1024, D], F32)
    wag_in = nc.dram_tensor("wag_in", [256, D], BF16)
    wag_out = nc.dram_tensor("wag_out", [D, D], BF16, addr_space="Shared")
    wdeq_spill = nc.dram_tensor("wdeq_spill", [DC, 128, D], BF16)
    wq_sp = nc.dram_tensor("wq_sp", [D, 512], BF16)
    wk_sp = nc.dram_tensor("wk_sp", [D, 512], BF16)
    wv_sp = nc.dram_tensor("wv_sp", [D, 512], BF16)
    wo_sp = nc.dram_tensor("wo_sp", [512, D], BF16)
    if taps:
        dbg_am = nc.dram_tensor("dbg_am", [DC, 128, DC], F32, kind="ExternalOutput")
        dbg_r2 = nc.dram_tensor("dbg_r2", [DC, 128, DC], F32, kind="ExternalOutput")
        dbg_q8 = nc.dram_tensor("dbg_q8", [DC, 128, D], FP8, kind="ExternalOutput")
        dbg_dq = nc.dram_tensor("dbg_dq", [DC, 128, D], BF16, kind="ExternalOutput")
        dbg_y = nc.dram_tensor("dbg_y", [1024, D], F32, kind="ExternalOutput")
        dbg_wd = nc.dram_tensor("dbg_wd", [DC, 128, D], BF16, kind="ExternalOutput")
        dbg_zo = nc.dram_tensor("dbg_zo", [4 * D, 1024], BF16, kind="ExternalOutput")
        dbg_q = nc.dram_tensor("dbg_q", [128, NH * S], BF16, kind="ExternalOutput")
        dbg_k = nc.dram_tensor("dbg_k", [128, NH * S], BF16, kind="ExternalOutput")
        dbg_v = nc.dram_tensor("dbg_v", [128, 32 * 512], BF16, kind="ExternalOutput")
        dbg_m = nc.dram_tensor("dbg_m", [128, NH * S], BF16, kind="ExternalOutput")
        dbg_g = nc.dram_tensor("dbg_g", [128, 2 * S], BF16, kind="ExternalOutput")
        dbg_rs = nc.dram_tensor("dbg_rs", [128, D], BF16, kind="ExternalOutput")

    b = Builder(nc, paranoid=paranoid)
    with ExitStack() as ctx:
        for s in ("ld", "gs", "cc", "pe", "dv", "sc", "gp", "lx",
                  "lwf0", "lwf1", "lz0", "lz1", "lw30", "lw31", "lcs",
                  "lwo", "lwd", "ly0", "ly1", "lr0", "lr1", "gr0", "gr1",
                  "sy", "sw", "vy", "ts", "gz0", "gz1", "gw", "lwn0", "lwn1",
                  "lk0", "lk1", "lc2",
                  "lsp0", "lsp1", "lsp2", "lsp3", "lsp4", "lsp5", "lsp6",
                  "lsp7", "ssp0", "ssp1", "lw3", "lk3", "lv3", "gr2", "gr3",
                  "lwd1", "lwd2", "lwd3",
                  "lx0", "lx1"):
            b.add_sem(ctx, s)

        P = [ctx.enter_context(nc.psum_tensor(f"P{k}", [128, 1024], F32))
             for k in range(3)]
        pt = [ctx.enter_context(nc.psum_tensor(f"pt{k}", [128, 1024], BF16))
              for k in range(2)]

        class PV:
            def __init__(self, t, off):
                self.t = t
                self.off = off

            def __getitem__(self, idx):
                rows, cols = idx
                start = self.off + (cols.start or 0)
                stop = self.off + (cols.stop if cols.stop is not None else QCW)
                return self.t[rows, start:stop]

        ps = [PV(P[k // 2], QCW * (k % 2)) for k in range(6)]
        banks = ps + pt

        ident_f = ctx.enter_context(nc.sbuf_tensor("ident_f", [128, 128], F32))
        ident_b = ctx.enter_context(nc.sbuf_tensor("ident_b", [128, 128], BF16))
        mask_add = ctx.enter_context(nc.sbuf_tensor("mask_add", [128, 128], F32))
        ones_b = ctx.enter_context(nc.sbuf_tensor("ones_b", [128, 1], BF16))
        ones1f = ctx.enter_context(nc.sbuf_tensor("ones1f", [1, 128], F32))
        ones1b = ctx.enter_context(nc.sbuf_tensor("ones1b", [1, 128], BF16))
        negC = ctx.enter_context(nc.sbuf_tensor("negC", [128, 1], F32))
        epsb = ctx.enter_context(nc.sbuf_tensor("epsb", [128, 1], F32))
        gb_sb = ctx.enter_context(nc.sbuf_tensor("gb_sb", [128, 1], F32))
        gwt_f = ctx.enter_context(nc.sbuf_tensor("gwt_f", [128, 128], F32))
        gwt_b = ctx.enter_context(nc.sbuf_tensor("gwt_b", [128, 128], BF16))
        rsum = ctx.enter_context(nc.sbuf_tensor("rsum", [128, 2], F32))
        nmean = ctx.enter_context(nc.sbuf_tensor("nmean", [128, 2], F32))
        ssq = ctx.enter_context(nc.sbuf_tensor("ssq", [128, 2], F32))
        sdvv = ctx.enter_context(nc.sbuf_tensor("sdvv", [128, 2], F32))
        rstd = ctx.enter_context(nc.sbuf_tensor("rstd", [128, 2], F32))
        tbx = ctx.enter_context(nc.sbuf_tensor("tbx", [128, 2], F32))
        rsum8 = ctx.enter_context(nc.sbuf_tensor("rsum8", [128, 8], F32))
        nm8 = ctx.enter_context(nc.sbuf_tensor("nm8", [128, 8], F32))
        ssq8 = ctx.enter_context(nc.sbuf_tensor("ssq8", [128, 8], F32))
        tb8 = ctx.enter_context(nc.sbuf_tensor("tb8", [128, 8], F32))
        sd8 = ctx.enter_context(nc.sbuf_tensor("sd8", [128, 8], F32))
        rst8 = ctx.enter_context(nc.sbuf_tensor("rst8", [128, 8], F32))
        nm8b = ctx.enter_context(nc.sbuf_tensor("nm8b", [128, 8], F32))
        ssq8b = ctx.enter_context(nc.sbuf_tensor("ssq8b", [128, 8], F32))
        tb8b = ctx.enter_context(nc.sbuf_tensor("tb8b", [128, 8], F32))
        sd8b = ctx.enter_context(nc.sbuf_tensor("sd8b", [128, 8], F32))
        rst8b = ctx.enter_context(nc.sbuf_tensor("rst8b", [128, 8], F32))

        def emit_ln_stats(xs, tiles, x_waits, NM, SQ, TB, SD, RS):
            for t in tiles:
                b.op("vector", "dv", lambda e, t=t: e.tensor_reduce(
                    out=rsum8[:, t:t + 1], in_=xs[:, t, :], axis=AX.X,
                    op=ALU.add, negate=True), waits=x_waits[t], risky=True)
            sc_q = 0
            for t in tiles:
                sc_q = b.op("scalar", "sc", lambda e, t=t: e.activation(
                    out=sq_scrap[:, 0, :], in_=xs[:, t, :], func=AF.Square,
                    scale=1.0, accum_out=SQ[:, t:t + 1]),
                    waits=list(x_waits[t]), chain=True)
            lo, n = tiles[0], len(tiles)
            b.op("vector", "dv", lambda e, lo=lo, n=n: e.tensor_scalar_mul(
                out=NM[:, lo:lo + n], in0=rsum8[:, lo:lo + n], scalar1=INV_D),
                risky=True)
            b.op("vector", "dv", lambda e, lo=lo, n=n: e.tensor_tensor(
                out=TB[:, lo:lo + n], in0=NM[:, lo:lo + n],
                in1=NM[:, lo:lo + n], op=ALU.mult), risky=True)
            dv_tb = b.op("vector", "dv", lambda e, lo=lo, n=n: e.tensor_scalar(
                out=TB[:, lo:lo + n], in0=TB[:, lo:lo + n], scalar1=-1.0,
                scalar2=EPS, op0=ALU.mult, op1=ALU.add), risky=True)
            sc_sd = 0
            for t in tiles:
                sc_sd = b.op("scalar", "sc", lambda e, t=t: e.activation(
                    out=SD[:, t:t + 1], in_=SQ[:, t:t + 1], func=AF.Sqrt,
                    bias=TB[:, t:t + 1], scale=INV_D),
                    waits=([("dv", dv_tb)] if t == tiles[0] else []), chain=True)
            dv_rs = b.op("vector", "dv", lambda e, lo=lo, n=n: e.reciprocal(
                out=RS[:, lo:lo + n], in_=SD[:, lo:lo + n]),
                waits=[("sc", sc_sd)], risky=True)
            return sc_sd, dv_rs
        nmean2 = ctx.enter_context(nc.sbuf_tensor("nmean2", [128, 2], F32))
        ssq2 = ctx.enter_context(nc.sbuf_tensor("ssq2", [128, 2], F32))
        sdv2 = ctx.enter_context(nc.sbuf_tensor("sdv2", [128, 2], F32))
        rstd2 = ctx.enter_context(nc.sbuf_tensor("rstd2", [128, 2], F32))
        tbx2 = ctx.enter_context(nc.sbuf_tensor("tbx2", [128, 2], F32))
        amax = ctx.enter_context(nc.sbuf_tensor("amax", [128, 2, DC], F32))
        r2 = ctx.enter_context(nc.sbuf_tensor("r2", [128, 2, DC], F32))
        s2 = ctx.enter_context(nc.sbuf_tensor("s2", [128, 2, DC], F32))
        sq_scrap = ctx.enter_context(nc.sbuf_tensor("sq_scrap", [128, 1, D], BF16))

        for (tname, val) in ((ident_f, 0.0), (ident_b, 0.0), (mask_add, 0.0),
                             (ones_b, 1.0), (ones1f, 1.0), (ones1b, 1.0),
                             (negC, -C_SHIFT), (epsb, EPS)):
            b.op("gpsimd", "gp", lambda e, t=tname, v=val: e.memset(t[:], v), chain=True)
        b.op("gpsimd", "gp", lambda e: e.affine_select(
            out=ident_f[:], in_=ident_f[:], compare_op=ALU.not_equal,
            fill=1.0, base=0, pattern=[[-1, 128]], channel_multiplier=1), chain=True)
        b.op("gpsimd", "gp", lambda e: e.affine_select(
            out=ident_b[:], in_=ident_b[:], compare_op=ALU.not_equal,
            fill=1.0, base=0, pattern=[[-1, 128]], channel_multiplier=1), chain=True)
        gm = b.op("gpsimd", "gp", lambda e: e.affine_select(
            out=mask_add[:], in_=mask_add[:], compare_op=ALU.is_ge,
            fill=-1.0e30, base=0, pattern=[[1, 128]], channel_multiplier=-1), chain=True)
        b.dma("sync", "ld", gb_sb[:], gb[:, :])
        lgw = b.dma("sync", "ld", gwt_f[:], gwt[:, :])
        GWT_B = b.op("vector", "dv", lambda e: e.tensor_copy(out=gwt_b[:], in_=gwt_f[:]),
                     waits=[("ld", lgw)])

        ctxA = ExitStack()
        x_str = ctxA.enter_context(nc.sbuf_tensor("x_str", [128, TT, D], F32))
        wdeqT = ctxA.enter_context(nc.sbuf_tensor("wdeqT", [128, DC, D], BF16))
        xh = ctxA.enter_context(nc.sbuf_tensor("xh", [128, 2, D], F32))
        xq8 = ctxA.enter_context(nc.sbuf_tensor("xq8", [128, 1, D], FP8))
        xdq = ctxA.enter_context(nc.sbuf_tensor("xdq", [128, 2, D], BF16))
        xdT = ctxA.enter_context(nc.sbuf_tensor("xdT", [128, 2, DC, 128], BF16))
        scrA = dict(xh=xh, xq8=xq8, xdq=xdq, xdT=xdT, sq=sq_scrap)
        zbuf = ctxA.enter_context(nc.sbuf_tensor("zbuf", [128, 2, 4, 128], BF16))
        zln = ctxA.enter_context(nc.sbuf_tensor("zln", [128, 2, D], BF16))
        ctxA0 = ExitStack()
        wfn = ctxA0.enter_context(nc.sbuf_tensor("wfn", [128, 2, D], F32))
        wq8 = ctxA0.enter_context(nc.sbuf_tensor("wq8", [128, 1, D], FP8))
        wdq = ctxA0.enter_context(nc.sbuf_tensor("wdq", [128, 2, D], BF16))

        ld_x = []
        for t in range(TT):
            sem = "lx0" if t == 0 else ("lx1" if t == 1 else "lx")
            ld_x.append((sem, b.dma("sync", sem, x_str[:, t, :],
                                    x_sh[128 * t:128 * (t + 1), :])))
        ld_x = [(s, v if s != "lx" else b.cnt["lx"]) for (s, v) in ld_x]

        GP_INIT = gm
        pe_wtr = [[0] * 4 for _ in range(DC)]
        dv_wev_flat = [0] * DC
        gw_sp = 0
        for i in range(2):
            p = i % 2
            ldwf = b.dma("sync", f"lwf{p}", wfn[:, p, :], wf[128 * i:128 * (i + 1), :])
            b.op("vector", "dv", lambda e, p=p: e.tensor_reduce(
                out=amax[:, p, :],
                in_=wfn[:, p, :].rearrange("p (c k) -> p c k", k=128),
                axis=AX.X, op=ALU.max, apply_absolute_value=True),
                waits=[(f"lwf{p}", ldwf)], risky=True)
            b.op("vector", "dv", lambda e, p=p: e.reciprocal(
                out=r2[:, p, :], in_=amax[:, p, :]), risky=True)
            b.op("vector", "dv", lambda e, p=p: e.tensor_scalar_mul(
                out=r2[:, p, :], in0=r2[:, p, :], scalar1=224.0), risky=True)
            dv_r2w = b.op("vector", "dv", lambda e, p=p: e.tensor_scalar_mul(
                out=s2[:, p, :], in0=amax[:, p, :], scalar1=1.0 / 224.0), risky=True)
            for blk in range(DC):
                b.op("vector", "dv", lambda e, p=p, blk=blk: e.tensor_scalar(
                    out=wq8[:, 0, 128 * blk:128 * (blk + 1)],
                    in0=wfn[:, p, 128 * blk:128 * (blk + 1)],
                    scalar1=r2[:, p, blk:blk + 1], scalar2=None, op0=ALU.mult))
            dv_wdq = 0
            for blk in range(DC):
                dv_wdq = b.op("vector", "dv", lambda e, p=p, blk=blk: e.tensor_scalar(
                    out=wdq[:, p, 128 * blk:128 * (blk + 1)],
                    in0=wq8[:, 0, 128 * blk:128 * (blk + 1)],
                    scalar1=s2[:, p, blk:blk + 1], scalar2=None, op0=ALU.mult))
            gw_sp = b.dma("gpsimd", "gw", wag_in[128 * i:128 * (i + 1), :],
                          wdq[:, p, :], waits=[("dv", dv_wdq)])
        cc_w = b.op("gpsimd", "cc", lambda e: e.collective_compute(
            "AllGather", ALU.bypass, replica_groups=[list(range(8))],
            ins=[wag_in.ap().opt()], outs=[wag_out.ap().opt()]),
            waits=[("gw", gw_sp)], chain=True)
        for i in range(DC):
            p = i % 2
            ldn = b.dma("sync", f"lwn{p}", wdq[:, p, :],
                        wag_out[128 * i:128 * (i + 1), :],
                        waits=[("cc", cc_w)] + ([("pe", pe_wtr[i - 2][3])]
                                                if i >= 2 else [("gw", gw_sp)]))
            for k in range(4):
                bank = 6 + (k % 2)
                for j in range(4):
                    c = 4 * k + j
                    w_ = [(f"lwn{p}", ldn)] if (k == 0 and j == 0) else []
                    if j == 0:
                        w_ += b.use_bank(bank)
                    if (i == 0 and k == 0 and j == 0) or b.paranoid:
                        w_.append(("gp", GP_INIT))
                    pe_wtr[i][k] = b.op("tensor", "pe", lambda e, p=p, c=c, bank=bank, j=j: e.transpose(
                        banks[bank][:, 128 * j:128 * (j + 1)],
                        wdq[:, p, 128 * c:128 * (c + 1)], ident_b[:]),
                        waits=w_)
                dvev = b.op("vector", "dv", lambda e, i=i, k=k, bank=bank: e.tensor_copy(
                    out=wdeqT[:, 4 * k:4 * (k + 1), 128 * i:128 * (i + 1)],
                    in_=banks[bank][:, 0:512].rearrange("p (j k) -> p j k", k=128)),
                    waits=[("pe", pe_wtr[i][k])])
                b.done_bank(bank, "dv", dvev)
                dv_wev_flat[i] = dvev
        W_DONE = dvev

        def emit_ffn(xs, wT, w_ready, x_ready, scr, NM, RS, ln_sc, ln_dv,
                     store=None, store_sem="gs", hook=None, ln_per_tile=None,
                     pre_tile=None):
            xh = scr["xh"]; xq8 = scr["xq8"]; xdq = scr["xdq"]
            xdT = scr["xdT"]
            sc_sdv = [0] * TT
            sc_ssqv = [0] * TT
            sc_q8 = [0] * TT
            dv_xdq = [0] * TT
            dv_ev = [[0] * 4 for _ in range(TT)]
            dv_res = [[0] * 4 for _ in range(TT)]
            pe_tr = [[0] * 4 for _ in range(TT)]
            pe_mm = [[0] * 4 for _ in range(TT)]
            st_out = []
            for t in range(TT):
                if pre_tile is not None:
                    pre_tile(t)
                p = t % 2
                if ln_per_tile is not None:
                    lsc, ldv = ln_per_tile[t]
                else:
                    lsc, ldv = ln_sc, ln_dv
                b.op("vector", "dv", lambda e, t=t, p=p: e.tensor_scalar(
                    out=xh[:, p, :], in0=xs[:, t, :], scalar1=NM[:, t:t + 1],
                    scalar2=RS[:, t:t + 1], op0=ALU.add, op1=ALU.mult),
                    waits=[("dv", ldv), ("sc", lsc)] + list(x_ready[t]))
                b.op("vector", "dv", lambda e, p=p: e.tensor_reduce(
                    out=amax[:, p, :],
                    in_=xh[:, p, :].rearrange("p (c k) -> p c k", k=128),
                    axis=AX.X, op=ALU.max, apply_absolute_value=True), risky=True)
                b.op("vector", "dv", lambda e, p=p: e.reciprocal(
                    out=r2[:, p, :], in_=amax[:, p, :]), risky=True)
                b.op("vector", "dv", lambda e, p=p: e.tensor_scalar_mul(
                    out=r2[:, p, :], in0=r2[:, p, :], scalar1=224.0), risky=True)
                dv_r2 = b.op("vector", "dv", lambda e, p=p: e.tensor_scalar_mul(
                    out=s2[:, p, :], in0=amax[:, p, :], scalar1=1.0 / 224.0), risky=True)
                for blk in range(DC):
                    sc_q8[t] = b.op("vector", "dv", lambda e, p=p, blk=blk: e.tensor_scalar(
                        out=xq8[:, 0, 128 * blk:128 * (blk + 1)],
                        in0=xh[:, p, 128 * blk:128 * (blk + 1)],
                        scalar1=r2[:, p, blk:blk + 1], scalar2=None, op0=ALU.mult))
                for blk in range(DC):
                    w_ = [("pe", pe_tr[t - 2][3])] if (t >= 2 and blk == 0) else []
                    dv_xdq[t] = b.op("vector", "dv", lambda e, p=p, blk=blk: e.tensor_scalar(
                        out=xdq[:, p, 128 * blk:128 * (blk + 1)],
                        in0=xq8[:, 0, 128 * blk:128 * (blk + 1)],
                        scalar1=s2[:, p, blk:blk + 1], scalar2=None, op0=ALU.mult),
                        waits=w_)
                for k in range(4):
                    bank = 6 + (k % 2)
                    for j in range(4):
                        c = 4 * k + j
                        w_ = [("dv", dv_xdq[t])] if (k == 0 and j == 0) else []
                        if j == 0:
                            w_ += b.use_bank(bank)
                        if b.paranoid:
                            w_.append(("gp", gm))
                        pe_tr[t][k] = b.op("tensor", "pe", lambda e, p=p, c=c, bank=bank, j=j: e.transpose(
                            banks[bank][:, 128 * j:128 * (j + 1)],
                            xdq[:, p, 128 * c:128 * (c + 1)], ident_b[:]), waits=w_)
                    w_ev = [("pe", pe_tr[t][k])]
                    if t >= 2 and k == 0:
                        w_ev.append(("pe", pe_mm[t - 2][3]))
                    dv_ev[t][k] = b.op("vector", "dv", lambda e, k=k, p=p, bank=bank: e.tensor_copy(
                        out=xdT[:, p, 4 * k:4 * (k + 1), :],
                        in_=banks[bank][:, 0:512].rearrange("p (j k) -> p j k", k=128)),
                        waits=w_ev)
                    b.done_bank(bank, "dv", dv_ev[t][k])
                def emit_resid(tt):
                    for oc in range(4):
                        dv_res[tt][oc] = b.op("vector", "dv",
                                              lambda e, tt=tt, oc=oc: e.scalar_tensor_tensor(
                                                  out=xs[:, tt, 512 * oc:512 * (oc + 1)],
                                                  in0=ps[oc][:, 0:512], scalar=0.5,
                                                  in1=xs[:, tt, 512 * oc:512 * (oc + 1)],
                                                  op0=ALU.mult, op1=ALU.add),
                                              waits=[("pe", pe_mm[tt][oc])])
                        b.done_bank(oc, "dv", dv_res[tt][oc])
                    if store is not None:
                        eng = "scalar" if store_sem == "vy" else "gpsimd"
                        st_out.append(b.dma(eng, store_sem,
                                            store[128 * tt:128 * (tt + 1), :],
                                            xs[:, tt, :], waits=[("dv", dv_res[tt][3])]))
                if t >= 1:
                    emit_resid(t - 1)
                mm_first = True
                for oc in range(4):
                    for c in range(DC):
                        w_ = []
                        if c == 0:
                            w_ = [("dv", dv_ev[t][3]), w_ready[oc]] + b.use_bank(oc)
                        pe_mm[t][oc] = b.op("tensor", "pe", lambda e, p=p, c=c, oc=oc: e.matmul(
                            ps[oc][:, 0:512], xdT[:, p, c, :],
                            wT[:, c, 512 * oc:512 * (oc + 1)],
                            start=(c == 0), stop=(c == DC - 1)), waits=w_)
                if hook is not None and t >= 1:
                    hook(t - 1, dv_res[t - 1][3])
            emit_resid(TT - 1)
            if hook is not None:
                hook(TT - 1, dv_res[TT - 1][3])
            return dv_res, st_out

        W_OC = [("dv", 0)] * 4
        for oc in range(4):
            W_OC[oc] = ("dv", dv_wev_flat[4 * oc + 3])
        SW_SPILL = 0
        for c in range(DC):
            SW_SPILL = b.dma("gpsimd", "ts", wdeq_spill[c, :, :], wdeqT[:, c, :],
                             waits=[("dv", W_DONE)])
        st2 = {"sc_sd": [0] * TT, "pe_ztr": [0] * TT, "gs_zb": [[0] * 4 for _ in range(TT)],
               "cc_ag": [0, 0], "dv_z": [0] * TT}
        xh_a = scrA["xh"]

        def ln2_hook(t, dv_resv):
            if t != 3 and t != TT - 1:
                return
            half = 0 if t == 3 else 1
            tiles = [4 * half + i for i in range(4)]
            xw = {tt: [("dv", dv_resv)] for tt in tiles}
            sc_sd, dv_rs = emit_ln_stats(x_str, tiles, xw,
                                         nm8b, ssq8b, tb8b, sd8b, rst8b)
            for tt in tiles:
                p = tt % 2
                dv_z = b.op("vector", "dv", lambda e, tt=tt, p=p: e.tensor_scalar(
                    out=zln[:, p, :], in0=x_str[:, tt, :],
                    scalar1=nm8b[:, tt:tt + 1], scalar2=rst8b[:, tt:tt + 1],
                    op0=ALU.add, op1=ALU.mult),
                    waits=([("dv", dv_rs), ("sc", sc_sd)] if tt == tiles[0] else [])
                    + ([("pe", st2["pe_ztr"][tt - 2])] if tt >= 2 else []))
                st2["dv_z"][tt] = dv_z
                for k in range(4):
                    bank = 6 + (k % 2)
                    pe_tr = 0
                    for j in range(4):
                        c = 4 * k + j
                        w_ = [("dv", dv_z)] if (k == 0 and j == 0) else []
                        if j == 0:
                            w_ += b.use_bank(bank)
                        pe_tr = b.op("tensor", "pe", lambda e, p=p, c=c, bank=bank, j=j: e.transpose(
                            banks[bank][:, 128 * j:128 * (j + 1)],
                            zln[:, p, 128 * c:128 * (c + 1)], ident_b[:]), waits=w_)
                    st2["pe_ztr"][tt] = pe_tr
                    zpar = (4 * tt + k) % 2
                    w_ev = [("pe", pe_tr)]
                    if tt >= 1 or k >= 2:
                        pk = (4 * tt + k) - 2
                        w_ev.append((f"gz{zpar}", st2["gs_zb"][pk // 4][pk % 4]))
                    dv_ze = b.op("vector", "dv", lambda e, zpar=zpar, bank=bank: e.tensor_copy(
                        out=zbuf[:, zpar, :, :],
                        in_=banks[bank][:, 0:512].rearrange("p (j k) -> p j k", k=128)),
                        waits=w_ev)
                    b.done_bank(bank, "dv", dv_ze)
                    halfq, tq = divmod(tt, 4)
                    for j in range(4):
                        c = 4 * k + j
                        st2["gs_zb"][tt][k] = b.dma(
                            "gpsimd", f"gz{zpar}",
                            zag_in[halfq][128 * c:128 * (c + 1), 128 * tq:128 * (tq + 1)],
                            zbuf[:, zpar, j, :], waits=([("dv", dv_ze)] if j == 0 else []))
            st2["cc_ag"][half] = b.op("gpsimd", "cc", lambda e, half=half: e.collective_compute(
                "AllGather", ALU.bypass, replica_groups=GROUPS,
                ins=[zag_in[half].ap().opt()], outs=[zag_out[half].ap().opt()]),
                waits=[("gz0", b.cnt["gz0"]), ("gz1", b.cnt["gz1"])], chain=True)

        spill_jobs = []
        for (wsrc, wdst) in ((wq_t, wq_sp), (wk_t, wk_sp), (wv_t, wv_sp)):
            for c in range(DC):
                spill_jobs.append((wsrc[128 * c:128 * (c + 1), :],
                                   wdst[128 * c:128 * (c + 1), :]))
        for c in range(4):
            for u in range(4):
                spill_jobs.append(
                    (wo_t[128 * c:128 * (c + 1), 512 * u:512 * (u + 1)],
                     wo_sp[128 * c:128 * (c + 1), 512 * u:512 * (u + 1)]))
        SP = {"k": 0, "lms": {}, "cms": {}, "sms": {}}
        wspf = [wfn[:, s // 4, 512 * (s % 4):512 * (s % 4 + 1)] for s in range(8)]
        wspb = [wdq[:, 0, 512 * p:512 * (p + 1)] for p in range(2)]

        def spill_casts(k):
            for i in range(8 * k, min(8 * k + 8, len(spill_jobs))):
                _, dst = spill_jobs[i]
                s = i % 8
                wc = [(f"lsp{s}", SP["lms"][i])]
                if i >= 2:
                    wc.append((f"ssp{i % 2}", SP["sms"][i - 2]))
                if i == 0:
                    wc.append(("gw", gw_sp))
                cv = b.op("vector", "dv", lambda e, s=s, i=i: e.tensor_copy(
                    out=wspb[i % 2], in_=wspf[s]), waits=wc)
                SP["cms"][i] = cv
                SP["sms"][i] = b.dma("sync", f"ssp{i % 2}", dst,
                                     wspb[i % 2], waits=[("dv", cv)])

        def spill_loads(k):
            for i in range(8 * k, min(8 * k + 8, len(spill_jobs))):
                src, _ = spill_jobs[i]
                s = i % 8
                wl = [("dv", SP["cms"].get(i - 8, 0))]
                if i == 0:
                    wl.append(("dv", W_DONE))
                SP["lms"][i] = b.dma("scalar", f"lsp{s}", wspf[s], src,
                                     waits=wl)

        def spill_step():
            k = SP["k"]
            SP["k"] = k + 1
            spill_casts(k)
            spill_loads(k + 1)

        def hook_all(t, dvr):
            ln2_hook(t, dvr)
            spill_step()

        spill_loads(0)
        ln1_sc, ln1_dv = emit_ln_stats(x_str, list(range(TT)),
                                       [[ld_x[t]] for t in range(TT)],
                                       nm8, ssq8, tb8, sd8, rst8)
        dv_res1, y_st = emit_ffn(x_str, wdeqT, W_OC,
                                 [[ld_x[t]] for t in range(TT)], scrA,
                                 nm8, rst8, ln1_sc, ln1_dv,
                                 store=y_spill, store_sem="vy", hook=hook_all)
        SPW = [("ssp0", SP["sms"][len(spill_jobs) - 2]),
               ("ssp1", SP["sms"][len(spill_jobs) - 1])]
        cc_ag = st2["cc_ag"]
        GS_SPILL = b.cnt["gs"]
        FENCE2 = [(s, b.cnt[s]) for s in ("ts", "pe", "dv", "sc")]
        FENCE2V = FENCE2 + [(s, b.cnt[s]) for s in ("gz0", "gz1")]
        ctxA0.close()

        ctxA.close()
        ctxBm = ExitStack()
        m_t = ctxBm.enter_context(nc.sbuf_tensor("m_t", [128, NH, S], BF16, side="right"))
        ctxB = ExitStack()
        q_t = ctxB.enter_context(nc.sbuf_tensor("q_t", [128, NH, S], BF16))
        k_t = ctxB.enter_context(nc.sbuf_tensor("k_t", [128, NH, S], BF16))
        v_t = ctxB.enter_context(nc.sbuf_tensor("v_t", [128, 32, 512], BF16))
        ctxB3 = ExitStack()
        zt_q = ctxB3.enter_context(nc.sbuf_tensor("zt_q", [128, DC, 512], BF16))
        wq_b = ctxB3.enter_context(nc.sbuf_tensor("wq_b", [128, DC, 512], BF16))
        wk_b = ctxB3.enter_context(nc.sbuf_tensor("wk_b", [128, DC, 512], BF16))
        wv_b = ctxB3.enter_context(nc.sbuf_tensor("wv_b", [128, DC, 512], BF16))

        for (wsrc, wbbuf, semw) in ((wq_sp, wq_b, "lw3"), (wk_sp, wk_b, "lk3"),
                                    (wv_sp, wv_b, "lv3")):
            for c in range(DC):
                w_ = []
                if wbbuf is wq_b and c == 0:
                    w_ = list(FENCE2V) + SPW + [("vy", y_st[-1])]
                b.dma("scalar", semw, wbbuf[:, c, :],
                      wsrc[128 * c:128 * (c + 1), :], waits=w_)
        pe_zread = [0] * DC
        for hq in range(8):
            qq, half = divmod(hq, 2)
            ldz = 0
            for c in range(DC):
                w_ = [("pe", pe_zread[c])] if hq >= 1 else (
                    list(FENCE2) + SPW + [("vy", y_st[-1])])
                ldz = b.dma("sync", f"lz{hq % 2}", zt_q[:, c, :],
                            zag_out[half][D * qq + 128 * c:D * qq + 128 * (c + 1), :],
                            waits=w_ + [("cc", cc_ag[half])])
            zq_ready = (f"lz{hq % 2}", ldz)
            for (wb, dst) in ((wq_b, q_t), (wk_b, k_t)):
                pe_h = [0] * NH
                for c in range(DC):
                    for h in range(NH):
                        w_ = []
                        if h == 0 and hq == 0 and c == 0:
                            w_.append(("lw3" if wb is wq_b else "lk3", 16 * DC))
                        if c == 0:
                            w_ = w_ + [zq_ready] + b.use_bank(h)
                        pe_h[h] = b.op("tensor", "pe",
                                       lambda e, c=c, h=h, wb=wb: e.matmul(
                                           ps[h][:, 0:512],
                                           wb[:, c, 128 * h:128 * (h + 1)],
                                           zt_q[:, c, :],
                                           start=(c == 0), stop=(c == DC - 1)),
                                       waits=w_)
                for h in range(NH):
                    dvx = b.op("vector", "dv",
                               lambda e, h=h, hq=hq, dst=dst: e.tensor_copy(
                                   out=dst[:, h, 512 * hq:512 * (hq + 1)],
                                   in_=ps[h][:, 0:512]), waits=[("pe", pe_h[h])])
                    b.done_bank(h, "dv", dvx)
            pe_g = [0] * 4
            for c in range(DC):
                for u in range(4):
                    w_ = []
                    if c == 0:
                        w_ = [zq_ready, ("lv3", 16 * DC)] + b.use_bank(u)
                    pe_g[u] = b.op("tensor", "pe",
                                   lambda e, c=c, u=u: e.matmul(
                                       ps[u][:, 0:512],
                                       zt_q[:, c, 128 * u:128 * (u + 1)],
                                       wv_b[:, c, 0:512],
                                       start=(c == 0), stop=(c == DC - 1)), waits=w_)
                    pe_zread[c] = pe_g[u]
            for u in range(4):
                dvx = b.op("vector", "dv", lambda e, hq=hq, u=u: e.tensor_copy(
                    out=v_t[:, 4 * hq + u, :], in_=ps[u][:, 0:512]),
                    waits=[("pe", pe_g[u])])
                b.done_bank(u, "dv", dvx)
        PH3_DV = b.cnt["dv"]
        ctxB3.close()

        ctxB4 = ExitStack()
        gate_h = ctxB4.enter_context(nc.sbuf_tensor("gate_h", [128, 2, S], BF16))
        av_sb = ctxB4.enter_context(nc.sbuf_tensor("av_sb", [128, 1, S], BF16))
        p_t = ctxB4.enter_context(nc.sbuf_tensor("p_t", [128, 3, 1024], BF16))
        cs_rows = [ctxB4.enter_context(nc.sbuf_tensor(f"cs_row{i}", [1, S], BF16))
                   for i in range(2)]
        cs128 = ctxB4.enter_context(nc.sbuf_tensor("cs128", [128, 32], BF16))
        rc128 = ctxB4.enter_context(nc.sbuf_tensor("rc128", [128, 32], BF16))
        recip_b = ctxB4.enter_context(nc.sbuf_tensor("recip_b", [128, S], BF16))

        dv_mt = [0] * NH
        pt_pe = [0, 0, 0]
        tail_args = []

        def emit_tail(h, ph, dvc, scg):
            ld1 = b.dma("sync", "lcs", cs128[:, :], cs_rows[ph][0:1, :],
                        waits=[("dv", dvc)])
            def _recip(e):
                with nc.allow_low_precision(reason="bf16 norm factor, matches recip_b precision"):
                    return e.reciprocal(out=rc128[:, :], in_=cs128[:, :])
            dvr = b.op("vector", "dv", _recip,
                waits=[("lcs", ld1)], risky=True)
            ld2 = b.dma("sync", "lc2", cs_rows[ph][0:1, :], rc128[:, :],
                        waits=[("dv", dvr)])
            dv_rb = 0
            for qc in range(NQC):
                bank = qc % 4
                w_ = b.use_bank(bank) + [("lc2", ld2)]
                peb = b.op("tensor", "pe", lambda e, qc=qc, bank=bank, ph=ph: e.matmul(
                    ps[bank][:, 0:512], ones1b[:],
                    cs_rows[ph][0:1, 512 * qc:512 * (qc + 1)],
                    start=True, stop=True), waits=w_)
                dv_rb = b.op("vector", "dv", lambda e, qc=qc, bank=bank: e.tensor_copy(
                    out=recip_b[:, 512 * qc:512 * (qc + 1)], in_=ps[bank][:, 0:512]),
                    waits=[("pe", peb)])
                b.done_bank(bank, "dv", dv_rb)
            dvt = b.op("vector", "dv", lambda e, ph=ph: e.tensor_tensor(
                out=recip_b[:, :], in0=av_sb[:, 0, :], in1=recip_b[:, :], op=ALU.mult),
                waits=[("dv", dv_rb), ("sc", scg)])
            dv_mt[h] = b.op("vector", "dv", lambda e, h=h, ph=ph: e.tensor_tensor(
                out=m_t[:, h, :], in0=recip_b[:, :], in1=gate_h[:, ph, :], op=ALU.mult),
                waits=[("dv", dvt)])
        for h in range(NH):
            ph = h % 2
            scg = 0
            for qc in range(NQC):
                bank = qc % 4
                w_ = b.use_bank(bank)
                if qc == 0:
                    w_ += [("gp", gm), ("dv", GWT_B)]
                    if h == 0:
                        w_.append(("dv", PH3_DV))
                    if h >= 2:
                        w_.append(("dv", dv_mt[h - 2]))
                peg = b.op("tensor", "pe", lambda e, h=h, qc=qc, bank=bank: e.matmul(
                    ps[bank][:, 0:512], gwt_b[:], q_t[:, h, 512 * qc:512 * (qc + 1)],
                    start=True, stop=True), waits=w_)
                scg = b.op("scalar", "sc", lambda e, ph=ph, qc=qc, bank=bank: e.activation(
                    out=gate_h[:, ph, 512 * qc:512 * (qc + 1)], in_=ps[bank][:, 0:512],
                    func=AF.Sigmoid, bias=gb_sb[:, 0:1], scale=1.0),
                    waits=[("pe", peg)] + ([("ld", 32), ("gp", gm)]
                                           if (h == 0 and qc == 0) else []),
                    chain=True)
                b.done_bank(bank, "sc", scg)
            if tail_args:
                emit_tail(*tail_args.pop(0))
            dva = 0
            dvc = 0
            for qc in range(NQC):
                klim = 4 * qc + 4
                npair = klim // 2
                pend = []

                def flush_avcs(h=h):
                    pr, slot, pslot, sc_e = pend.pop(0)
                    ki0, ki1 = 2 * pr, 2 * pr + 1
                    klim_ = pend_klim[0]
                    pe_av = pe_cs = 0
                    for u, ki in ((0, ki0), (1, ki1)):
                        w_ = [("sc", sc_e)]
                        if ki == 0:
                            w_ += b.use_bank(4)
                        pe_av = b.op("tensor", "pe",
                                     lambda e, h=h, ki=ki, pslot=pslot, u=u, klim_=klim_: e.matmul(
                                         ps[4][:, 0:512],
                                         v_t[:, ki, 128 * h:128 * (h + 1)],
                                         p_t[:, pslot, 512 * u:512 * (u + 1)],
                                         start=(ki == 0), stop=(ki == klim_ - 1)),
                                     waits=w_)
                    for u, ki in ((0, ki0), (1, ki1)):
                        w_ = [("sc", sc_e)]
                        if ki == 0:
                            w_ += b.use_bank(5)
                        pe_cs = b.op("tensor", "pe",
                                     lambda e, ki=ki, pslot=pslot, u=u, klim_=klim_: e.matmul(
                                         ps[5][0:1, 0:512], ones_b[:],
                                         p_t[:, pslot, 512 * u:512 * (u + 1)],
                                         start=(ki == 0), stop=(ki == klim_ - 1)),
                                     waits=w_)
                    pt_pe[pslot] = pe_cs
                    return pe_av, pe_cs

                pend_klim = [klim]
                pe_av_last = pe_cs_last = 0
                for pr in range(npair):
                    ki0, ki1 = 2 * pr, 2 * pr + 1
                    off0 = max(0, 128 * ki0 - 512 * qc)
                    off1 = max(0, 128 * ki1 - 512 * qc)
                    slot = pr % 2
                    pslot = pr % 3
                    wS = b.use_bank(2 * slot) + b.use_bank(2 * slot + 1)
                    pes = b.op("tensor", "pe",
                               lambda e, h=h, qc=qc, ki0=ki0, off0=off0, slot=slot: e.matmul(
                                   P[slot][:, off0:512],
                                   k_t[:, h, 128 * ki0:128 * (ki0 + 1)],
                                   q_t[:, h, 512 * qc + off0:512 * (qc + 1)],
                                   start=True, stop=True), waits=wS)
                    pes = b.op("tensor", "pe",
                               lambda e, h=h, qc=qc, ki1=ki1, off1=off1, slot=slot: e.matmul(
                                   P[slot][:, 512 + off1:1024],
                                   k_t[:, h, 128 * ki1:128 * (ki1 + 1)],
                                   q_t[:, h, 512 * qc + off1:512 * (qc + 1)],
                                   start=True, stop=True), waits=wS)
                    dv_ms = 0
                    if ki1 >= 4 * qc:
                        wm = [("pe", pes)]
                        if h == 0 and qc == 0 and pr == 0:
                            wm.append(("gp", gm))
                        if off0 or ki0 >= 4 * qc:
                            dv_ms = b.op("vector", "dv",
                                         lambda e, slot=slot, off0=off0: e.tensor_tensor(
                                             out=P[slot][:, off0:off0 + 128],
                                             in0=P[slot][:, off0:off0 + 128],
                                             in1=mask_add[:], op=ALU.add), waits=wm)
                            wm = []
                        dv_ms = b.op("vector", "dv",
                                     lambda e, slot=slot, off1=off1: e.tensor_tensor(
                                         out=P[slot][:, 512 + off1:512 + off1 + 128],
                                         in0=P[slot][:, 512 + off1:512 + off1 + 128],
                                         in1=mask_add[:], op=ALU.add), waits=wm)
                        if off0:
                            dv_ms = b.op("vector", "dv", lambda e, slot=slot, off0=off0: e.memset(
                                P[slot][:, 0:off0], -1.0e30))
                        if off1:
                            dv_ms = b.op("vector", "dv", lambda e, slot=slot, off1=off1: e.memset(
                                P[slot][:, 512:512 + off1], -1.0e30))
                    w_e = [("pe", pes), ("pe", pt_pe[pslot]), ("dv", dv_ms)]
                    sc_e = b.op("scalar", "sc",
                                lambda e, slot=slot, pslot=pslot: e.activation(
                                    out=p_t[:, pslot, :], in_=P[slot][:, 0:1024],
                                    func=AF.Exp, bias=negC[:, 0:1], scale=SCALE),
                                chain=True, waits=w_e)
                    b.done_bank(2 * slot, "sc", sc_e)
                    b.done_bank(2 * slot + 1, "sc", sc_e)
                    pend.append((pr, slot, pslot, sc_e))
                    if len(pend) > 2:
                        pe_av_last, pe_cs_last = flush_avcs()
                while pend:
                    pe_av_last, pe_cs_last = flush_avcs()
                dva = b.op("vector", "dv", lambda e, ph=ph, qc=qc: e.tensor_copy(
                    out=av_sb[:, 0, 512 * qc:512 * (qc + 1)], in_=ps[4][:, 0:512]),
                    waits=[("pe", pe_av_last)])
                b.done_bank(4, "dv", dva)
                dvc = b.op("vector", "dv", lambda e, qc=qc, ph=ph: e.tensor_copy(
                    out=cs_rows[ph][0:1, 512 * qc:512 * (qc + 1)],
                    in_=ps[5][0:1, 0:512]),
                    waits=[("pe", pe_cs_last)])
                b.done_bank(5, "dv", dvc)
            tail_args.append((h, ph, dvc, scg))
        while tail_args:
            emit_tail(*tail_args.pop(0))
        ctxB4.close()
        ctxB.close()

        ctxC = ExitStack()
        x_str2 = ctxC.enter_context(nc.sbuf_tensor("x_str2", [128, TT, D], F32))
        wdeqT2 = ctxC.enter_context(nc.sbuf_tensor("wdeqT2", [128, DC, D], BF16))
        ctxC0 = ExitStack()
        wo_b = ctxC0.enter_context(nc.sbuf_tensor("wo_b", [128, 4, D], BF16))
        rs_buf = ctxC0.enter_context(nc.sbuf_tensor("rs_buf", [128, 4, 2048], BF16))

        for c in range(4):
            b.dma("sync", "lwo", wo_b[:, c, :], wo_sp[128 * c:128 * (c + 1), :],
                  waits=(([("dv", dv_mt[NH - 1])] + SPW) if c == 0 else []))
        cc_rs = [0, 0]
        gr_last = [0, 0, 0, 0]
        nqt = 0
        for j in range(TT):
            for g in range(4):
                qt = 8 * g + j
                slot = nqt % 4
                dvo = 0
                for half in range(2):
                    peo = 0
                    for e2p in (2 * half, 2 * half + 1):
                        view = e2p
                        for c in range(4):
                            w_ = []
                            if c == 0:
                                w_ = b.use_bank(view)
                                if nqt == 0 and e2p == 0:
                                    w_ += [("lwo", 64), ("dv", dv_mt[NH - 1]),
                                           ("dv", dv_mt[NH - 2])]
                            peo = b.op("tensor", "pe",
                                       lambda e, e2p=e2p, qt=qt, c=c, view=view: e.matmul(
                                           ps[view][:, 0:512],
                                           m_t[:, c, 128 * qt:128 * (qt + 1)],
                                           wo_b[:, c, 512 * e2p:512 * (e2p + 1)],
                                           start=(c == 0), stop=(c == 3)), waits=w_)
                    Pt = P[half]
                    dvo = b.op("vector", "dv", lambda e, Pt=Pt, slot=slot, half=half: e.tensor_copy(
                        out=rs_buf[:, slot, 1024 * half:1024 * (half + 1)],
                        in_=Pt[:, 0:1024]),
                        waits=[("pe", peo), (f"gr{slot}", gr_last[slot])])
                    b.done_bank(2 * half, "dv", dvo)
                    b.done_bank(2 * half + 1, "dv", dvo)
                hv = j // 4
                gr_last[slot] = b.dma("gpsimd" if slot % 2 == 0 else "scalar",
                                      f"gr{slot}",
                                      rs_in[hv][512 * g + 128 * (j % 4):
                                                512 * g + 128 * (j % 4) + 128, :],
                                      rs_buf[:, slot, :], waits=[("dv", dvo)])
                nqt += 1
            if j % 4 == 3:
                cc_rs[j // 4] = b.op("gpsimd", "cc",
                                     lambda e, hv=j // 4: e.collective_compute(
                                         "ReduceScatter", ALU.add,
                                         replica_groups=GROUPS,
                                         ins=[rs_in[hv].ap().opt()],
                                         outs=[rs_out[hv].ap().opt()]),
                                     waits=[(f"gr{s}", gr_last[s])
                                            for s in range(4)], chain=True)
        PE_OPROJ_END = b.cnt["pe"]
        W2_OC = []
        for oc in range(4):
            sem = "lwd" if oc == 0 else f"lwd{oc}"
            for c in range(DC):
                ld_wd = b.dma("sync", sem, wdeqT2[:, c, 512 * oc:512 * (oc + 1)],
                              wdeq_spill[c, :, 512 * oc:512 * (oc + 1)],
                              waits=([("ts", SW_SPILL), ("dv", dv_mt[NH - 1])]
                                     if (oc == 0 and c == 0) else []))
            W2_OC.append((sem, ld_wd))
        if taps:
            gt = b.dma("gpsimd", "gs", dbg_y[:, :], y_spill[:, :],
                       waits=[("gs", GS_SPILL)])
            gt = b.dma("gpsimd", "gs", dbg_wd[:, :, :], wdeq_spill[:, :, :])
            gt = b.dma("gpsimd", "gs", dbg_zo[:, 0:512], zag_out[0][:, :],
                       waits=[("cc", cc_ag[1])])
            gt = b.dma("gpsimd", "gs", dbg_zo[:, 512:1024], zag_out[1][:, :])
            gt = b.dma("gpsimd", "gs", dbg_rs[:, :], rs_out[0][0:128, :],
                       waits=[("cc", cc_rs[1])])
            b.wait("gpsimd", [("gs", gt)])
        ctxC0.close()
        ctxBm.close()
        y_buf = ctxC.enter_context(nc.sbuf_tensor("y_buf", [128, 2, D], F32))
        rs_sb = ctxC.enter_context(nc.sbuf_tensor("rs_sb", [128, 2, D], BF16))
        xh2 = ctxC.enter_context(nc.sbuf_tensor("xh2", [128, 2, D], F32))
        xq82 = ctxC.enter_context(nc.sbuf_tensor("xq82", [128, 1, D], FP8))
        xdq2 = ctxC.enter_context(nc.sbuf_tensor("xdq2", [128, 2, D], BF16))
        xdT2 = ctxC.enter_context(nc.sbuf_tensor("xdT2", [128, 2, DC, 128], BF16))
        scrC = dict(xh=xh2, xq8=xq82, xdq=xdq2, xdT=xdT2, sq=sq_scrap)

        dv_x2 = [0] * TT
        ln3 = [None] * TT
        xr2 = [None] * TT
        ld_yr = {}

        def emit_x2_loads(t):
            if t >= TT:
                return
            p = t % 2
            w_ = [("dv", dv_x2[t - 2])] if t >= 2 else []
            hz = [] if t else [("pe", PE_OPROJ_END)] + [
                (f"gr{s}", gr_last[s]) for s in range(4)]
            ldy = b.dma("sync", f"ly{p}", y_buf[:, p, :],
                        y_spill[128 * t:128 * (t + 1), :],
                        waits=w_ + hz + [("vy", y_st[-1]), ("dv", dv_mt[NH - 1])])
            ldr = b.dma("sync", f"lr{p}", rs_sb[:, p, :],
                        rs_out[t // 4][128 * (t % 4):128 * (t % 4) + 128, :],
                        waits=w_ + [("cc", cc_rs[t // 4])])
            ld_yr[t] = (ldy, ldr)

        def pre_tile2(t):
            emit_x2_loads(t + 1)
            p = t % 2
            ldy, ldr = ld_yr[t]
            dv_x2[t] = b.op("vector", "dv", lambda e, t=t, p=p: e.tensor_tensor(
                out=x_str2[:, t, :], in0=y_buf[:, p, :],
                in1=rs_sb[:, p, :], op=ALU.add),
                waits=[(f"ly{p}", ldy), (f"lr{p}", ldr)])
            ln3[t] = emit_ln_stats(x_str2, [t], {t: [("dv", dv_x2[t])]},
                                   nm8, ssq8, tb8, sd8, rst8)
            xr2[t] = [("dv", dv_x2[t])]

        emit_x2_loads(0)
        _, st_out = emit_ffn(x_str2, wdeqT2, W2_OC, xr2, scrC,
                             nm8, rst8, 0, 0, store=out_sh, store_sem="vy",
                             ln_per_tile=ln3, pre_tile=pre_tile2)
        b.wait("scalar", [("vy", st_out[-1])])
        b.flush("all")
        ctxC.close()

    return nc


_cache = {}


def kernel(L0_attn_x, L0_attn_q_w, L0_attn_k_w, L0_attn_v_w, L0_attn_o_w,
           L0_attn_gate_w, L0_attn_gate_b, L0_ffn_weight):
    x = np.ascontiguousarray(np.asarray(L0_attn_x, dtype=np.float32).reshape(8192, 2048))
    qw = np.asarray(L0_attn_q_w, dtype=np.float32)
    kw = np.asarray(L0_attn_k_w, dtype=np.float32)
    vw = np.asarray(L0_attn_v_w, dtype=np.float32)
    ow = np.asarray(L0_attn_o_w, dtype=np.float32)
    gw = np.asarray(L0_attn_gate_w, dtype=np.float32)
    gbv = np.ascontiguousarray(np.asarray(L0_attn_gate_b, dtype=np.float32).reshape(128, 1))
    fw = np.ascontiguousarray(np.asarray(L0_ffn_weight, dtype=np.float32))
    owt = np.ascontiguousarray(ow.T)
    gwt_host = np.ascontiguousarray(gw.T)

    if "nc" not in _cache:
        _cache["nc"] = build()
    nc = _cache["nc"]

    in_maps = []
    for c in range(NCORES):
        bb, r = divmod(c, 4)
        sl = slice(512 * r, 512 * (r + 1))
        in_maps.append({
            "x_sh": np.ascontiguousarray(x[4096 * bb + 1024 * r: 4096 * bb + 1024 * (r + 1)]),
            "wq_t": np.ascontiguousarray(qw[sl, :].T),
            "wk_t": np.ascontiguousarray(kw[sl, :].T),
            "wv_t": np.ascontiguousarray(vw[sl, :].T),
            "wo_t": np.ascontiguousarray(owt[sl, :]),
            "gwt": gwt_host,
            "gb": gbv,
            "wf": np.ascontiguousarray(fw[256 * c:256 * (c + 1)]),
        })
    res = run_bass_kernel_spmd(nc, in_maps, core_ids=list(range(NCORES)))
    out = np.concatenate([np.asarray(res.results[c]["out_sh"]) for c in range(NCORES)],
                         axis=0)
    return out.reshape(2, 4096, 2048).astype(np.float32)


# version 22 (final)
# speedup vs baseline: 1.1384x; 1.0357x over previous
import numpy as np
from contextlib import ExitStack

import concourse.bass as bass
import concourse.mybir as mybir
from concourse.bass_utils import run_bass_kernel_spmd

F32 = mybir.dt.float32
BF16 = mybir.dt.bfloat16
FP8 = mybir.dt.float8e4
AF = mybir.ActivationFunctionType
ALU = mybir.AluOpType
AX = mybir.AxisListType

NCORES = 8
TT = 8
D = 2048
DC = 16
S = 4096
NH = 4
QCW = 512
NQC = S // QCW
C_SHIFT = 20.0
SCALE = 0.08838834764831845
INV_D = 1.0 / 2048.0
EPS = 1e-5
GROUPS = [[0, 1, 2, 3], [4, 5, 6, 7]]


class Builder:
    def __init__(self, nc, paranoid=False):
        self.nc = nc
        self.paranoid = paranoid
        self.q = {k: [] for k in ("sync", "gpsimd", "vector", "scalar", "tensor")}
        self.sem = {}
        self.cnt = {}
        self.bank_state = [None] * 8
        self.risky = {}

    def add_sem(self, ctx, name):
        self.sem[name] = ctx.enter_context(self.nc.semaphore(name))
        self.cnt[name] = 0

    def flush(self, blockname):
        with self.nc.Block(blockname) as block:
            for eng, fns in self.q.items():
                if fns:
                    getattr(block, eng)(lambda e, fns=tuple(fns): [f(e) for f in fns])
        self.q = {k: [] for k in self.q}

    def op(self, eng, semname, fn, waits=(), inc=1, chain=False, risky=False):
        w = [(s, v) for (s, v) in waits if v > 0]
        if chain or (self.paranoid and eng == "vector"):
            prev = self.cnt[semname]
            if prev > 0:
                w.append((semname, prev))
        pend = self.risky.pop(eng, None)
        if pend is not None:
            w.append(pend)
        self.cnt[semname] += inc
        if risky:
            self.risky[eng] = (semname, self.cnt[semname])
        sems = self.sem

        def run(e, w=tuple(w), fn=fn, semname=semname, inc=inc):
            seen = {}
            for (s, v) in w:
                if seen.get(s, -1) < v:
                    e.wait_ge(sems[s], v)
                    seen[s] = v
            fn(e).then_inc(sems[semname], inc)
        self.q[eng].append(run)
        return self.cnt[semname]

    def wait(self, eng, waits):
        w = [(s, v) for (s, v) in waits if v > 0]
        sems = self.sem

        def run(e):
            for (s, v) in w:
                e.wait_ge(sems[s], v)
        self.q[eng].append(run)

    def dma(self, eng, semname, out, in_, waits=()):
        return self.op(eng, semname, lambda e: e.dma_start(out=out, in_=in_),
                       waits=waits, inc=16)

    def use_bank(self, k):
        st = self.bank_state[k]
        return [st] if st is not None else []

    def done_bank(self, k, semname, val):
        self.bank_state[k] = (semname, val)


def build(paranoid=False, taps=False):
    nc = bass.Bass("TRN2", target_bir_lowering=False)

    x_sh = nc.dram_tensor("x_sh", [1024, D], F32, kind="ExternalInput")
    wq_t = nc.dram_tensor("wq_t", [D, 512], F32, kind="ExternalInput")
    wk_t = nc.dram_tensor("wk_t", [D, 512], F32, kind="ExternalInput")
    wv_t = nc.dram_tensor("wv_t", [D, 512], F32, kind="ExternalInput")
    wo_t = nc.dram_tensor("wo_t", [512, D], F32, kind="ExternalInput")
    gwt = nc.dram_tensor("gwt", [128, 128], F32, kind="ExternalInput")
    gb = nc.dram_tensor("gb", [128, 1], F32, kind="ExternalInput")
    wf = nc.dram_tensor("wf", [256, D], F32, kind="ExternalInput")
    out_sh = nc.dram_tensor("out_sh", [1024, D], F32, kind="ExternalOutput")

    zag_in = [nc.dram_tensor(f"zag_in{i}", [D, QCW], BF16) for i in range(2)]
    zag_out = [nc.dram_tensor(f"zag_out{i}", [4 * D, QCW], BF16) for i in range(2)]
    rs_in = [nc.dram_tensor(f"rs_in{p}", [4 * 512, D], BF16) for p in range(2)]
    rs_out = [nc.dram_tensor(f"rs_out{p}", [512, D], BF16) for p in range(2)]
    y_spill = nc.dram_tensor("y_spill", [1024, D], F32)
    wag_in = nc.dram_tensor("wag_in", [256, D], BF16)
    wag_out = nc.dram_tensor("wag_out", [D, D], BF16, addr_space="Shared")
    wdeq_spill = nc.dram_tensor("wdeq_spill", [DC, 128, D], BF16)
    wq_sp = nc.dram_tensor("wq_sp", [D, 512], BF16)
    wk_sp = nc.dram_tensor("wk_sp", [D, 512], BF16)
    wv_sp = nc.dram_tensor("wv_sp", [D, 512], BF16)
    wo_sp = nc.dram_tensor("wo_sp", [512, D], BF16)
    if taps:
        dbg_am = nc.dram_tensor("dbg_am", [DC, 128, DC], F32, kind="ExternalOutput")
        dbg_r2 = nc.dram_tensor("dbg_r2", [DC, 128, DC], F32, kind="ExternalOutput")
        dbg_q8 = nc.dram_tensor("dbg_q8", [DC, 128, D], FP8, kind="ExternalOutput")
        dbg_dq = nc.dram_tensor("dbg_dq", [DC, 128, D], BF16, kind="ExternalOutput")
        dbg_y = nc.dram_tensor("dbg_y", [1024, D], F32, kind="ExternalOutput")
        dbg_wd = nc.dram_tensor("dbg_wd", [DC, 128, D], BF16, kind="ExternalOutput")
        dbg_zo = nc.dram_tensor("dbg_zo", [4 * D, 1024], BF16, kind="ExternalOutput")
        dbg_q = nc.dram_tensor("dbg_q", [128, NH * S], BF16, kind="ExternalOutput")
        dbg_k = nc.dram_tensor("dbg_k", [128, NH * S], BF16, kind="ExternalOutput")
        dbg_v = nc.dram_tensor("dbg_v", [128, 32 * 512], BF16, kind="ExternalOutput")
        dbg_m = nc.dram_tensor("dbg_m", [128, NH * S], BF16, kind="ExternalOutput")
        dbg_g = nc.dram_tensor("dbg_g", [128, 2 * S], BF16, kind="ExternalOutput")
        dbg_rs = nc.dram_tensor("dbg_rs", [128, D], BF16, kind="ExternalOutput")

    b = Builder(nc, paranoid=paranoid)
    with ExitStack() as ctx:
        for s in ("ld", "gs", "cc", "pe", "dv", "sc", "gp", "lx",
                  "lwf0", "lwf1", "lz0", "lz1", "lw30", "lw31", "lcs",
                  "lwo", "lwd", "ly0", "ly1", "lr0", "lr1", "gr0", "gr1",
                  "sy", "sw", "vy", "ts", "gz0", "gz1", "gw", "lwn0", "lwn1",
                  "lk0", "lk1", "lc2",
                  "lsp0", "lsp1", "lsp2", "lsp3", "lsp4", "lsp5", "lsp6",
                  "lsp7", "ssp0", "ssp1", "lw3", "lk3", "lv3", "gr2", "gr3",
                  "lwd1", "lwd2", "lwd3",
                  "lx0", "lx1"):
            b.add_sem(ctx, s)

        P = [ctx.enter_context(nc.psum_tensor(f"P{k}", [128, 1024], F32))
             for k in range(3)]
        pt = [ctx.enter_context(nc.psum_tensor(f"pt{k}", [128, 1024], BF16))
              for k in range(2)]

        class PV:
            def __init__(self, t, off):
                self.t = t
                self.off = off

            def __getitem__(self, idx):
                rows, cols = idx
                start = self.off + (cols.start or 0)
                stop = self.off + (cols.stop if cols.stop is not None else QCW)
                return self.t[rows, start:stop]

        ps = [PV(P[k // 2], QCW * (k % 2)) for k in range(6)]
        banks = ps + pt

        ident_f = ctx.enter_context(nc.sbuf_tensor("ident_f", [128, 128], F32))
        ident_b = ctx.enter_context(nc.sbuf_tensor("ident_b", [128, 128], BF16))
        mask_add = ctx.enter_context(nc.sbuf_tensor("mask_add", [128, 128], F32))
        ones_b = ctx.enter_context(nc.sbuf_tensor("ones_b", [128, 1], BF16))
        ones1f = ctx.enter_context(nc.sbuf_tensor("ones1f", [1, 128], F32))
        ones1b = ctx.enter_context(nc.sbuf_tensor("ones1b", [1, 128], BF16))
        negC = ctx.enter_context(nc.sbuf_tensor("negC", [128, 1], F32))
        epsb = ctx.enter_context(nc.sbuf_tensor("epsb", [128, 1], F32))
        gb_sb = ctx.enter_context(nc.sbuf_tensor("gb_sb", [128, 1], F32))
        gwt_f = ctx.enter_context(nc.sbuf_tensor("gwt_f", [128, 128], F32))
        gwt_b = ctx.enter_context(nc.sbuf_tensor("gwt_b", [128, 128], BF16))
        rsum = ctx.enter_context(nc.sbuf_tensor("rsum", [128, 2], F32))
        nmean = ctx.enter_context(nc.sbuf_tensor("nmean", [128, 2], F32))
        ssq = ctx.enter_context(nc.sbuf_tensor("ssq", [128, 2], F32))
        sdvv = ctx.enter_context(nc.sbuf_tensor("sdvv", [128, 2], F32))
        rstd = ctx.enter_context(nc.sbuf_tensor("rstd", [128, 2], F32))
        tbx = ctx.enter_context(nc.sbuf_tensor("tbx", [128, 2], F32))
        rsum8 = ctx.enter_context(nc.sbuf_tensor("rsum8", [128, 8], F32))
        nm8 = ctx.enter_context(nc.sbuf_tensor("nm8", [128, 8], F32))
        ssq8 = ctx.enter_context(nc.sbuf_tensor("ssq8", [128, 8], F32))
        tb8 = ctx.enter_context(nc.sbuf_tensor("tb8", [128, 8], F32))
        sd8 = ctx.enter_context(nc.sbuf_tensor("sd8", [128, 8], F32))
        rst8 = ctx.enter_context(nc.sbuf_tensor("rst8", [128, 8], F32))
        nm8b = ctx.enter_context(nc.sbuf_tensor("nm8b", [128, 8], F32))
        ssq8b = ctx.enter_context(nc.sbuf_tensor("ssq8b", [128, 8], F32))
        tb8b = ctx.enter_context(nc.sbuf_tensor("tb8b", [128, 8], F32))
        sd8b = ctx.enter_context(nc.sbuf_tensor("sd8b", [128, 8], F32))
        rst8b = ctx.enter_context(nc.sbuf_tensor("rst8b", [128, 8], F32))

        def emit_ln_stats(xs, tiles, x_waits, NM, SQ, TB, SD, RS):
            for t in tiles:
                b.op("vector", "dv", lambda e, t=t: e.tensor_reduce(
                    out=rsum8[:, t:t + 1], in_=xs[:, t, :], axis=AX.X,
                    op=ALU.add, negate=True), waits=x_waits[t], risky=True)
            sc_q = 0
            for t in tiles:
                sc_q = b.op("scalar", "sc", lambda e, t=t: e.activation(
                    out=sq_scrap[:, 0, :], in_=xs[:, t, :], func=AF.Square,
                    scale=1.0, accum_out=SQ[:, t:t + 1]),
                    waits=list(x_waits[t]), chain=True)
            lo, n = tiles[0], len(tiles)
            b.op("vector", "dv", lambda e, lo=lo, n=n: e.tensor_scalar_mul(
                out=NM[:, lo:lo + n], in0=rsum8[:, lo:lo + n], scalar1=INV_D),
                risky=True)
            b.op("vector", "dv", lambda e, lo=lo, n=n: e.tensor_tensor(
                out=TB[:, lo:lo + n], in0=NM[:, lo:lo + n],
                in1=NM[:, lo:lo + n], op=ALU.mult), risky=True)
            dv_tb = b.op("vector", "dv", lambda e, lo=lo, n=n: e.tensor_scalar(
                out=TB[:, lo:lo + n], in0=TB[:, lo:lo + n], scalar1=-1.0,
                scalar2=EPS, op0=ALU.mult, op1=ALU.add), risky=True)
            sc_sd = 0
            for t in tiles:
                sc_sd = b.op("scalar", "sc", lambda e, t=t: e.activation(
                    out=SD[:, t:t + 1], in_=SQ[:, t:t + 1], func=AF.Sqrt,
                    bias=TB[:, t:t + 1], scale=INV_D),
                    waits=([("dv", dv_tb)] if t == tiles[0] else []), chain=True)
            dv_rs = b.op("vector", "dv", lambda e, lo=lo, n=n: e.reciprocal(
                out=RS[:, lo:lo + n], in_=SD[:, lo:lo + n]),
                waits=[("sc", sc_sd)], risky=True)
            return sc_sd, dv_rs
        nmean2 = ctx.enter_context(nc.sbuf_tensor("nmean2", [128, 2], F32))
        ssq2 = ctx.enter_context(nc.sbuf_tensor("ssq2", [128, 2], F32))
        sdv2 = ctx.enter_context(nc.sbuf_tensor("sdv2", [128, 2], F32))
        rstd2 = ctx.enter_context(nc.sbuf_tensor("rstd2", [128, 2], F32))
        tbx2 = ctx.enter_context(nc.sbuf_tensor("tbx2", [128, 2], F32))
        amax = ctx.enter_context(nc.sbuf_tensor("amax", [128, 2, DC], F32))
        r2 = ctx.enter_context(nc.sbuf_tensor("r2", [128, 2, DC], F32))
        s2 = ctx.enter_context(nc.sbuf_tensor("s2", [128, 2, DC], F32))
        sq_scrap = ctx.enter_context(nc.sbuf_tensor("sq_scrap", [128, 1, D], BF16))

        for (tname, val) in ((ident_f, 0.0), (ident_b, 0.0), (mask_add, 0.0),
                             (ones_b, 1.0), (ones1f, 1.0), (ones1b, 1.0),
                             (negC, -C_SHIFT), (epsb, EPS)):
            b.op("gpsimd", "gp", lambda e, t=tname, v=val: e.memset(t[:], v), chain=True)
        b.op("gpsimd", "gp", lambda e: e.affine_select(
            out=ident_f[:], in_=ident_f[:], compare_op=ALU.not_equal,
            fill=1.0, base=0, pattern=[[-1, 128]], channel_multiplier=1), chain=True)
        b.op("gpsimd", "gp", lambda e: e.affine_select(
            out=ident_b[:], in_=ident_b[:], compare_op=ALU.not_equal,
            fill=1.0, base=0, pattern=[[-1, 128]], channel_multiplier=1), chain=True)
        gm = b.op("gpsimd", "gp", lambda e: e.affine_select(
            out=mask_add[:], in_=mask_add[:], compare_op=ALU.is_ge,
            fill=-1.0e30, base=0, pattern=[[1, 128]], channel_multiplier=-1), chain=True)
        b.dma("sync", "ld", gb_sb[:], gb[:, :])
        lgw = b.dma("sync", "ld", gwt_f[:], gwt[:, :])
        GWT_B = b.op("vector", "dv", lambda e: e.tensor_copy(out=gwt_b[:], in_=gwt_f[:]),
                     waits=[("ld", lgw)])

        ctxA = ExitStack()
        x_str = ctxA.enter_context(nc.sbuf_tensor("x_str", [128, TT, D], F32))
        wdeqT = ctxA.enter_context(nc.sbuf_tensor("wdeqT", [128, DC, D], BF16))
        xh = ctxA.enter_context(nc.sbuf_tensor("xh", [128, 2, D], F32))
        xq8 = ctxA.enter_context(nc.sbuf_tensor("xq8", [128, 1, D], FP8))
        xdq = ctxA.enter_context(nc.sbuf_tensor("xdq", [128, 2, D], BF16))
        xdT = ctxA.enter_context(nc.sbuf_tensor("xdT", [128, 2, DC, 128], BF16))
        scrA = dict(xh=xh, xq8=xq8, xdq=xdq, xdT=xdT, sq=sq_scrap)
        zbuf = ctxA.enter_context(nc.sbuf_tensor("zbuf", [128, 2, 4, 128], BF16))
        zln = ctxA.enter_context(nc.sbuf_tensor("zln", [128, 2, D], BF16))
        ctxA0 = ExitStack()
        wfn = ctxA0.enter_context(nc.sbuf_tensor("wfn", [128, 2, D], F32))
        wq8 = ctxA0.enter_context(nc.sbuf_tensor("wq8", [128, 1, D], FP8))
        wdq = ctxA0.enter_context(nc.sbuf_tensor("wdq", [128, 2, D], BF16))

        ld_x = []
        for t in range(TT):
            sem = "lx0" if t == 0 else ("lx1" if t == 1 else "lx")
            ld_x.append((sem, b.dma("sync", sem, x_str[:, t, :],
                                    x_sh[128 * t:128 * (t + 1), :])))
        ld_x = [(s, v if s != "lx" else b.cnt["lx"]) for (s, v) in ld_x]

        GP_INIT = gm
        pe_wtr = [[0] * 4 for _ in range(DC)]
        dv_wev_flat = [0] * DC
        gw_sp = 0
        for i in range(2):
            p = i % 2
            ldwf = b.dma("sync", f"lwf{p}", wfn[:, p, :], wf[128 * i:128 * (i + 1), :])
            b.op("vector", "dv", lambda e, p=p: e.tensor_reduce(
                out=amax[:, p, :],
                in_=wfn[:, p, :].rearrange("p (c k) -> p c k", k=128),
                axis=AX.X, op=ALU.max, apply_absolute_value=True),
                waits=[(f"lwf{p}", ldwf)], risky=True)
            b.op("vector", "dv", lambda e, p=p: e.reciprocal(
                out=r2[:, p, :], in_=amax[:, p, :]), risky=True)
            b.op("vector", "dv", lambda e, p=p: e.tensor_scalar_mul(
                out=r2[:, p, :], in0=r2[:, p, :], scalar1=224.0), risky=True)
            dv_r2w = b.op("vector", "dv", lambda e, p=p: e.tensor_scalar_mul(
                out=s2[:, p, :], in0=amax[:, p, :], scalar1=1.0 / 224.0), risky=True)
            for blk in range(DC):
                b.op("vector", "dv", lambda e, p=p, blk=blk: e.tensor_scalar(
                    out=wq8[:, 0, 128 * blk:128 * (blk + 1)],
                    in0=wfn[:, p, 128 * blk:128 * (blk + 1)],
                    scalar1=r2[:, p, blk:blk + 1], scalar2=None, op0=ALU.mult))
            dv_wdq = 0
            for blk in range(DC):
                dv_wdq = b.op("vector", "dv", lambda e, p=p, blk=blk: e.tensor_scalar(
                    out=wdq[:, p, 128 * blk:128 * (blk + 1)],
                    in0=wq8[:, 0, 128 * blk:128 * (blk + 1)],
                    scalar1=s2[:, p, blk:blk + 1], scalar2=None, op0=ALU.mult))
            gw_sp = b.dma("gpsimd", "gw", wag_in[128 * i:128 * (i + 1), :],
                          wdq[:, p, :], waits=[("dv", dv_wdq)])
        cc_w = b.op("gpsimd", "cc", lambda e: e.collective_compute(
            "AllGather", ALU.bypass, replica_groups=[list(range(8))],
            ins=[wag_in.ap().opt()], outs=[wag_out.ap().opt()]),
            waits=[("gw", gw_sp)], chain=True)
        for i in range(DC):
            p = i % 2
            ldn = b.dma("sync", f"lwn{p}", wdq[:, p, :],
                        wag_out[128 * i:128 * (i + 1), :],
                        waits=[("cc", cc_w)] + ([("pe", pe_wtr[i - 2][3])]
                                                if i >= 2 else [("gw", gw_sp)]))
            for k in range(4):
                bank = 6 + (k % 2)
                for j in range(4):
                    c = 4 * k + j
                    w_ = [(f"lwn{p}", ldn)] if (k == 0 and j == 0) else []
                    if j == 0:
                        w_ += b.use_bank(bank)
                    if (i == 0 and k == 0 and j == 0) or b.paranoid:
                        w_.append(("gp", GP_INIT))
                    pe_wtr[i][k] = b.op("tensor", "pe", lambda e, p=p, c=c, bank=bank, j=j: e.transpose(
                        banks[bank][:, 128 * j:128 * (j + 1)],
                        wdq[:, p, 128 * c:128 * (c + 1)], ident_b[:]),
                        waits=w_)
                dvev = b.op("scalar", "sc", lambda e, i=i, k=k, bank=bank: e.activation(
                    out=wdeqT[:, 4 * k:4 * (k + 1), 128 * i:128 * (i + 1)],
                    in_=banks[bank][:, 0:512].rearrange("p (j k) -> p j k", k=128),
                    func=AF.Copy, scale=1.0),
                    waits=[("pe", pe_wtr[i][k])], chain=True)
                b.done_bank(bank, "sc", dvev)
                dv_wev_flat[i] = dvev
        W_DONE = dvev

        def emit_ffn(xs, wT, w_ready, x_ready, scr, NM, RS, ln_sc, ln_dv,
                     store=None, store_sem="gs", hook=None, ln_per_tile=None,
                     pre_tile=None):
            xh = scr["xh"]; xq8 = scr["xq8"]; xdq = scr["xdq"]
            xdT = scr["xdT"]
            sc_sdv = [0] * TT
            sc_ssqv = [0] * TT
            sc_q8 = [0] * TT
            dv_xdq = [0] * TT
            dv_ev = [[0] * 4 for _ in range(TT)]
            dv_res = [[0] * 4 for _ in range(TT)]
            pe_tr = [[0] * 4 for _ in range(TT)]
            pe_mm = [[0] * 4 for _ in range(TT)]
            st_out = []
            for t in range(TT):
                if pre_tile is not None:
                    pre_tile(t)
                p = t % 2
                if ln_per_tile is not None:
                    lsc, ldv = ln_per_tile[t]
                else:
                    lsc, ldv = ln_sc, ln_dv
                b.op("vector", "dv", lambda e, t=t, p=p: e.tensor_scalar(
                    out=xh[:, p, :], in0=xs[:, t, :], scalar1=NM[:, t:t + 1],
                    scalar2=RS[:, t:t + 1], op0=ALU.add, op1=ALU.mult),
                    waits=[("dv", ldv), ("sc", lsc)] + list(x_ready[t]))
                b.op("vector", "dv", lambda e, p=p: e.tensor_reduce(
                    out=amax[:, p, :],
                    in_=xh[:, p, :].rearrange("p (c k) -> p c k", k=128),
                    axis=AX.X, op=ALU.max, apply_absolute_value=True), risky=True)
                b.op("vector", "dv", lambda e, p=p: e.reciprocal(
                    out=r2[:, p, :], in_=amax[:, p, :]), risky=True)
                b.op("vector", "dv", lambda e, p=p: e.tensor_scalar_mul(
                    out=r2[:, p, :], in0=r2[:, p, :], scalar1=224.0), risky=True)
                dv_r2 = b.op("vector", "dv", lambda e, p=p: e.tensor_scalar_mul(
                    out=s2[:, p, :], in0=amax[:, p, :], scalar1=1.0 / 224.0), risky=True)
                for blk in range(DC):
                    sc_q8[t] = b.op("vector", "dv", lambda e, p=p, blk=blk: e.tensor_scalar(
                        out=xq8[:, 0, 128 * blk:128 * (blk + 1)],
                        in0=xh[:, p, 128 * blk:128 * (blk + 1)],
                        scalar1=r2[:, p, blk:blk + 1], scalar2=None, op0=ALU.mult))
                for blk in range(DC):
                    w_ = [("pe", pe_tr[t - 2][3])] if (t >= 2 and blk == 0) else []
                    dv_xdq[t] = b.op("vector", "dv", lambda e, p=p, blk=blk: e.tensor_scalar(
                        out=xdq[:, p, 128 * blk:128 * (blk + 1)],
                        in0=xq8[:, 0, 128 * blk:128 * (blk + 1)],
                        scalar1=s2[:, p, blk:blk + 1], scalar2=None, op0=ALU.mult),
                        waits=w_)
                for k in range(4):
                    bank = 6 + (k % 2)
                    for j in range(4):
                        c = 4 * k + j
                        w_ = [("dv", dv_xdq[t])] if (k == 0 and j == 0) else []
                        if j == 0:
                            w_ += b.use_bank(bank)
                        if b.paranoid:
                            w_.append(("gp", gm))
                        pe_tr[t][k] = b.op("tensor", "pe", lambda e, p=p, c=c, bank=bank, j=j: e.transpose(
                            banks[bank][:, 128 * j:128 * (j + 1)],
                            xdq[:, p, 128 * c:128 * (c + 1)], ident_b[:]), waits=w_)
                    w_ev = [("pe", pe_tr[t][k])]
                    if t >= 2 and k == 0:
                        w_ev.append(("pe", pe_mm[t - 2][3]))
                    dv_ev[t][k] = b.op("scalar", "sc", lambda e, k=k, p=p, bank=bank: e.activation(
                        out=xdT[:, p, 4 * k:4 * (k + 1), :],
                        in_=banks[bank][:, 0:512].rearrange("p (j k) -> p j k", k=128),
                        func=AF.Copy, scale=1.0),
                        waits=w_ev, chain=True)
                    b.done_bank(bank, "sc", dv_ev[t][k])
                def emit_resid(tt):
                    for oc in range(4):
                        dv_res[tt][oc] = b.op("vector", "dv",
                                              lambda e, tt=tt, oc=oc: e.scalar_tensor_tensor(
                                                  out=xs[:, tt, 512 * oc:512 * (oc + 1)],
                                                  in0=ps[oc][:, 0:512], scalar=0.5,
                                                  in1=xs[:, tt, 512 * oc:512 * (oc + 1)],
                                                  op0=ALU.mult, op1=ALU.add),
                                              waits=[("pe", pe_mm[tt][oc])])
                        b.done_bank(oc, "dv", dv_res[tt][oc])
                    if store is not None:
                        eng = "scalar" if store_sem == "vy" else "gpsimd"
                        st_out.append(b.dma(eng, store_sem,
                                            store[128 * tt:128 * (tt + 1), :],
                                            xs[:, tt, :], waits=[("dv", dv_res[tt][3])]))
                if t >= 1:
                    emit_resid(t - 1)
                mm_first = True
                for oc in range(4):
                    for c in range(DC):
                        w_ = []
                        if c == 0:
                            w_ = [("sc", dv_ev[t][3]), w_ready[oc]] + b.use_bank(oc)
                        pe_mm[t][oc] = b.op("tensor", "pe", lambda e, p=p, c=c, oc=oc: e.matmul(
                            ps[oc][:, 0:512], xdT[:, p, c, :],
                            wT[:, c, 512 * oc:512 * (oc + 1)],
                            start=(c == 0), stop=(c == DC - 1)), waits=w_)
                if hook is not None and t >= 1:
                    hook(t - 1, dv_res[t - 1][3])
            emit_resid(TT - 1)
            if hook is not None:
                hook(TT - 1, dv_res[TT - 1][3])
            return dv_res, st_out

        W_OC = [("dv", 0)] * 4
        for oc in range(4):
            W_OC[oc] = ("sc", dv_wev_flat[4 * oc + 3])
        SW_SPILL = 0
        for c in range(DC):
            SW_SPILL = b.dma("gpsimd", "ts", wdeq_spill[c, :, :], wdeqT[:, c, :],
                             waits=[("sc", W_DONE)])
        st2 = {"sc_sd": [0] * TT, "pe_ztr": [0] * TT, "gs_zb": [[0] * 4 for _ in range(TT)],
               "cc_ag": [0, 0], "dv_z": [0] * TT}
        xh_a = scrA["xh"]

        def ln2_hook(t, dv_resv):
            if t != 3 and t != TT - 1:
                return
            half = 0 if t == 3 else 1
            tiles = [4 * half + i for i in range(4)]
            xw = {tt: [("dv", dv_resv)] for tt in tiles}
            sc_sd, dv_rs = emit_ln_stats(x_str, tiles, xw,
                                         nm8b, ssq8b, tb8b, sd8b, rst8b)
            for tt in tiles:
                p = tt % 2
                dv_z = b.op("vector", "dv", lambda e, tt=tt, p=p: e.tensor_scalar(
                    out=zln[:, p, :], in0=x_str[:, tt, :],
                    scalar1=nm8b[:, tt:tt + 1], scalar2=rst8b[:, tt:tt + 1],
                    op0=ALU.add, op1=ALU.mult),
                    waits=([("dv", dv_rs), ("sc", sc_sd)] if tt == tiles[0] else [])
                    + ([("pe", st2["pe_ztr"][tt - 2])] if tt >= 2 else []))
                st2["dv_z"][tt] = dv_z
                for k in range(4):
                    bank = 6 + (k % 2)
                    pe_tr = 0
                    for j in range(4):
                        c = 4 * k + j
                        w_ = [("dv", dv_z)] if (k == 0 and j == 0) else []
                        if j == 0:
                            w_ += b.use_bank(bank)
                        pe_tr = b.op("tensor", "pe", lambda e, p=p, c=c, bank=bank, j=j: e.transpose(
                            banks[bank][:, 128 * j:128 * (j + 1)],
                            zln[:, p, 128 * c:128 * (c + 1)], ident_b[:]), waits=w_)
                    st2["pe_ztr"][tt] = pe_tr
                    zpar = (4 * tt + k) % 2
                    w_ev = [("pe", pe_tr)]
                    if tt >= 1 or k >= 2:
                        pk = (4 * tt + k) - 2
                        w_ev.append((f"gz{zpar}", st2["gs_zb"][pk // 4][pk % 4]))
                    dv_ze = b.op("scalar", "sc", lambda e, zpar=zpar, bank=bank: e.activation(
                        out=zbuf[:, zpar, :, :],
                        in_=banks[bank][:, 0:512].rearrange("p (j k) -> p j k", k=128),
                        func=AF.Copy, scale=1.0),
                        waits=w_ev, chain=True)
                    b.done_bank(bank, "sc", dv_ze)
                    halfq, tq = divmod(tt, 4)
                    for j in range(4):
                        c = 4 * k + j
                        st2["gs_zb"][tt][k] = b.dma(
                            "gpsimd", f"gz{zpar}",
                            zag_in[halfq][128 * c:128 * (c + 1), 128 * tq:128 * (tq + 1)],
                            zbuf[:, zpar, j, :], waits=([("sc", dv_ze)] if j == 0 else []))
            st2["cc_ag"][half] = b.op("gpsimd", "cc", lambda e, half=half: e.collective_compute(
                "AllGather", ALU.bypass, replica_groups=GROUPS,
                ins=[zag_in[half].ap().opt()], outs=[zag_out[half].ap().opt()]),
                waits=[("gz0", b.cnt["gz0"]), ("gz1", b.cnt["gz1"])], chain=True)

        spill_jobs = []
        for (wsrc, wdst) in ((wq_t, wq_sp), (wk_t, wk_sp), (wv_t, wv_sp)):
            for c in range(DC):
                spill_jobs.append((wsrc[128 * c:128 * (c + 1), :],
                                   wdst[128 * c:128 * (c + 1), :]))
        for c in range(4):
            for u in range(4):
                spill_jobs.append(
                    (wo_t[128 * c:128 * (c + 1), 512 * u:512 * (u + 1)],
                     wo_sp[128 * c:128 * (c + 1), 512 * u:512 * (u + 1)]))
        SP = {"k": 0, "lms": {}, "cms": {}, "sms": {}}
        wspf = [wfn[:, s // 4, 512 * (s % 4):512 * (s % 4 + 1)] for s in range(8)]
        wspb = [wdq[:, 0, 512 * p:512 * (p + 1)] for p in range(2)]

        def spill_casts(k):
            for i in range(8 * k, min(8 * k + 8, len(spill_jobs))):
                _, dst = spill_jobs[i]
                s = i % 8
                wc = [(f"lsp{s}", SP["lms"][i])]
                if i >= 2:
                    wc.append((f"ssp{i % 2}", SP["sms"][i - 2]))
                if i == 0:
                    wc.append(("gw", gw_sp))
                cv = b.op("vector", "dv", lambda e, s=s, i=i: e.tensor_copy(
                    out=wspb[i % 2], in_=wspf[s]), waits=wc)
                SP["cms"][i] = cv
                SP["sms"][i] = b.dma("sync", f"ssp{i % 2}", dst,
                                     wspb[i % 2], waits=[("dv", cv)])

        def spill_loads(k):
            for i in range(8 * k, min(8 * k + 8, len(spill_jobs))):
                src, _ = spill_jobs[i]
                s = i % 8
                wl = [("dv", SP["cms"].get(i - 8, 0))]
                if i == 0:
                    wl.append(("sc", W_DONE))
                SP["lms"][i] = b.dma("scalar", f"lsp{s}", wspf[s], src,
                                     waits=wl)

        def spill_step():
            k = SP["k"]
            SP["k"] = k + 1
            spill_casts(k)
            spill_loads(k + 1)

        def hook_all(t, dvr):
            ln2_hook(t, dvr)
            spill_step()

        spill_loads(0)
        ln1 = [None] * TT

        def pre_tile1(t):
            grp = {0: [0], 1: [1, 2, 3], 4: [4, 5, 6, 7]}.get(t)
            if grp is not None:
                s = emit_ln_stats(x_str, grp, {tt: [ld_x[tt]] for tt in grp},
                                  nm8, ssq8, tb8, sd8, rst8)
                for tt in grp:
                    ln1[tt] = s
        dv_res1, y_st = emit_ffn(x_str, wdeqT, W_OC,
                                 [[ld_x[t]] for t in range(TT)], scrA,
                                 nm8, rst8, 0, 0,
                                 store=y_spill, store_sem="vy", hook=hook_all,
                                 ln_per_tile=ln1, pre_tile=pre_tile1)
        SPW = [("ssp0", SP["sms"][len(spill_jobs) - 2]),
               ("ssp1", SP["sms"][len(spill_jobs) - 1])]
        cc_ag = st2["cc_ag"]
        GS_SPILL = b.cnt["gs"]
        FENCE2 = [(s, b.cnt[s]) for s in ("ts", "pe", "dv", "sc")]
        FENCE2V = FENCE2 + [(s, b.cnt[s]) for s in ("gz0", "gz1")]
        ctxA0.close()

        ctxA.close()
        ctxBm = ExitStack()
        m_t = ctxBm.enter_context(nc.sbuf_tensor("m_t", [128, NH, S], BF16, side="right"))
        ctxB = ExitStack()
        q_t = ctxB.enter_context(nc.sbuf_tensor("q_t", [128, NH, S], BF16))
        k_t = ctxB.enter_context(nc.sbuf_tensor("k_t", [128, NH, S], BF16))
        v_t = ctxB.enter_context(nc.sbuf_tensor("v_t", [128, 32, 512], BF16))
        ctxB3 = ExitStack()
        zt_q = ctxB3.enter_context(nc.sbuf_tensor("zt_q", [128, DC, 512], BF16))
        wq_b = ctxB3.enter_context(nc.sbuf_tensor("wq_b", [128, DC, 512], BF16))
        wk_b = ctxB3.enter_context(nc.sbuf_tensor("wk_b", [128, DC, 512], BF16))
        wv_b = ctxB3.enter_context(nc.sbuf_tensor("wv_b", [128, DC, 512], BF16))

        for (wsrc, wbbuf, semw) in ((wq_sp, wq_b, "lw3"), (wk_sp, wk_b, "lk3"),
                                    (wv_sp, wv_b, "lv3")):
            for c in range(DC):
                w_ = []
                if wbbuf is wq_b and c == 0:
                    w_ = list(FENCE2V) + SPW + [("vy", y_st[-1])]
                b.dma("scalar", semw, wbbuf[:, c, :],
                      wsrc[128 * c:128 * (c + 1), :], waits=w_)
        pe_zread = [0] * DC
        for hq in range(8):
            qq, half = divmod(hq, 2)
            ldz = 0
            for c in range(DC):
                w_ = [("pe", pe_zread[c])] if hq >= 1 else (
                    list(FENCE2) + SPW + [("vy", y_st[-1])])
                ldz = b.dma("sync", f"lz{hq % 2}", zt_q[:, c, :],
                            zag_out[half][D * qq + 128 * c:D * qq + 128 * (c + 1), :],
                            waits=w_ + [("cc", cc_ag[half])])
            zq_ready = (f"lz{hq % 2}", ldz)
            for (wb, dst) in ((wq_b, q_t), (wk_b, k_t)):
                pe_h = [0] * NH
                for c in range(DC):
                    for h in range(NH):
                        w_ = []
                        if h == 0 and hq == 0 and c == 0:
                            w_.append(("lw3" if wb is wq_b else "lk3", 16 * DC))
                        if c == 0:
                            w_ = w_ + [zq_ready] + b.use_bank(h)
                        pe_h[h] = b.op("tensor", "pe",
                                       lambda e, c=c, h=h, wb=wb: e.matmul(
                                           ps[h][:, 0:512],
                                           wb[:, c, 128 * h:128 * (h + 1)],
                                           zt_q[:, c, :],
                                           start=(c == 0), stop=(c == DC - 1)),
                                       waits=w_)
                for h in range(NH):
                    dvx = b.op("vector", "dv",
                               lambda e, h=h, hq=hq, dst=dst: e.tensor_copy(
                                   out=dst[:, h, 512 * hq:512 * (hq + 1)],
                                   in_=ps[h][:, 0:512]), waits=[("pe", pe_h[h])])
                    b.done_bank(h, "dv", dvx)
            pe_g = [0] * 4
            for c in range(DC):
                for u in range(4):
                    w_ = []
                    if c == 0:
                        w_ = [zq_ready, ("lv3", 16 * DC)] + b.use_bank(u)
                    pe_g[u] = b.op("tensor", "pe",
                                   lambda e, c=c, u=u: e.matmul(
                                       ps[u][:, 0:512],
                                       zt_q[:, c, 128 * u:128 * (u + 1)],
                                       wv_b[:, c, 0:512],
                                       start=(c == 0), stop=(c == DC - 1)), waits=w_)
                    pe_zread[c] = pe_g[u]
            for u in range(4):
                dvx = b.op("vector", "dv", lambda e, hq=hq, u=u: e.tensor_copy(
                    out=v_t[:, 4 * hq + u, :], in_=ps[u][:, 0:512]),
                    waits=[("pe", pe_g[u])])
                b.done_bank(u, "dv", dvx)
        PH3_DV = b.cnt["dv"]
        ctxB3.close()

        ctxB4 = ExitStack()
        gate_h = ctxB4.enter_context(nc.sbuf_tensor("gate_h", [128, 2, S], BF16))
        av_sb = ctxB4.enter_context(nc.sbuf_tensor("av_sb", [128, 1, S], BF16))
        p_t = ctxB4.enter_context(nc.sbuf_tensor("p_t", [128, 3, 1024], BF16))
        cs_rows = [ctxB4.enter_context(nc.sbuf_tensor(f"cs_row{i}", [1, S], BF16))
                   for i in range(2)]
        cs128 = ctxB4.enter_context(nc.sbuf_tensor("cs128", [128, 32], BF16))
        rc128 = ctxB4.enter_context(nc.sbuf_tensor("rc128", [128, 32], BF16))
        recip_b = ctxB4.enter_context(nc.sbuf_tensor("recip_b", [128, S], BF16))

        dv_mt = [0] * NH
        pt_pe = [0, 0, 0]
        tail_args = []

        def emit_tail(h, ph, dvc, scg):
            ld1 = b.dma("sync", "lcs", cs128[:, :], cs_rows[ph][0:1, :],
                        waits=[("dv", dvc)])
            def _recip(e):
                with nc.allow_low_precision(reason="bf16 norm factor, matches recip_b precision"):
                    return e.reciprocal(out=rc128[:, :], in_=cs128[:, :])
            dvr = b.op("vector", "dv", _recip,
                waits=[("lcs", ld1)], risky=True)
            ld2 = b.dma("sync", "lc2", cs_rows[ph][0:1, :], rc128[:, :],
                        waits=[("dv", dvr)])
            dv_rb = 0
            for qc in range(NQC):
                bank = qc % 4
                w_ = b.use_bank(bank) + [("lc2", ld2)]
                peb = b.op("tensor", "pe", lambda e, qc=qc, bank=bank, ph=ph: e.matmul(
                    ps[bank][:, 0:512], ones1b[:],
                    cs_rows[ph][0:1, 512 * qc:512 * (qc + 1)],
                    start=True, stop=True), waits=w_)
                dv_rb = b.op("vector", "dv", lambda e, qc=qc, bank=bank: e.tensor_copy(
                    out=recip_b[:, 512 * qc:512 * (qc + 1)], in_=ps[bank][:, 0:512]),
                    waits=[("pe", peb)])
                b.done_bank(bank, "dv", dv_rb)
            dvt = b.op("vector", "dv", lambda e, ph=ph: e.tensor_tensor(
                out=recip_b[:, :], in0=av_sb[:, 0, :], in1=recip_b[:, :], op=ALU.mult),
                waits=[("dv", dv_rb), ("sc", scg)])
            dv_mt[h] = b.op("vector", "dv", lambda e, h=h, ph=ph: e.tensor_tensor(
                out=m_t[:, h, :], in0=recip_b[:, :], in1=gate_h[:, ph, :], op=ALU.mult),
                waits=[("dv", dvt)])
        for h in range(NH):
            ph = h % 2
            scg = 0
            for qc in range(NQC):
                bank = qc % 4
                w_ = b.use_bank(bank)
                if qc == 0:
                    w_ += [("gp", gm), ("dv", GWT_B)]
                    if h == 0:
                        w_.append(("dv", PH3_DV))
                    if h >= 2:
                        w_.append(("dv", dv_mt[h - 2]))
                peg = b.op("tensor", "pe", lambda e, h=h, qc=qc, bank=bank: e.matmul(
                    ps[bank][:, 0:512], gwt_b[:], q_t[:, h, 512 * qc:512 * (qc + 1)],
                    start=True, stop=True), waits=w_)
                scg = b.op("scalar", "sc", lambda e, ph=ph, qc=qc, bank=bank: e.activation(
                    out=gate_h[:, ph, 512 * qc:512 * (qc + 1)], in_=ps[bank][:, 0:512],
                    func=AF.Sigmoid, bias=gb_sb[:, 0:1], scale=1.0),
                    waits=[("pe", peg)] + ([("ld", 32), ("gp", gm)]
                                           if (h == 0 and qc == 0) else []),
                    chain=True)
                b.done_bank(bank, "sc", scg)
            if tail_args:
                emit_tail(*tail_args.pop(0))
            dva = 0
            dvc = 0
            for qc in range(NQC):
                klim = 4 * qc + 4
                npair = klim // 2
                pend = []

                def flush_avcs(h=h):
                    pr, slot, pslot, sc_e = pend.pop(0)
                    ki0, ki1 = 2 * pr, 2 * pr + 1
                    klim_ = pend_klim[0]
                    pe_av = pe_cs = 0
                    for u, ki in ((0, ki0), (1, ki1)):
                        w_ = [("sc", sc_e)]
                        if ki == 0:
                            w_ += b.use_bank(4)
                        pe_av = b.op("tensor", "pe",
                                     lambda e, h=h, ki=ki, pslot=pslot, u=u, klim_=klim_: e.matmul(
                                         ps[4][:, 0:512],
                                         v_t[:, ki, 128 * h:128 * (h + 1)],
                                         p_t[:, pslot, 512 * u:512 * (u + 1)],
                                         start=(ki == 0), stop=(ki == klim_ - 1)),
                                     waits=w_)
                    for u, ki in ((0, ki0), (1, ki1)):
                        w_ = [("sc", sc_e)]
                        if ki == 0:
                            w_ += b.use_bank(5)
                        pe_cs = b.op("tensor", "pe",
                                     lambda e, ki=ki, pslot=pslot, u=u, klim_=klim_: e.matmul(
                                         ps[5][0:1, 0:512], ones_b[:],
                                         p_t[:, pslot, 512 * u:512 * (u + 1)],
                                         start=(ki == 0), stop=(ki == klim_ - 1)),
                                     waits=w_)
                    pt_pe[pslot] = pe_cs
                    return pe_av, pe_cs

                pend_klim = [klim]
                pe_av_last = pe_cs_last = 0
                for pr in range(npair):
                    ki0, ki1 = 2 * pr, 2 * pr + 1
                    off0 = max(0, 128 * ki0 - 512 * qc)
                    off1 = max(0, 128 * ki1 - 512 * qc)
                    slot = pr % 2
                    pslot = pr % 3
                    wS = b.use_bank(2 * slot) + b.use_bank(2 * slot + 1)
                    pes = b.op("tensor", "pe",
                               lambda e, h=h, qc=qc, ki0=ki0, off0=off0, slot=slot: e.matmul(
                                   P[slot][:, off0:512],
                                   k_t[:, h, 128 * ki0:128 * (ki0 + 1)],
                                   q_t[:, h, 512 * qc + off0:512 * (qc + 1)],
                                   start=True, stop=True), waits=wS)
                    pes = b.op("tensor", "pe",
                               lambda e, h=h, qc=qc, ki1=ki1, off1=off1, slot=slot: e.matmul(
                                   P[slot][:, 512 + off1:1024],
                                   k_t[:, h, 128 * ki1:128 * (ki1 + 1)],
                                   q_t[:, h, 512 * qc + off1:512 * (qc + 1)],
                                   start=True, stop=True), waits=wS)
                    dv_ms = 0
                    if ki1 >= 4 * qc:
                        wm = [("pe", pes)]
                        if h == 0 and qc == 0 and pr == 0:
                            wm.append(("gp", gm))
                        if off0 or ki0 >= 4 * qc:
                            dv_ms = b.op("vector", "dv",
                                         lambda e, slot=slot, off0=off0: e.tensor_tensor(
                                             out=P[slot][:, off0:off0 + 128],
                                             in0=P[slot][:, off0:off0 + 128],
                                             in1=mask_add[:], op=ALU.add), waits=wm)
                            wm = []
                        dv_ms = b.op("vector", "dv",
                                     lambda e, slot=slot, off1=off1: e.tensor_tensor(
                                         out=P[slot][:, 512 + off1:512 + off1 + 128],
                                         in0=P[slot][:, 512 + off1:512 + off1 + 128],
                                         in1=mask_add[:], op=ALU.add), waits=wm)
                        if off0:
                            dv_ms = b.op("vector", "dv", lambda e, slot=slot, off0=off0: e.memset(
                                P[slot][:, 0:off0], -1.0e30))
                        if off1:
                            dv_ms = b.op("vector", "dv", lambda e, slot=slot, off1=off1: e.memset(
                                P[slot][:, 512:512 + off1], -1.0e30))
                    w_e = [("pe", pes), ("pe", pt_pe[pslot]), ("dv", dv_ms)]
                    sc_e = b.op("scalar", "sc",
                                lambda e, slot=slot, pslot=pslot: e.activation(
                                    out=p_t[:, pslot, :], in_=P[slot][:, 0:1024],
                                    func=AF.Exp, bias=negC[:, 0:1], scale=SCALE),
                                chain=True, waits=w_e)
                    b.done_bank(2 * slot, "sc", sc_e)
                    b.done_bank(2 * slot + 1, "sc", sc_e)
                    pend.append((pr, slot, pslot, sc_e))
                    if len(pend) > 2:
                        pe_av_last, pe_cs_last = flush_avcs()
                while pend:
                    pe_av_last, pe_cs_last = flush_avcs()
                dva = b.op("vector", "dv", lambda e, ph=ph, qc=qc: e.tensor_copy(
                    out=av_sb[:, 0, 512 * qc:512 * (qc + 1)], in_=ps[4][:, 0:512]),
                    waits=[("pe", pe_av_last)])
                b.done_bank(4, "dv", dva)
                dvc = b.op("vector", "dv", lambda e, qc=qc, ph=ph: e.tensor_copy(
                    out=cs_rows[ph][0:1, 512 * qc:512 * (qc + 1)],
                    in_=ps[5][0:1, 0:512]),
                    waits=[("pe", pe_cs_last)])
                b.done_bank(5, "dv", dvc)
            tail_args.append((h, ph, dvc, scg))
        while tail_args:
            emit_tail(*tail_args.pop(0))
        ctxB4.close()
        ctxB.close()

        ctxC = ExitStack()
        x_str2 = ctxC.enter_context(nc.sbuf_tensor("x_str2", [128, TT, D], F32))
        wdeqT2 = ctxC.enter_context(nc.sbuf_tensor("wdeqT2", [128, DC, D], BF16))
        ctxC0 = ExitStack()
        wo_b = ctxC0.enter_context(nc.sbuf_tensor("wo_b", [128, 4, D], BF16))
        rs_buf = ctxC0.enter_context(nc.sbuf_tensor("rs_buf", [128, 4, 2048], BF16))

        for c in range(4):
            b.dma("sync", "lwo", wo_b[:, c, :], wo_sp[128 * c:128 * (c + 1), :],
                  waits=(([("dv", dv_mt[NH - 1])] + SPW) if c == 0 else []))
        cc_rs = [0, 0]
        gr_last = [0, 0, 0, 0]
        nqt = 0
        for j in range(TT):
            for g in range(4):
                qt = 8 * g + j
                slot = nqt % 4
                dvo = 0
                for half in range(2):
                    peo = 0
                    for e2p in (2 * half, 2 * half + 1):
                        view = e2p
                        for c in range(4):
                            w_ = []
                            if c == 0:
                                w_ = b.use_bank(view)
                                if nqt == 0 and e2p == 0:
                                    w_ += [("lwo", 64), ("dv", dv_mt[NH - 1]),
                                           ("dv", dv_mt[NH - 2])]
                            peo = b.op("tensor", "pe",
                                       lambda e, e2p=e2p, qt=qt, c=c, view=view: e.matmul(
                                           ps[view][:, 0:512],
                                           m_t[:, c, 128 * qt:128 * (qt + 1)],
                                           wo_b[:, c, 512 * e2p:512 * (e2p + 1)],
                                           start=(c == 0), stop=(c == 3)), waits=w_)
                    Pt = P[half]
                    dvo = b.op("vector", "dv", lambda e, Pt=Pt, slot=slot, half=half: e.tensor_copy(
                        out=rs_buf[:, slot, 1024 * half:1024 * (half + 1)],
                        in_=Pt[:, 0:1024]),
                        waits=[("pe", peo), (f"gr{slot}", gr_last[slot])])
                    b.done_bank(2 * half, "dv", dvo)
                    b.done_bank(2 * half + 1, "dv", dvo)
                hv = j // 4
                gr_last[slot] = b.dma("gpsimd" if slot % 2 == 0 else "scalar",
                                      f"gr{slot}",
                                      rs_in[hv][512 * g + 128 * (j % 4):
                                                512 * g + 128 * (j % 4) + 128, :],
                                      rs_buf[:, slot, :], waits=[("dv", dvo)])
                nqt += 1
            if j % 4 == 3:
                cc_rs[j // 4] = b.op("gpsimd", "cc",
                                     lambda e, hv=j // 4: e.collective_compute(
                                         "ReduceScatter", ALU.add,
                                         replica_groups=GROUPS,
                                         ins=[rs_in[hv].ap().opt()],
                                         outs=[rs_out[hv].ap().opt()]),
                                     waits=[(f"gr{s}", gr_last[s])
                                            for s in range(4)], chain=True)
        PE_OPROJ_END = b.cnt["pe"]
        W2_OC = []
        for oc in range(4):
            sem = "lwd" if oc == 0 else f"lwd{oc}"
            for c in range(DC):
                ld_wd = b.dma("sync", sem, wdeqT2[:, c, 512 * oc:512 * (oc + 1)],
                              wdeq_spill[c, :, 512 * oc:512 * (oc + 1)],
                              waits=([("ts", SW_SPILL), ("dv", dv_mt[NH - 1])]
                                     if (oc == 0 and c == 0) else []))
            W2_OC.append((sem, ld_wd))
        if taps:
            gt = b.dma("gpsimd", "gs", dbg_y[:, :], y_spill[:, :],
                       waits=[("gs", GS_SPILL)])
            gt = b.dma("gpsimd", "gs", dbg_wd[:, :, :], wdeq_spill[:, :, :])
            gt = b.dma("gpsimd", "gs", dbg_zo[:, 0:512], zag_out[0][:, :],
                       waits=[("cc", cc_ag[1])])
            gt = b.dma("gpsimd", "gs", dbg_zo[:, 512:1024], zag_out[1][:, :])
            gt = b.dma("gpsimd", "gs", dbg_rs[:, :], rs_out[0][0:128, :],
                       waits=[("cc", cc_rs[1])])
            b.wait("gpsimd", [("gs", gt)])
        ctxC0.close()
        ctxBm.close()
        y_buf = ctxC.enter_context(nc.sbuf_tensor("y_buf", [128, 2, D], F32))
        rs_sb = ctxC.enter_context(nc.sbuf_tensor("rs_sb", [128, 2, D], BF16))
        xh2 = ctxC.enter_context(nc.sbuf_tensor("xh2", [128, 2, D], F32))
        xq82 = ctxC.enter_context(nc.sbuf_tensor("xq82", [128, 1, D], FP8))
        xdq2 = ctxC.enter_context(nc.sbuf_tensor("xdq2", [128, 2, D], BF16))
        xdT2 = ctxC.enter_context(nc.sbuf_tensor("xdT2", [128, 2, DC, 128], BF16))
        scrC = dict(xh=xh2, xq8=xq82, xdq=xdq2, xdT=xdT2, sq=sq_scrap)

        dv_x2 = [0] * TT
        ln3 = [None] * TT
        xr2 = [None] * TT
        ld_yr = {}

        def emit_x2_loads(t):
            if t >= TT:
                return
            p = t % 2
            w_ = [("dv", dv_x2[t - 2])] if t >= 2 else []
            hz = [] if t else [("pe", PE_OPROJ_END)] + [
                (f"gr{s}", gr_last[s]) for s in range(4)]
            ldy = b.dma("sync", f"ly{p}", y_buf[:, p, :],
                        y_spill[128 * t:128 * (t + 1), :],
                        waits=w_ + hz + [("vy", y_st[-1]), ("dv", dv_mt[NH - 1])])
            ldr = b.dma("sync", f"lr{p}", rs_sb[:, p, :],
                        rs_out[t // 4][128 * (t % 4):128 * (t % 4) + 128, :],
                        waits=w_ + [("cc", cc_rs[t // 4])])
            ld_yr[t] = (ldy, ldr)

        def pre_tile2(t):
            emit_x2_loads(t + 1)
            p = t % 2
            ldy, ldr = ld_yr[t]
            dv_x2[t] = b.op("vector", "dv", lambda e, t=t, p=p: e.tensor_tensor(
                out=x_str2[:, t, :], in0=y_buf[:, p, :],
                in1=rs_sb[:, p, :], op=ALU.add),
                waits=[(f"ly{p}", ldy), (f"lr{p}", ldr)])
            ln3[t] = emit_ln_stats(x_str2, [t], {t: [("dv", dv_x2[t])]},
                                   nm8, ssq8, tb8, sd8, rst8)
            xr2[t] = [("dv", dv_x2[t])]

        emit_x2_loads(0)
        _, st_out = emit_ffn(x_str2, wdeqT2, W2_OC, xr2, scrC,
                             nm8, rst8, 0, 0, store=out_sh, store_sem="vy",
                             ln_per_tile=ln3, pre_tile=pre_tile2)
        b.wait("scalar", [("vy", st_out[-1])])
        b.flush("all")
        ctxC.close()

    return nc


_cache = {}


def kernel(L0_attn_x, L0_attn_q_w, L0_attn_k_w, L0_attn_v_w, L0_attn_o_w,
           L0_attn_gate_w, L0_attn_gate_b, L0_ffn_weight):
    x = np.ascontiguousarray(np.asarray(L0_attn_x, dtype=np.float32).reshape(8192, 2048))
    qw = np.asarray(L0_attn_q_w, dtype=np.float32)
    kw = np.asarray(L0_attn_k_w, dtype=np.float32)
    vw = np.asarray(L0_attn_v_w, dtype=np.float32)
    ow = np.asarray(L0_attn_o_w, dtype=np.float32)
    gw = np.asarray(L0_attn_gate_w, dtype=np.float32)
    gbv = np.ascontiguousarray(np.asarray(L0_attn_gate_b, dtype=np.float32).reshape(128, 1))
    fw = np.ascontiguousarray(np.asarray(L0_ffn_weight, dtype=np.float32))
    owt = np.ascontiguousarray(ow.T)
    gwt_host = np.ascontiguousarray(gw.T)

    if "nc" not in _cache:
        _cache["nc"] = build()
    nc = _cache["nc"]

    in_maps = []
    for c in range(NCORES):
        bb, r = divmod(c, 4)
        sl = slice(512 * r, 512 * (r + 1))
        in_maps.append({
            "x_sh": np.ascontiguousarray(x[4096 * bb + 1024 * r: 4096 * bb + 1024 * (r + 1)]),
            "wq_t": np.ascontiguousarray(qw[sl, :].T),
            "wk_t": np.ascontiguousarray(kw[sl, :].T),
            "wv_t": np.ascontiguousarray(vw[sl, :].T),
            "wo_t": np.ascontiguousarray(owt[sl, :]),
            "gwt": gwt_host,
            "gb": gbv,
            "wf": np.ascontiguousarray(fw[256 * c:256 * (c + 1)]),
        })
    res = run_bass_kernel_spmd(nc, in_maps, core_ids=list(range(NCORES)))
    out = np.concatenate([np.asarray(res.results[c]["out_sh"]) for c in range(NCORES)],
                         axis=0)
    return out.reshape(2, 4096, 2048).astype(np.float32)
